# version 06
# baseline (speedup 1.0000x reference)
import os
import numpy as np
import ml_dtypes

_P1LVL = int(os.environ.get("KP1LVL", "99"))
_P1SUB = int(os.environ.get("KP1SUB", "99"))
_DBG = int(os.environ.get("KDBG", "0"))
DBG_RES = {}

import concourse.bass as bass
import concourse.mybir as mybir
from concourse.bass_utils import run_bass_kernel_spmd

F32 = mybir.dt.float32
BF16 = mybir.dt.bfloat16
I32 = mybir.dt.int32
AF = mybir.ActivationFunctionType
ALU = mybir.AluOpType
AX = mybir.AxisListType
BF = ml_dtypes.bfloat16

E, H, HID = 1024, 16, 4096
TLOC = 2048
NPT, PT = 4, 512
NBLK, CB = 8, 256
NFT, FT = 4, 512
EPS = 1e-5
ENGINES = ("sync", "tensor", "scalar", "vector", "gpsimd")


def _consts():
    c = {}
    c["ones_row_f"] = np.ones((1, 128), np.float32)
    c["ones_row_b"] = np.ones((1, 128), BF)
    c["ones_col_f"] = np.ones((128, 1), np.float32)
    c["ones_col_b"] = np.ones((128, 1), BF)
    c["ident_f"] = np.eye(128, dtype=np.float32)
    c["ident_b"] = np.eye(128, dtype=BF)
    s = np.arange(128)[:, None]
    t = np.arange(CB)[None, :]
    c["dk"] = np.concatenate(
        [np.maximum(t - 128 * kc - s, 0) for kc in (0, 1)],
        axis=1).astype(np.float32)
    c["qdec"] = (np.arange(CB, dtype=np.float32)[None, :] + 1.0)
    j = np.arange(128, dtype=np.float32)[:, None]
    c["dcol"] = np.concatenate([255.0 - 128.0 * kc - j for kc in (0, 1)],
                               axis=1).astype(np.float32)
    c["epsc"] = np.full((128, 1), 1e-5, np.float32)
    c["l127c"] = np.full((128, 1), np.log(127.0), np.float32)
    return c


class Prog:

    PERSIST = ("ld", "sp", "bar", "st")

    def __init__(self, nc, stack):
        self.nc = nc
        self.stack = stack
        self.ops = []
        self.cnt = {}
        self.base = {}
        self.phys_of = {}
        self.phys = []
        self.phys_cnt = []
        self.free_phys = []
        self.barn = 0

    def sem(self, n):
        if n not in self.phys_of:
            if self.free_phys:
                idx = self.free_phys.pop()
            else:
                idx = len(self.phys)
                self.phys.append(
                    self.stack.enter_context(self.nc.semaphore(n)))
                self.phys_cnt.append(0)
            self.phys_of[n] = idx
            self.base[n] = self.phys_cnt[idx]
            self.cnt[n] = 0
        return self.phys[self.phys_of[n]]

    def phase_end(self):
        for n in list(self.phys_of):
            if n not in self.PERSIST:
                self.free_phys.append(self.phys_of[n])
                del self.phys_of[n], self.base[n], self.cnt[n]

    def v(self, n):
        self.sem(n)
        return self.cnt[n]

    def inc_prev(self, eng, semn, k=1):
        h = self.sem(semn)
        self.cnt[semn] += k
        self.phys_cnt[self.phys_of[semn]] += k
        for i in range(len(self.ops) - 1, -1, -1):
            e0, f0, m0 = self.ops[i]
            if e0 == eng:
                assert m0 is None or m0[0] == "inc", \
                    f"inc_prev would stack on {m0}"
                ev = ("inc", semn, k)
                meta = ev if m0 is None else (("multi",) +
                                              (m0[1:] if m0[0] == "multi"
                                               else (m0,)) + (ev,))
                self.ops[i] = (e0, lambda e, f0=f0, h=h, k=k:
                               f0(e).then_inc(h, k), meta)
                return self.cnt[semn]
        raise RuntimeError("no prior op for " + eng)

    def op(self, eng, fn, meta=None):
        self.ops.append((eng, fn, meta))

    def wait(self, eng, semn, val=None):
        h = self.sem(semn)
        v = self.cnt[semn] if val is None else val
        pv = self.base[semn] + v
        self.op(eng, lambda e, h=h, v=pv: e.wait_ge(h, v),
                ("wait", semn, v))

    def inc(self, eng, fn, semn, k=1):
        h = self.sem(semn)
        self.cnt[semn] += k
        self.phys_cnt[self.phys_of[semn]] += k
        self.op(eng, lambda e, fn=fn, h=h, k=k: fn(e).then_inc(h, k),
                ("inc", semn, k))
        return self.cnt[semn]

    def dma(self, eng, out, in_, semn="ld"):
        h = self.sem(semn)
        self.cnt[semn] += 16
        self.phys_cnt[self.phys_of[semn]] += 16
        self.op(eng, lambda e, out=out, in_=in_, h=h:
                e.dma_start(out=out, in_=in_).then_inc(h, 16),
                ("inc", semn, 16))
        return self.cnt[semn]

    def barrier(self):
        self.sem("bar")
        self.barn += 1
        tgt = 5 * self.barn
        ldv = self.cnt.get("ld", 0)
        spv = self.cnt.get("sp", 0)
        for e in ENGINES:
            if e == "sync" and "ld" in self.phys_of:
                self.wait(e, "ld", ldv)
            if e == "gpsimd" and "sp" in self.phys_of:
                self.wait(e, "sp", spv)
            h = self.sem("bar")
            self.op(e, lambda en, h=h: en.sem_inc(h, 1), ("inc", "bar", 1))
            self.op(e, lambda en, h=h, t=tgt: en.wait_ge(h, t),
                    ("wait", "bar", tgt))
        self.cnt["bar"] += 5
        self.phys_cnt[self.phys_of["bar"]] += 5

    def emit(self):
        with self.nc.Block() as block:
            for name in ENGINES:
                def mk(name):
                    def f(eng):
                        for e, fn, _m in self.ops:
                            if e == name:
                                fn(eng)
                    return f
                getattr(block, name)(mk(name))


def build_nc(upto=99):
    from contextlib import ExitStack
    nc = bass.Bass("TRN2", target_bir_lowering=False,
                   detect_race_conditions=False)
    dti = lambda n, s: nc.dram_tensor(n, s, F32, kind="ExternalInput").ap()
    xT = dti("xT", [E, TLOC])
    WqT, WkT = dti("WqT", [E, E]), dti("WkT", [E, E])
    WvT, WgT = dti("WvT", [E, E]), dti("WgT", [E, E])
    w1T, w2T = dti("w1T", [E, HID]), dti("w2T", [HID, E])
    brows = dti("brows", [3, E])
    bgc_d = dti("bgc", [128, 8])
    b1c_d, b2c_d = dti("b1c", [128, 32]), dti("b2c", [128, 8])
    gmc_d, btc_d = dti("gmc", [128, 8]), dti("btc", [128, 8])
    gam_d = dti("gam", [1, H])
    nd4_d, zm4_d = dti("nd4", [1, 4]), dti("zm4", [1, 4])
    dtb = lambda n, s: nc.dram_tensor(n, s, BF16, kind="ExternalInput").ap()
    kcos_d, ksin_d = dtb("kcos", [TLOC, E]), dtb("ksin", [TLOC, E])
    qcos_d, qsin_d = dtb("qcos", [TLOC, E]), dtb("qsin", [TLOC, E])
    y_d = nc.dram_tensor("y", [E, TLOC], F32, kind="ExternalOutput").ap()

    kfm_sp = nc.dram_tensor("kfm_sp", [128, NPT * 4096], BF16).ap()
    vtm_sp = nc.dram_tensor("vtm_sp", [128, 16 * 1024], BF16).ap()
    gp_sp = nc.dram_tensor("gp_sp", [64, 2 * 8192], F32).ap()
    h1q_sp = nc.dram_tensor("h1q_sp", [HID, TLOC], BF16).ap()
    part_sp = nc.dram_tensor("part_sp", [E, TLOC], F32).ap()
    ag_in = nc.dram_tensor("ag_in", [2048, 64], F32).ap()
    ag_out = nc.dram_tensor("ag_out", [8192, 64], F32).ap()
    g1024_sp = nc.dram_tensor("g1024_sp", [1, 1024], F32).ap()
    h0q_dbg = nc.dram_tensor("h0q_dbg", [128, NFT * 4096], BF16).ap()
    rw_dbg = nc.dram_tensor("rw_dbg", [1, 4 * FT], F32).ap()
    w1q_dbg = nc.dram_tensor("w1q_dbg", [128, 8 * 4096], BF16).ap()
    mc_dbg = nc.dram_tensor("mc_dbg", [128, 4], F32).ap()
    ws_dbg = nc.dram_tensor("ws_dbg", [1, 8], F32).ap()
    xp_dbg = nc.dram_tensor("xp_dbg", [128, 48], F32).ap()

    CN = _consts()
    inl = {k: nc.inline_tensor(v, k) for k, v in CN.items()}

    with ExitStack() as stk:
        P = Prog(nc, stk)
        sb = lambda n, shp, dt=F32: stk.enter_context(
            nc.sbuf_tensor(n, shp, dt))
        op, wait, inc, dma, barrier = P.op, P.wait, P.inc, P.dma, P.barrier

        xnb = sb("xnb", [128, 8 * TLOC], BF16)
        WTt = sb("WTt", [128, H * 512], BF16)
        Bq = sb("Bq", [128, (H // 2) * CB], BF16)
        cons = {}
        for nm, shp, dt in [("ones_row_f", [1, 128], F32),
                            ("ones_row_b", [1, 128], BF16),
                            ("ones_col_f", [128, 1], F32),
                            ("ones_col_b", [128, 1], BF16),
                            ("ident_f", [128, 128], F32),
                            ("ident_b", [128, 128], BF16),
                            ("dcol", [128, 2], F32),
                            ("epsc", [128, 1], F32),
                            ("l127c", [128, 1], F32)]:
            cons[nm] = sb("c_" + nm, shp, dt)
        bqr = sb("bqr", [1, E], BF16)
        bkr = sb("bkr", [1, E], BF16)
        bvr = sb("bvr", [1, E], BF16)
        bgc = sb("bgc_s", [128, 8])
        b1c, b2c = sb("b1c_s", [128, 32]), sb("b2c_s", [128, 8])
        gmc, btc = sb("gmc_s", [128, 8]), sb("btc_s", [128, 8])
        gam, gsig = sb("gam_s", [1, H]), sb("gsig", [1, H])
        nd4, zm4 = sb("nd4_s", [1, 4]), sb("zm4_s", [1, 4])
        gncol = sb("gncol", [128, H])
        zcol = sb("zcol", [128, 4])
        g1024 = sb("g1024", [1, 1024])
        grows = sb("grows", [16, 64])
        dcole = sb("dcole", [128, 2 * H])
        m1rc, m2rc = sb("m1rc", [128, 1]), sb("m2rc", [128, 1])
        tcol = sb("tcol", [128, 1])
        dbgc = sb("dbgc", [128, 4]) if _DBG else None
        s1c, s2c = sb("s1c", [128, 1]), sb("s2c", [128, 1])

        from contextlib import ExitStack as ES0
        p0scope = ES0()
        for nm, shp, dt in [("dk", [128, 512], F32), ("qdec", [1, CB], F32)]:
            cons[nm] = p0scope.enter_context(
                nc.sbuf_tensor("c_" + nm, shp, dt))
        btmpf = p0scope.enter_context(nc.sbuf_tensor("btmpf", [1, E], F32))
        for nm in cons:
            dma("sync", cons[nm][:, :], inl[nm].ap())
        for i, row in enumerate([bqr, bkr, bvr]):
            tgt = dma("sync", btmpf[:, :], brows[i:i + 1, :])
            wait("vector", "ld", tgt)
            inc("vector", lambda v, row=row: v.tensor_copy(
                row[:, :], btmpf[:, :]), "p0v")
            wait("sync", "p0v")
        dma("sync", bgc[:, :], bgc_d)
        dma("sync", b1c[:, :], b1c_d)
        dma("sync", b2c[:, :], b2c_d)
        dma("sync", gmc[:, :], gmc_d)
        dma("sync", btc[:, :], btc_d)
        dma("sync", gam[:, :], gam_d)
        dma("sync", nd4[:, :], nd4_d)
        dma("sync", zm4[:, :], zm4_d)
        p0ld = P.v("ld")
        wait("scalar", "ld", p0ld)
        inc("scalar", lambda s: s.activation(gsig[:, :], gam[:, :],
                                             AF.Sigmoid), "p0s")
        with nc.psum_tensor("p0a", [128, 64], F32) as p0a, \
             nc.psum_tensor("p0b", [16, 64], F32) as p0b:
            wait("tensor", "ld", p0ld)
            wait("tensor", "p0s")
            op("tensor", lambda t: t.matmul(
                p0a[:, 0:H], cons["ones_row_f"][:, :], gsig[:, :],
                start=True, stop=True))
            op("tensor", lambda t: t.matmul(
                p0a[:, 32:36], cons["ones_row_f"][:, :], zm4[:, :],
                start=True, stop=True))
            inc("tensor", lambda t: t.matmul(
                p0b[:, :], gsig[:, :], cons["ones_row_f"][0:1, 0:64],
                start=True, stop=True), "p0t")
            wait("vector", "p0t")
            op("vector", lambda v: v.tensor_scalar_mul(
                gncol[:, :], p0a[:, 0:H], -1.0))
            op("vector", lambda v: v.tensor_copy(zcol[:, :], p0a[:, 32:36]))
            inc("vector", lambda v: v.tensor_copy(grows[:, :], p0b[:, :]),
                "p0v")
        wait("gpsimd", "p0v")
        dma("gpsimd", g1024_sp.rearrange("x (h d) -> (x h) d", h=16),
            grows[:, :], semn="sp")
        dma("gpsimd", g1024[:, :], g1024_sp, semn="sp")
        barrier()
        for h in range(H):
            inc("scalar", lambda s, h=h: s.activation(
                WTt[:, h * 512:(h + 1) * 512], cons["dk"][:, :], AF.Exp,
                scale=gncol[:, h:h + 1]), "p0s")
            inc("scalar", lambda s, h=h: s.activation(
                dcole[:, 2 * h:2 * h + 2], cons["dcol"][:, :], AF.Exp,
                scale=gncol[:, h:h + 1]), "p0s")
        with nc.sbuf_tensor("brt", [1, H * CB], BF16) as brt, \
             nc.psum_tensor("pbq", [128, CB], F32) as pbq:
            for h in range(H):
                inc("scalar", lambda s, h=h: s.activation(
                    brt[:, h * CB:(h + 1) * CB], cons["qdec"][:, :], AF.Exp,
                    scale=gncol[0:1, h:h + 1]), "p0s")
            wait("tensor", "p0s")
            for pi in range(H // 2):
                if pi:
                    wait("tensor", "p0v")
                op("tensor", lambda t, pi=pi: t.matmul(
                    pbq[0:64, :], cons["ones_row_b"][0:1, 0:64],
                    brt[:, (2 * pi) * CB:(2 * pi + 1) * CB],
                    start=True, stop=True))
                inc("tensor", lambda t, pi=pi: t.matmul(
                    pbq[64:128, :], cons["ones_row_b"][0:1, 0:64],
                    brt[:, (2 * pi + 1) * CB:(2 * pi + 2) * CB],
                    start=True, stop=True), "p0t")
                wait("vector", "p0t")
                inc("vector", lambda v, pi=pi: v.tensor_copy(
                    Bq[:, pi * CB:(pi + 1) * CB], pbq[:, :]), "p0v")
            wait("tensor", "p0v")
        barrier()
        p0scope.close()
        P.phase_end()
        if upto < 1:
            nc._prog = P
            P.emit()
            return nc

        from contextlib import ExitStack as ES
        with ES() as p1:
            s1 = lambda n, shp, dt=F32: p1.enter_context(
                nc.sbuf_tensor(n, shp, dt))
            x_sb = s1("x_sb", [128, 8 * PT])
            xsq = s1("xsq", [128, 2 * PT])
            Wk_b = s1("Wk_b", [128, 8 * 1024], BF16)
            Wv_b = s1("Wv_b", [128, 8 * 1024], BF16)
            wstg = s1("wstg", [128, 1024])
            rowt = s1("rowt", [1, 3 * PT])
            mub, rsb = s1("mub", [128, PT]), s1("rsb", [128, PT])
            tA = s1("tA", [128, PT])
            ktm = s1("ktm", [128, 2 * 1024], BF16)
            ksw = s1("ksw", [128, 1024], BF16)
            kr1 = s1("kr1", [128, 1024], BF16)
            krot = s1("krot", [128, 2 * 1024], BF16)
            ktil = s1("ktil", [128, 2 * 1024], BF16)
            kstg = s1("kstg", [128, 4096], BF16)
            vstg = s1("vstg", [128, 4096], BF16)
            gpev = s1("gpev", [64, 2048])
            ctab = s1("ctab", [128, 2 * 1024], BF16)
            stab = s1("stab", [128, 2 * 1024], BF16)
            for wd, wb in ((WkT, Wk_b), (WvT, Wv_b)):
                for ec in range(8):
                    tgt = dma("sync", wstg[:, :],
                              wd[128 * ec:128 * (ec + 1), :])
                    wait("vector", "ld", tgt)
                    inc("vector", lambda v, wb=wb, ec=ec: v.tensor_copy(
                        wb[:, 1024 * ec:1024 * (ec + 1)], wstg[:, :]), "p1w")
                    wait("sync", "p1w")
            barrier()
            with nc.psum_tensor("ps_sum", [1, PT], F32) as ps_sum, \
                 nc.psum_tensor("ps_sq", [1, PT], F32) as ps_sq, \
                 nc.psum_tensor("ps_k0", [128, 512], F32) as ps_k0, \
                 nc.psum_tensor("ps_k1", [128, 512], F32) as ps_k1, \
                 nc.psum_tensor("ps_t0", [128, 128], BF16) as ps_t0, \
                 nc.psum_tensor("ps_t1", [128, 128], BF16) as ps_t1, \
                 nc.psum_tensor("ps_gp", [64, 1024], F32) as ps_gp:
                ps_t = (ps_t0, ps_t1)
                ps_bc = ps_k0
                gp_dma_tgt = [0]
                for pt in range(NPT):
                    if _P1LVL < 1:
                        barrier()
                        continue
                    xt = dma("sync",
                             x_sb[:, :].rearrange("p (c t) -> p c t", c=8),
                             xT[:, PT * pt:PT * (pt + 1)]
                             .rearrange("(c p) t -> p c t", p=128))
                    wait("scalar", "ld", xt)
                    for fc in range(8):
                        k = pt * 8 + fc
                        if k >= 2 and _P1SUB >= 2:
                            wait("scalar", "sqm", k - 1)
                        inc("scalar", lambda s, fc=fc: s.activation(
                            xsq[:, (fc % 2) * PT:(fc % 2 + 1) * PT],
                            x_sb[:, PT * fc:PT * (fc + 1)], AF.Square), "sqa")
                    if _P1SUB < 2:
                        barrier()
                        continue
                    wait("tensor", "ld", xt)
                    for fc in range(8):
                        op("tensor", lambda t, fc=fc: t.matmul(
                            ps_sum[:, :], cons["ones_col_f"][:, :],
                            x_sb[:, PT * fc:PT * (fc + 1)],
                            start=(fc == 0), stop=(fc == 7)))
                    for fc in range(8):
                        wait("tensor", "sqa", pt * 8 + fc + 1)
                        inc("tensor", lambda t, fc=fc: t.matmul(
                            ps_sq[:, :], cons["ones_col_f"][:, :],
                            xsq[:, (fc % 2) * PT:(fc % 2 + 1) * PT],
                            start=(fc == 0), stop=(fc == 7)), "sqm")
                    if _P1SUB < 3:
                        barrier()
                        continue
                    r0 = rowt[0:1, 0:PT]
                    r1 = rowt[0:1, PT:2 * PT]
                    r2 = rowt[0:1, 2 * PT:3 * PT]
                    wait("vector", "sqm", 8 * (pt + 1))
                    op("vector", lambda v: v.tensor_scalar_mul(
                        r0, ps_sum[:, :], 1.0 / 1024.0))
                    if _P1SUB == 25:
                        inc("vector", lambda v: v.engine_nop(), "rw1")
                        barrier()
                        continue
                    op("vector", lambda v: v.tensor_scalar_mul(
                        r1, ps_sq[:, :], 1.0 / 1024.0))
                    if _P1SUB == 26:
                        inc("vector", lambda v: v.engine_nop(), "rw1")
                        barrier()
                        continue
                    op("vector", lambda v: v.tensor_mul(r2, r0, r0))
                    inc("vector", lambda v: v.tensor_sub(r1, r1, r2), "rw1")
                    if _P1SUB == 27:
                        barrier()
                        continue
                    wait("scalar", "rw1")
                    op("scalar", lambda s: s.activation(
                        r2, r1, AF.Ln, bias=cons["epsc"][0:1, :]))
                    inc("scalar", lambda s: s.activation(r1, r2, AF.Exp,
                                                         scale=-0.5), "rw3")
                    if _P1SUB < 4:
                        barrier()
                        continue
                    wait("tensor", "rw3")
                    inc("tensor", lambda t: t.matmul(
                        ps_bc[:, :], cons["ones_row_f"][:, :], r0,
                        start=True, stop=True), "bcd")
                    wait("vector", "bcd")
                    inc("vector", lambda v: v.tensor_copy(
                        mub[:, :], ps_bc[:, :]), "bce")
                    wait("tensor", "bce")
                    inc("tensor", lambda t: t.matmul(
                        ps_bc[:, :], cons["ones_row_f"][:, :], r1,
                        start=True, stop=True), "bcd")
                    wait("vector", "bcd")
                    op("vector", lambda v: v.tensor_copy(
                        rsb[:, :], ps_bc[:, :]))
                    if _P1SUB < 5:
                        inc("vector", lambda v: v.engine_nop(), "xnd")
                        barrier()
                        continue
                    for fc in range(8):
                        op("vector", lambda v, fc=fc: v.tensor_sub(
                            tA[:, :], x_sb[:, PT * fc:PT * (fc + 1)],
                            mub[:, :]))
                        op("vector", lambda v, fc=fc, pt=pt: v.tensor_mul(
                            xnb[:, 2048 * fc + PT * pt:
                                2048 * fc + PT * (pt + 1)],
                            tA[:, :], rsb[:, :]))
                    inc("vector", lambda v: v.engine_nop(), "xnd")

                    for tc in range(4):
                        if _P1LVL < 2:
                            continue
                        tg = pt * 4 + tc
                        kc = tc % 2
                        if tg >= 2:
                            wait("sync", "krd", tg - 1)
                        t_c = dma("sync",
                                  ctab[:, kc * 1024:(kc + 1) * 1024],
                                  kcos_d[PT * pt + 128 * tc:
                                         PT * pt + 128 * (tc + 1), :])
                        t_s = dma("sync",
                                  stab[:, kc * 1024:(kc + 1) * 1024],
                                  ksin_d[PT * pt + 128 * tc:
                                         PT * pt + 128 * (tc + 1), :])
                        wait("tensor", "xnd", pt + 1)
                        if tg >= 1:
                            wait("tensor", "kve", 2 * tg)
                        for nh, pk in ((0, ps_k0), (1, ps_k1)):
                            for ec in range(8):
                                op("tensor", lambda t, ec=ec, nh=nh, pk=pk,
                                   pt=pt, tc=tc: t.matmul(
                                       pk[:, :],
                                       xnb[:, 2048 * ec + PT * pt + 128 * tc:
                                           2048 * ec + PT * pt
                                           + 128 * (tc + 1)],
                                       Wk_b[:, 1024 * ec + 512 * nh:
                                            1024 * ec + 512 * (nh + 1)],
                                       start=(ec == 0), stop=False))
                            op("tensor", lambda t, nh=nh, pk=pk: t.matmul(
                                pk[:, :], cons["ones_row_b"][:, :],
                                bkr[0:1, 512 * nh:512 * (nh + 1)],
                                start=False, stop=True))
                        P.inc_prev("tensor", "kmm")
                        wait("scalar", "kmm")
                        op("scalar", lambda s, kc=kc: s.copy(
                            ktm[:, kc * 1024:kc * 1024 + 512], ps_k0[:, :]))
                        inc("scalar", lambda s, kc=kc: s.copy(
                            ktm[:, kc * 1024 + 512:(kc + 1) * 1024],
                            ps_k1[:, :]), "kve")
                        if _P1LVL < 3:
                            continue
                        wait("gpsimd", "kve")
                        if tg >= 1:
                            wait("gpsimd", "krd", tg)
                        op("gpsimd", lambda e, kc=kc: e.tensor_scalar_mul(
                            ksw[:, 0:1024:2],
                            ktm[:, kc * 1024 + 1:(kc + 1) * 1024:2], -1.0))
                        inc("gpsimd", lambda e, kc=kc: e.tensor_copy(
                            ksw[:, 1:1024:2],
                            ktm[:, kc * 1024:(kc + 1) * 1024:2]), "ksd")
                        wait("vector", "ksd")
                        wait("vector", "ld", max(t_c, t_s))
                        if tg >= 2:
                            wait("vector", "pmm", tg // 2)
                        op("vector", lambda v, kc=kc: v.tensor_mul(
                            krot[:, kc * 1024:(kc + 1) * 1024],
                            ktm[:, kc * 1024:(kc + 1) * 1024],
                            ctab[:, kc * 1024:(kc + 1) * 1024]))
                        op("vector", lambda v, kc=kc: v.tensor_mul(
                            kr1[:, :], ksw[:, :],
                            stab[:, kc * 1024:(kc + 1) * 1024]))
                        inc("vector", lambda v, kc=kc: v.tensor_add(
                            krot[:, kc * 1024:(kc + 1) * 1024],
                            krot[:, kc * 1024:(kc + 1) * 1024],
                            kr1[:, :]), "krd")
                        if _P1LVL < 4:
                            continue
                        wait("tensor", "krd")
                        for jj in range(8):
                            jg = tg * 8 + jj
                            if jg >= 2:
                                wait("tensor", "kte", jg - 1)
                            inc("tensor", lambda t, jj=jj, kc=kc:
                                t.transpose(
                                    ps_t[jj % 2][:, :],
                                    krot[:, kc * 1024 + 128 * jj:
                                         kc * 1024 + 128 * (jj + 1)],
                                    cons["ident_b"][:, :]), "ktt")
                            wait("vector", "ktt", tg * 8 + jj + 1)
                            inc("vector", lambda v, tc=tc, jj=jj:
                                v.tensor_copy(
                                    kstg[:, 1024 * tc + 128 * jj:
                                         1024 * tc + 128 * (jj + 1)],
                                    ps_t[jj % 2][:, :]), "kte")
                        if _P1LVL < 5:
                            continue
                        wait("tensor", "kve", 2 * tg + 1)
                        for nh, pk in ((0, ps_k0), (1, ps_k1)):
                            for ec in range(8):
                                op("tensor", lambda t, ec=ec, nh=nh, pk=pk,
                                   pt=pt, tc=tc: t.matmul(
                                       pk[:, :],
                                       xnb[:, 2048 * ec + PT * pt + 128 * tc:
                                           2048 * ec + PT * pt
                                           + 128 * (tc + 1)],
                                       Wv_b[:, 1024 * ec + 512 * nh:
                                            1024 * ec + 512 * (nh + 1)],
                                       start=(ec == 0), stop=False))
                            op("tensor", lambda t, nh=nh, pk=pk: t.matmul(
                                pk[:, :], cons["ones_row_b"][:, :],
                                bvr[0:1, 512 * nh:512 * (nh + 1)],
                                start=False, stop=True))
                        P.inc_prev("tensor", "vmm")
                        wait("scalar", "vmm")
                        op("scalar", lambda s, tc=tc: s.copy(
                            vstg[:, 1024 * tc:1024 * tc + 512], ps_k0[:, :]))
                        inc("scalar", lambda s, tc=tc: s.copy(
                            vstg[:, 1024 * tc + 512:1024 * (tc + 1)],
                            ps_k1[:, :]), "kve")
                        if _P1LVL < 6:
                            continue
                        if tg >= 2:
                            wait("vector", "gsm", tg // 2)
                        for h in range(H):
                            op("vector", lambda v, h=h, kc=kc:
                               v.tensor_scalar_mul(
                                   ktil[:, kc * 1024 + 64 * h:
                                        kc * 1024 + 64 * (h + 1)],
                                   krot[:, kc * 1024 + 64 * h:
                                        kc * 1024 + 64 * (h + 1)],
                                   dcole[:, 2 * h + kc:2 * h + kc + 1]))
                        inc("vector", lambda v: v.engine_nop(), "ktl")
                        if kc == 1:
                            sbk = pt * 2 + tc // 2
                            wait("tensor", "ktl")
                            wait("tensor", "kve", 2 * tg + 2)
                            for h in range(H):
                                for k2 in (0, 1):
                                    op("tensor", lambda t, h=h, k2=k2,
                                       tc=tc: t.matmul(
                                           ps_gp[:, 64 * h:64 * (h + 1)],
                                           ktil[:, k2 * 1024 + 64 * h:
                                                k2 * 1024 + 64 * (h + 1)],
                                           vstg[:, 1024 * (tc - 1 + k2)
                                                + 64 * h:
                                                1024 * (tc - 1 + k2)
                                                + 64 * (h + 1)],
                                           start=(k2 == 0),
                                           stop=(k2 == 1)))
                            P.inc_prev("tensor", "gsm")
                            wait("vector", "gsm")
                            wait("vector", "sp", gp_dma_tgt[0])
                            inc("vector", lambda v: v.tensor_copy(
                                gpev[:, 0:1024], ps_gp[:, :]), "gp1")
                            wait("tensor", "gp1")
                            for h in range(H):
                                for k2 in (0, 1):
                                    op("tensor", lambda t, h=h, k2=k2,
                                       tc=tc: t.matmul(
                                           ps_gp[:, 64 * h:64 * (h + 1)],
                                           krot[:, k2 * 1024 + 64 * h:
                                                k2 * 1024 + 64 * (h + 1)],
                                           vstg[:, 1024 * (tc - 1 + k2)
                                                + 64 * h:
                                                1024 * (tc - 1 + k2)
                                                + 64 * (h + 1)],
                                           start=(k2 == 0),
                                           stop=(k2 == 1)))
                            P.inc_prev("tensor", "pmm")
                            wait("vector", "pmm")
                            inc("vector", lambda v: v.tensor_copy(
                                gpev[:, 1024:2048], ps_gp[:, :]), "gpe")
                            wait("gpsimd", "gpe")
                            dma("gpsimd",
                                gp_sp[:, 1024 * sbk:1024 * (sbk + 1)],
                                gpev[:, 0:1024], semn="sp")
                            dma("gpsimd",
                                gp_sp[:, 8192 + 1024 * sbk:
                                      8192 + 1024 * (sbk + 1)],
                                gpev[:, 1024:2048], semn="sp")
                            gp_dma_tgt[0] = P.v("sp")
                    if _P1LVL >= 7:
                        wait("gpsimd", "kte", 32 * (pt + 1))
                        dma("gpsimd", kfm_sp[:, 4096 * pt:4096 * (pt + 1)],
                            kstg[:, :], semn="sp")
                        wait("gpsimd", "kve", 8 * (pt + 1))
                        dma("gpsimd", vtm_sp[:, 4096 * pt:4096 * (pt + 1)],
                            vstg[:, :], semn="sp")
                    barrier()
        barrier()
        P.phase_end()
        if upto < 2:
            nc._prog = P
            P.emit()
            return nc

        p23 = stk.enter_context(ES())
        Rb16 = p23.enter_context(
            nc.sbuf_tensor("Rb16", [128, NBLK * 1024], BF16))
        Zb16 = p23.enter_context(
            nc.sbuf_tensor("Zb16", [128, NBLK * 1024], BF16))
        with ES() as p2:
            s2 = lambda n, shp, dt=F32: p2.enter_context(
                nc.sbuf_tensor(n, shp, dt))
            GtPt = s2("GtPt", [128, 8192])
            RpZp = s2("RpZp", [128, 8192])
            gC = s2("gC", [64, 1024])
            rowe = s2("rowe", [1, 1024])
            gat = s2("gat", [128, 4096])
            RacZac = s2("RacZac", [128, 1024])
            tt1 = s2("tt1", [128, 1024])
            wst2 = s2("wst2", [128, 1024])
            wabs = s2("wabs", [128, 1024])
            Gt = lambda a, b: GtPt[0:64, a:b]
            Pt = lambda a, b: GtPt[64:128, a:b]
            Rp = lambda a, b: RpZp[0:64, a:b]
            Zp = lambda a, b: RpZp[64:128, a:b]
            with nc.psum_tensor("ps_r", [64, 1024], F32) as ps_r, \
                 nc.psum_tensor("ps_w", [64, 1024], F32) as ps_w, \
                 nc.psum_tensor("ps_s2", [128, 1], F32) as ps_s2, \
                 nc.psum_tensor("ps_ws", [1, 512], F32) as ps_ws, \
                 nc.psum_tensor("ps_ws2", [1, 512], F32) as ps_ws2:
                t1 = dma("gpsimd", GtPt[0:64, :], gp_sp[:, 0:8192],
                         semn="sp")
                t2 = dma("gpsimd", GtPt[64:128, :], gp_sp[:, 8192:16384],
                         semn="sp")
                inc("scalar", lambda s: s.activation(
                    rowe[:, :], g1024[:, :], AF.Exp, scale=-256.0), "q2s")
                wait("tensor", "q2s")
                op("tensor", lambda t: t.matmul(
                    ps_r[:, 0:512], cons["ones_row_f"][0:1, 0:64],
                    rowe[0:1, 0:512], start=True, stop=True))
                inc("tensor", lambda t: t.matmul(
                    ps_r[:, 512:1024], cons["ones_row_f"][0:1, 0:64],
                    rowe[0:1, 512:1024], start=True, stop=True), "q2t")
                wait("vector", "q2t")
                inc("vector", lambda v: v.tensor_copy(gC[:, :], ps_r[:, :]),
                    "q2v")
                wait("vector", "sp", max(t1, t2))
                op("vector", lambda v: v.memset(Rp(0, 1024), 0.0))
                op("vector", lambda v: v.memset(Zp(7 * 1024, 8192), 0.0))
                for b in range(1, NBLK):
                    op("vector", lambda v, b=b: v.tensor_mul(
                        tt1[0:64, :], Rp(1024 * (b - 1), 1024 * b),
                        gC[:, :]))
                    op("vector", lambda v, b=b: v.tensor_add(
                        Rp(1024 * b, 1024 * (b + 1)), tt1[0:64, :],
                        Gt(1024 * (b - 1), 1024 * b)))
                for b in range(NBLK - 2, -1, -1):
                    op("vector", lambda v, b=b: v.tensor_add(
                        Zp(1024 * b, 1024 * (b + 1)),
                        Zp(1024 * (b + 1), 1024 * (b + 2)),
                        Pt(1024 * (b + 1), 1024 * (b + 2))))
                op("vector", lambda v: v.tensor_mul(
                    tt1[0:64, :], Rp(7 * 1024, 8192), gC[:, :]))
                op("vector", lambda v: v.tensor_add(
                    tt1[0:64, :], tt1[0:64, :], Gt(7 * 1024, 8192)))
                inc("vector", lambda v: v.tensor_add(
                    tt1[64:128, :], Zp(0, 1024), Pt(0, 1024)), "q2v")
                wait("gpsimd", "q2v")
                dma("gpsimd",
                    ag_in.rearrange("(h r) d -> h r d", r=128)[:, 0:64, :]
                    .transpose([1, 0, 2]),
                    tt1[0:64, :].rearrange("i (h d) -> i h d", h=16),
                    semn="sp")
                t3 = dma("gpsimd",
                         ag_in.rearrange("(h r) d -> h r d",
                                         r=128)[:, 64:128, :]
                         .transpose([1, 0, 2]),
                         tt1[64:128, :].rearrange("i (h d) -> i h d", h=16),
                         semn="sp")
                wait("gpsimd", "sp", t3)
                inc("gpsimd", lambda e: e.collective_compute(
                    "AllGather", ALU.bypass,
                    replica_groups=[[0, 1, 2, 3], [4, 5, 6, 7]],
                    ins=[ag_in], outs=[ag_out]), "agd")
                wait("gpsimd", "agd")
                t4 = dma("gpsimd",
                         gat[:, :].rearrange("p (j h d) -> p j h d",
                                             j=4, h=16),
                         ag_out.rearrange("(j h two i) d -> (two i) j h d",
                                          j=4, h=16, two=2), semn="sp")

                wait("tensor", "q2t", 1)
                nw = 0
                for wi, wd, nchunk, cols, mc, sc in (
                        (0, w1T, 8, HID, m1rc, s1c),
                        (1, w2T, 32, E, m2rc, s2c)):
                    psw = ps_ws if wi == 0 else ps_ws2
                    npc = nchunk * (cols // 1024)
                    k0 = 0
                    for ch in range(nchunk):
                        for pc in range(cols // 1024):
                            tgt = dma("sync", wst2[:, :],
                                      wd[128 * ch:128 * (ch + 1),
                                         1024 * pc:1024 * (pc + 1)])
                            wait("scalar", "ld", tgt)
                            inc("scalar", lambda s: s.activation(
                                wabs[:, :], wst2[:, :], AF.Abs), "wsa")
                            wait("sync", "wsa")
                            wait("tensor", "wsa", P.v("wsa"))
                            for nh in range(2):
                                op("tensor", lambda t, nh=nh, k0=k0,
                                   npc=npc, psw=psw: t.matmul(
                                       psw[:, 0:512],
                                       cons["ones_col_f"][:, :],
                                       wabs[:, 512 * nh:512 * (nh + 1)],
                                       start=(k0 == 0 and nh == 0),
                                       stop=(k0 == npc - 1 and nh == 1)))
                            P.inc_prev("tensor", "wsm")
                            wait("scalar", "wsm", P.v("wsm") + 0)
                            k0 += 1
                            nw += 1
                    wait("vector", "wsm", P.v("wsm"))
                    inc("vector", lambda v, psw=psw: v.tensor_reduce(
                        rowe[0:1, 0:1], psw[0:1, 0:512], AX.X, ALU.add),
                        "wm1")
                    wait("vector", "wm1", P.v("wm1"))
                    op("vector", lambda v: v.tensor_scalar(
                        rowe[0:1, 1:2], rowe[0:1, 0:1], 1.0 / (E * HID),
                        EPS, ALU.mult, ALU.max))
                    inc("vector", lambda v: v.engine_nop(), "wm2")
                    if _DBG:
                        wait("gpsimd", "wm2", P.v("wm2"))
                        dma("gpsimd", ws_dbg[:, 4 * wi:4 * wi + 2],
                            rowe[0:1, 0:2], semn="sp")
                        wait("vector", "sp", P.v("sp"))
                    wait("tensor", "wm2", P.v("wm2"))
                    inc("tensor", lambda t: t.matmul(
                        ps_s2[:, :], cons["ones_row_f"][:, :],
                        rowe[0:1, 1:2], start=True, stop=True), "wm3")
                    wait("vector", "wm3", P.v("wm3"))
                    inc("vector", lambda v, mc=mc: v.tensor_copy(
                        mc[:, :], ps_s2[:, :]), "wm4")
                    wait("scalar", "wm4", P.v("wm4"))
                    op("scalar", lambda s, mc=mc: s.activation(
                        tcol[:, :], mc[:, :], AF.Ln))
                    inc("scalar", lambda s, sc=sc: s.activation(
                        sc[:, :], tcol[:, :], AF.Exp, scale=-1.0), "wm5")
                    wait("tensor", "wm2", P.v("wm2"))
                wait("tensor", "wm3", P.v("wm3"))
                wait("vector", "wm5", P.v("wm5"))

                wait("vector", "sp", t4)
                op("vector", lambda v: v.memset(RacZac[:, :], 0.0))
                wait("scalar", "wm3", P.v("wm3"))
                for j in range(4):
                    inc("scalar", lambda s, j=j: s.activation(
                        rowe[:, :], g1024[:, :], AF.Exp,
                        scale=nd4[0:1, j:j + 1]), "bd1")
                    wait("tensor", "bd1")
                    op("tensor", lambda t: t.matmul(
                        ps_w[:, 0:512], cons["ones_row_f"][0:1, 0:64],
                        rowe[0:1, 0:512], start=True, stop=True))
                    inc("tensor", lambda t: t.matmul(
                        ps_w[:, 512:1024], cons["ones_row_f"][0:1, 0:64],
                        rowe[0:1, 512:1024], start=True, stop=True), "bd2")
                    wait("vector", "bd2")
                    op("vector", lambda v, j=j: v.tensor_mul(
                        tt1[0:64, :], gat[0:64, 1024 * j:1024 * (j + 1)],
                        ps_w[:, :]))
                    op("vector", lambda v: v.tensor_add(
                        RacZac[0:64, :], RacZac[0:64, :], tt1[0:64, :]))
                    inc("vector", lambda v, j=j: v.scalar_tensor_tensor(
                        RacZac[64:128, :],
                        gat[64:128, 1024 * j:1024 * (j + 1)],
                        zcol[64:128, j:j + 1], RacZac[64:128, :],
                        ALU.mult, ALU.add), "bd3")
                    wait("tensor", "bd3")
                    wait("scalar", "bd2", P.v("bd2"))
                for b in range(NBLK):
                    inc("scalar", lambda s, b=b: s.activation(
                        rowe[:, :], g1024[:, :], AF.Exp, scale=-256.0 * b),
                        "bd4")
                    wait("tensor", "bd4")
                    op("tensor", lambda t: t.matmul(
                        ps_w[:, 0:512], cons["ones_row_f"][0:1, 0:64],
                        rowe[0:1, 0:512], start=True, stop=True))
                    inc("tensor", lambda t: t.matmul(
                        ps_w[:, 512:1024], cons["ones_row_f"][0:1, 0:64],
                        rowe[0:1, 512:1024], start=True, stop=True), "bd5")
                    wait("vector", "bd5")
                    op("vector", lambda v: v.tensor_mul(
                        tt1[0:64, :], RacZac[0:64, :], ps_w[:, :]))
                    op("vector", lambda v, b=b: v.tensor_add(
                        Rb16[0:64, 1024 * b:1024 * (b + 1)], tt1[0:64, :],
                        Rp(1024 * b, 1024 * (b + 1))))
                    inc("vector", lambda v, b=b: v.tensor_add(
                        Zb16[64:128, 1024 * b:1024 * (b + 1)],
                        Zp(1024 * b, 1024 * (b + 1)),
                        RacZac[64:128, :]), "bd6")
                    wait("tensor", "bd6")
                    wait("scalar", "bd5", P.v("bd5"))
                wait("gpsimd", "bd6", P.v("bd6"))
                dma("gpsimd", Rb16[64:128, :], Rb16[0:64, :], semn="sp")
                dma("gpsimd", Zb16[0:64, :], Zb16[64:128, :], semn="sp")
            barrier()
        barrier()
        P.phase_end()
        if upto < 3:
            nc._prog = P
            P.emit()
            return nc

        with ES() as p3:
            s3 = lambda n, shp, dt=F32: p3.enter_context(
                nc.sbuf_tensor(n, shp, dt))
            Wq_b = s3("Wq_b", [128, 8 * 1024], BF16)
            Wg_b = s3("Wg_b", [128, 8 * 1024], BF16)
            wstg3 = s3("wstg3", [128, 1024])
            Kb = s3("Kb", [128, 2 * 2048], BF16)
            Vb = s3("Vb", [128, 2 * 2048], BF16)
            x3 = s3("x3", [128, 2048])
            qtm = s3("qtm", [128, 2 * 1024], BF16)
            qsw = s3("qsw", [128, 1024], BF16)
            qr1 = s3("qr1", [128, 1024], BF16)
            qrot = s3("qrot", [128, 1024], BF16)
            qfm = s3("qfm", [128, 2048], BF16)
            qd = s3("qd", [128, 2048], BF16)
            gate = s3("gate", [128, 2048], BF16)
            swt = s3("swt", [128, 2 * 512], BF16)
            otm = s3("otm", [128, 256])
            outsb = s3("outsb", [128, 2048])
            ctb3 = s3("ctb3", [128, 2 * 1024], BF16)
            stb3 = s3("stb3", [128, 2 * 1024], BF16)
            for wd, wb in ((WqT, Wq_b), (WgT, Wg_b)):
                for ec in range(8):
                    tgt = dma("sync", wstg3[:, :],
                              wd[128 * ec:128 * (ec + 1), :])
                    wait("vector", "ld", tgt)
                    inc("vector", lambda v, wb=wb, ec=ec: v.tensor_copy(
                        wb[:, 1024 * ec:1024 * (ec + 1)], wstg3[:, :]),
                        "p3w")
                    wait("sync", "p3w")
            barrier()
            with nc.psum_tensor("q3_0", [128, 512], F32) as ps_q0, \
                 nc.psum_tensor("q3_1", [128, 512], F32) as ps_q1, \
                 nc.psum_tensor("q3_t0", [128, 128], BF16) as ps3_t0, \
                 nc.psum_tensor("q3_t1", [128, 128], BF16) as ps3_t1, \
                 nc.psum_tensor("s3_0", [128, 256], F32) as ps3_s0, \
                 nc.psum_tensor("s3_1", [128, 256], F32) as ps3_s1, \
                 nc.psum_tensor("o3_0", [128, 256], F32) as ps3_o0, \
                 nc.psum_tensor("o3_1", [128, 256], F32) as ps3_o1:
                ps3_g = ps3_s0
                ps3_t = (ps3_t0, ps3_t1)
                pss = (ps3_s0, ps3_s1)
                pso = (ps3_o0, ps3_o1)
                for blk in range(NBLK):
                    pr = blk % 2
                    lk = dma("gpsimd", Kb[:, 2048 * pr:2048 * (pr + 1)],
                             kfm_sp[:, 2048 * blk:2048 * (blk + 1)],
                             semn="sp")
                    lv = dma("gpsimd", Vb[:, 2048 * pr:2048 * (pr + 1)],
                             vtm_sp[:, 2048 * blk:2048 * (blk + 1)],
                             semn="sp")
                    lx = dma("sync", x3[:, :]
                             .rearrange("p (c t) -> p c t", c=8),
                             xT[:, CB * blk:CB * (blk + 1)]
                             .rearrange("(c p) t -> p c t", p=128))
                    for tc in range(2):
                        tg3 = blk * 2 + tc
                        if tg3 >= 2:
                            wait("sync", "qrd", tg3 - 1)
                        t_c = dma("sync", ctb3[:, tc * 1024:(tc + 1) * 1024],
                                  qcos_d[CB * blk + 128 * tc:
                                         CB * blk + 128 * (tc + 1), :])
                        t_s = dma("sync", stb3[:, tc * 1024:(tc + 1) * 1024],
                                  qsin_d[CB * blk + 128 * tc:
                                         CB * blk + 128 * (tc + 1), :])
                        if tg3 >= 1:
                            wait("tensor", "qev", tg3)
                        for nh, pk in ((0, ps_q0), (1, ps_q1)):
                            for ec in range(8):
                                op("tensor", lambda t, ec=ec, nh=nh, pk=pk,
                                   blk=blk, tc=tc: t.matmul(
                                       pk[:, :],
                                       xnb[:, 2048 * ec + CB * blk
                                           + 128 * tc:
                                           2048 * ec + CB * blk
                                           + 128 * (tc + 1)],
                                       Wq_b[:, 1024 * ec + 512 * nh:
                                            1024 * ec + 512 * (nh + 1)],
                                       start=(ec == 0), stop=False))
                            op("tensor", lambda t, nh=nh, pk=pk: t.matmul(
                                pk[:, :], cons["ones_row_b"][:, :],
                                bqr[0:1, 512 * nh:512 * (nh + 1)],
                                start=False, stop=True))
                        P.inc_prev("tensor", "qmm")
                        wait("scalar", "qmm")
                        op("scalar", lambda s, tc=tc: s.copy(
                            qtm[:, tc * 1024:tc * 1024 + 512], ps_q0[:, :]))
                        inc("scalar", lambda s, tc=tc: s.copy(
                            qtm[:, tc * 1024 + 512:(tc + 1) * 1024],
                            ps_q1[:, :]), "qev")
                        wait("gpsimd", "qev")
                        if tg3 >= 1:
                            wait("gpsimd", "qrd", tg3)
                        op("gpsimd", lambda e, tc=tc: e.tensor_scalar_mul(
                            qsw[:, 0:1024:2],
                            qtm[:, tc * 1024 + 1:(tc + 1) * 1024:2], -1.0))
                        inc("gpsimd", lambda e, tc=tc: e.tensor_copy(
                            qsw[:, 1:1024:2],
                            qtm[:, tc * 1024:(tc + 1) * 1024:2]), "qsd")
                        wait("vector", "qsd")
                        wait("vector", "ld", max(t_c, t_s))
                        op("vector", lambda v, tc=tc: v.tensor_mul(
                            qrot[:, :], qtm[:, tc * 1024:(tc + 1) * 1024],
                            ctb3[:, tc * 1024:(tc + 1) * 1024]))
                        op("vector", lambda v, tc=tc: v.tensor_mul(
                            qr1[:, :], qsw[:, :],
                            stb3[:, tc * 1024:(tc + 1) * 1024]))
                        inc("vector", lambda v: v.tensor_add(
                            qrot[:, :], qrot[:, :], qr1[:, :]), "qrd")
                        wait("tensor", "qrd")
                        for jj in range(8):
                            jg = tg3 * 8 + jj
                            if jg >= 2:
                                wait("tensor", "qte", jg - 1)
                            inc("tensor", lambda t, jj=jj: t.transpose(
                                ps3_t[jj % 2][:, :],
                                qrot[:, 128 * jj:128 * (jj + 1)],
                                cons["ident_b"][:, :]), "qtt")
                            wait("vector", "qtt", tg3 * 8 + jj + 1)
                            inc("vector", lambda v, jj=jj, tc=tc:
                                v.tensor_copy(
                                    qfm[:, 256 * jj + 128 * tc:
                                        256 * jj + 128 * (tc + 1)],
                                    ps3_t[jj % 2][:, :]), "qte")
                    for fc in range(8):
                        kf = blk * 8 + fc
                        if kf >= 1:
                            wait("tensor", "gev", kf)
                        for ec in range(8):
                            op("tensor", lambda t, ec=ec, fc=fc, blk=blk:
                               t.matmul(
                                   ps3_g[:, :],
                                   Wg_b[:, 1024 * ec + 128 * fc:
                                        1024 * ec + 128 * (fc + 1)],
                                   xnb[:, 2048 * ec + CB * blk:
                                       2048 * ec + CB * (blk + 1)],
                                   start=(ec == 0), stop=(ec == 7)))
                        P.inc_prev("tensor", "gmm")
                        wait("scalar", "gmm")
                        inc("scalar", lambda s, fc=fc: s.activation(
                            gate[:, 256 * fc:256 * (fc + 1)], ps3_g[:, :],
                            AF.Silu, bias=bgc[:, fc:fc + 1]), "gev")
                    wait("vector", "qte", 16 * (blk + 1))
                    for pi in range(8):
                        op("vector", lambda v, pi=pi: v.tensor_mul(
                            qd[:, 256 * pi:256 * (pi + 1)],
                            qfm[:, 256 * pi:256 * (pi + 1)],
                            Bq[:, 256 * pi:256 * (pi + 1)]))
                    inc("vector", lambda v: v.engine_nop(), "qdd")
                    wait("tensor", "sp", max(lk, lv))
                    wait("tensor", "qdd")
                    wait("tensor", "gev", blk * 8 + 8)
                    for m in range(16):
                        mg = blk * 16 + m
                        pi, hh = m // 2, m % 2
                        hb = 64 * hh
                        h = m
                        for kc in range(2):
                            if mg >= 1:
                                wait("tensor", "swd", 2 * (mg - 1) + kc + 1)
                            inc("tensor", lambda t, kc=kc, pi=pi, hb=hb,
                                pr=pr: t.matmul(
                                    pss[kc][:, :],
                                    Kb[hb:hb + 64,
                                       2048 * pr + 1024 * kc + 128 * pi:
                                       2048 * pr + 1024 * kc
                                       + 128 * (pi + 1)],
                                    qfm[hb:hb + 64, 256 * pi:256 * (pi + 1)],
                                    start=True, stop=True), "stm")
                        for kc in range(2):
                            wait("vector", "stm", 2 * mg + kc + 1)
                            if mg >= 2:
                                wait("vector", "omd", mg - 1)
                            inc("vector", lambda v, kc=kc, h=h, m=m:
                                v.tensor_mul(
                                    swt[:, 512 * (m % 2) + 256 * kc:
                                        512 * (m % 2) + 256 * (kc + 1)],
                                    pss[kc][:, :],
                                    WTt[:, 512 * h + 256 * kc:
                                        512 * h + 256 * (kc + 1)]), "swd")
                        wait("tensor", "swd")
                        if hh == 0 and pi >= 2:
                            wait("tensor", "oev", blk * 8 + pi - 1)
                        for kc in range(2):
                            op("tensor", lambda t, kc=kc, h=h, hb=hb, pr=pr,
                               pi=pi, m=m: t.matmul(
                                   pso[pi % 2][hb:hb + 64, :],
                                   Vb[:, 2048 * pr + 1024 * kc + 64 * h:
                                      2048 * pr + 1024 * kc + 64 * (h + 1)],
                                   swt[:, 512 * (m % 2) + 256 * kc:
                                       512 * (m % 2) + 256 * (kc + 1)],
                                   start=(kc == 0), stop=False))
                        op("tensor", lambda t, h=h, hb=hb, blk=blk, pi=pi:
                           t.matmul(
                               pso[pi % 2][hb:hb + 64, :],
                               Rb16[hb:hb + 64, 1024 * blk + 64 * h:
                                    1024 * blk + 64 * (h + 1)],
                               qd[hb:hb + 64, 256 * pi:256 * (pi + 1)],
                               start=False, stop=False))
                        op("tensor", lambda t, h=h, hb=hb, blk=blk, pi=pi:
                           t.matmul(
                               pso[pi % 2][hb:hb + 64, :],
                               Zb16[hb:hb + 64, 1024 * blk + 64 * h:
                                    1024 * blk + 64 * (h + 1)],
                               qfm[hb:hb + 64, 256 * pi:256 * (pi + 1)],
                               start=False, stop=True))
                        P.inc_prev("tensor", "omd")
                        if hh == 1:
                            wait("vector", "omd")
                            wait("vector", "ld", lx)
                            wait("vector", "gev", blk * 8 + 8)
                            op("vector", lambda v, pi=pi: v.tensor_mul(
                                otm[:, :], pso[pi % 2][:, :],
                                gate[:, 256 * pi:256 * (pi + 1)]))
                            inc("vector", lambda v, pi=pi, pr=pr:
                                v.tensor_add(
                                    outsb[:, 256 * pi:256 * (pi + 1)],
                                    otm[:, :],
                                    x3[:, 256 * pi:256 * (pi + 1)]), "oev")
                    wait("gpsimd", "oev", blk * 8 + 8)
                    dma("gpsimd",
                        part_sp.rearrange("(c p) T -> p c T", p=128)
                        [:, :, CB * blk:CB * (blk + 1)],
                        outsb[:, :].rearrange("p (c t) -> p c t", c=8),
                        semn="sp")
                    barrier()
        p23.close()
        barrier()
        P.phase_end()
        if upto < 4:
            nc._prog = P
            P.emit()
            return nc

        ry2sb = stk.enter_context(
            nc.sbuf_tensor("ry2sb", [128, NFT * FT], F32))
        with ES() as p4:
            s4 = lambda n, shp, dt=F32: p4.enter_context(
                nc.sbuf_tensor(n, shp, dt))
            b1outer = ES()
            w1q = b1outer.enter_context(
                nc.sbuf_tensor("w1q", [128, 8 * 4096], BF16))
            wq_scope = ES()
            s4b = lambda n, shp, dt=F32: wq_scope.enter_context(
                nc.sbuf_tensor(n, shp, dt))
            wst4 = s4b("wst4", [128, 4096])
            t4a = s4b("t4a", [128, 4096])
            ti4 = s4b("ti4", [128, 4096], I32)
            tf4 = s4b("tf4", [128, 4096])
            for ch in range(8):
                tgt = dma("sync", wst4[:, :],
                          w1T[128 * ch:128 * (ch + 1), :])
                wait("sync", "wqa", P.v("wqa") + 1)
                wait("vector", "ld", tgt)
                op("vector", lambda v: v.tensor_scalar_mul(
                    t4a[:, :], wst4[:, :], s1c[:, :]))
                inc("vector", lambda v: v.tensor_copy(ti4[:, :], t4a[:, :]),
                    "wqa")
                op("vector", lambda v: v.tensor_copy(tf4[:, :], ti4[:, :]))
                inc("vector", lambda v, ch=ch: v.tensor_scalar(
                    w1q[:, 4096 * ch:4096 * (ch + 1)], tf4[:, :],
                    -1.0, 1.0, ALU.max, ALU.min), "wqd")
            if _DBG:
                dma("gpsimd", w1q_dbg, w1q[:, :], semn="sp")
                op("vector", lambda v: v.tensor_copy(dbgc[:, 0:1],
                                                     m1rc[:, :]))
                op("vector", lambda v: v.tensor_copy(dbgc[:, 1:2],
                                                     m2rc[:, :]))
                op("vector", lambda v: v.tensor_copy(dbgc[:, 2:3],
                                                     s1c[:, :]))
                inc("vector", lambda v: v.tensor_copy(dbgc[:, 3:4],
                                                      s2c[:, :]), "dgc")
                wait("gpsimd", "dgc")
                dma("gpsimd", mc_dbg, dbgc[:, :], semn="sp")
            barrier()
            P.phase_end()
            wq_scope.close()
            with ES() as b1s:
                s41 = lambda n, shp, dt=F32: b1s.enter_context(
                    nc.sbuf_tensor(n, shp, dt))
                xq4 = s41("xq4", [128, FT], BF16)
                rwt4 = s41("rwt4", [1, 2 * FT])
                h0 = s41("h0", [128, 8 * FT])
                h0q = s41("h0q", [128, 8 * FT], BF16)
                M4 = s41("M4", [128, FT])
                X4 = s41("X4", [128, 8])
                XC = s41("XC", [128, 8])
                XP = s41("XP", [128, 8])
                rws4 = s41("rws4", [32, 128])
                sh4 = s41("sh4", [32, 128])
                h1 = s41("h1", [128, 32 * FT], BF16)
                M42 = s41("M42", [128, FT], BF16)
                t4b = s41("t4b", [128, FT])
                ti4b = s41("ti4b", [128, FT], I32)
                tf4b = s41("tf4b", [128, FT])
                for ft in range(NFT):
                    with nc.psum_tensor("f4su_%d" % ft, [1, FT], F32) as f4su, \
                         nc.psum_tensor("f4sq_%d" % ft, [1, FT], F32) as f4sq, \
                         nc.psum_tensor("f4bc_%d" % ft, [128, 2 * FT], F32) as f4bc:
                        for fc in range(8):
                            k = ft * 8 + fc
                            if k >= 1:
                                wait("scalar", "fqm", k)
                            inc("scalar", lambda s, fc=fc, ft=ft:
                                s.activation(
                                    xq4[:, :],
                                    xnb[:, 2048 * fc + FT * ft:
                                        2048 * fc + FT * (ft + 1)],
                                    AF.Square), "fqa")
                        for fc in range(8):
                            op("tensor", lambda t, fc=fc, ft=ft,
                               f4su=f4su: t.matmul(
                                f4su[:, :], cons["ones_col_b"][:, :],
                                xnb[:, 2048 * fc + FT * ft:
                                    2048 * fc + FT * (ft + 1)],
                                start=(fc == 0), stop=(fc == 7)))
                        for fc in range(8):
                            wait("tensor", "fqa", ft * 8 + fc + 1)
                            inc("tensor", lambda t, fc=fc,
                                f4sq=f4sq: t.matmul(
                                f4sq[:, :], cons["ones_col_b"][:, :],
                                xq4[:, :],
                                start=(fc == 0), stop=(fc == 7)), "fqm")
                        q0 = rwt4[0:1, 0:FT]
                        q1 = rwt4[0:1, FT:2 * FT]
                        wait("vector", "fqm", 8 * (ft + 1))
                        op("vector", lambda v, f4su=f4su: v.tensor_scalar_mul(
                            q0, f4su[:, :], 1.0 / 1024.0))
                        inc("vector", lambda v, f4sq=f4sq:
                            v.tensor_scalar_mul(
                                q1, f4sq[:, :], 1.0 / 1024.0), "fr1")
                        if _DBG and ft == 0:
                            wait("gpsimd", "fr1", P.v("fr1"))
                            dma("gpsimd", rw_dbg[:, 0:2 * FT],
                                rwt4[:, :], semn="sp")
                            wait("vector", "sp", P.v("sp"))
                        wait("tensor", "fr1")
                        inc("tensor", lambda t, f4bc=f4bc: t.matmul(
                            f4bc[:, 0:FT], cons["ones_row_f"][:, :], q0,
                            start=True, stop=True), "fbd")
                        wait("vector", "fbd")
                        op("vector", lambda v: v.tensor_mul(q0, q0, q0))
                        inc("vector", lambda v: v.tensor_sub(q1, q1, q0),
                            "fr2")
                        wait("scalar", "fr2")
                        op("scalar", lambda s: s.activation(
                            q0, q1, AF.Ln, bias=cons["epsc"][0:1, :]))
                        inc("scalar", lambda s: s.activation(
                            q1, q0, AF.Exp, scale=-0.5), "fr4")
                        if _DBG and ft == 0:
                            wait("gpsimd", "fr4", P.v("fr4"))
                            dma("gpsimd", rw_dbg[:, 2 * FT:4 * FT],
                                rwt4[:, :], semn="sp")
                            wait("tensor", "sp", P.v("sp"))
                        wait("tensor", "fr4")
                        inc("tensor", lambda t, f4bc=f4bc: t.matmul(
                            f4bc[:, FT:2 * FT], cons["ones_row_f"][:, :],
                            q1, start=True, stop=True), "fbc")
                        wait("vector", "fbc")
                        for fc in range(8):
                            op("vector", lambda v, fc=fc, ft=ft,
                               f4bc=f4bc: v.tensor_sub(
                                t4b[:, :],
                                xnb[:, 2048 * fc + FT * ft:
                                    2048 * fc + FT * (ft + 1)],
                                f4bc[:, 0:FT]))
                            op("vector", lambda v, fc=fc, f4bc=f4bc:
                               v.tensor_mul(
                                t4b[:, :], t4b[:, :], f4bc[:, FT:2 * FT]))
                            op("vector", lambda v, fc=fc: v.tensor_scalar(
                                h0[:, FT * fc:FT * (fc + 1)], t4b[:, :],
                                gmc[:, fc:fc + 1], btc[:, fc:fc + 1],
                                ALU.mult, ALU.add))
                        inc("vector", lambda v: v.engine_nop(), "fh0")
                    with nc.psum_tensor("f4ta_%d" % ft, [128, 128], F32) as f4ta, \
                         nc.psum_tensor("f4tb_%d" % ft, [128, 128], F32) as f4tb, \
                         nc.psum_tensor("f4tx_%d" % ft, [8, 128], F32) as f4tx, \
                         nc.psum_tensor("f4b1_%d" % ft, [128, FT], F32) as f4b1, \
                         nc.psum_tensor("f4b2_%d" % ft, [128, FT], F32) as f4b2:
                        f4t = (f4ta, f4tb)
                        inc("vector", lambda v: v.tensor_reduce(
                            M4[:, :],
                            h0[:, :].rearrange("p (c t) -> p t c", c=8),
                            AX.X, ALU.max, apply_absolute_value=True),
                            "fm1")
                        wait("tensor", "fm1")
                        for r in range(4):
                            rg = ft * 4 + r
                            if r >= 2:
                                wait("tensor", "fm2b", rg - 1)
                            inc("tensor", lambda t, r=r, f4t=f4t: t.transpose(
                                f4t[r % 2][:, :],
                                M4[:, 128 * r:128 * (r + 1)],
                                cons["ident_f"][:, :]), "fm2")
                            wait("vector", "fm2", ft * 4 + r + 1)
                            inc("vector", lambda v, r=r, f4t=f4t:
                                v.tensor_reduce(
                                    X4[:, r:r + 1], f4t[r % 2][:, :],
                                    AX.X, ALU.max), "fm2b")
                        wait("vector", "fm2b", P.v("fm2b"))
                        inc("vector", lambda v: v.tensor_scalar_max(
                            XC[:, 0:4], X4[:, 0:4], EPS), "fmc")
                        wait("scalar", "fmc")
                        op("scalar", lambda s: s.activation(
                            XC[:, 4:8], XC[:, 0:4], AF.Ln))
                        inc("scalar", lambda s: s.activation(
                            XP[:, 0:4], XC[:, 4:8], AF.Exp, scale=-1.0,
                            bias=cons["l127c"][:, :]), "fmd")
                        wait("vector", "fmd")
                        inc("vector", lambda v: v.tensor_scalar_mul(
                            XP[:, 4:8], XC[:, 0:4], 1.0 / 127.0), "fm3")
                        if _DBG and ft == 0:
                            wait("gpsimd", "fm3", P.v("fm3"))
                            dma("gpsimd", xp_dbg[:, 0:8], X4[:, :],
                                semn="sp")
                            dma("gpsimd", xp_dbg[:, 8:16], XP[:, :],
                                semn="sp")
                            wait("vector", "sp", P.v("sp"))
                        wait("tensor", "fm3")
                        inc("tensor", lambda t, f4tx=f4tx: t.transpose(
                            f4tx[:, :], XP[:, :], cons["ident_f"][:, :]),
                            "fm4")
                        wait("vector", "fm4")
                        inc("vector", lambda v, f4tx=f4tx: v.tensor_copy(
                            rws4[0:8, :], f4tx[:, :]), "fm5")
                        if _DBG and ft == 0:
                            wait("gpsimd", "fm5", P.v("fm5"))
                            dma("gpsimd",
                                xp_dbg[:, 16:48].rearrange(
                                    "p (a b) -> (p a) b", a=1)
                                if False else xp_dbg[0:32, 16:48],
                                rws4[:, 0:32], semn="sp")
                            wait("vector", "sp", P.v("sp"))
                        for r in range(4):
                            wait("vector", "fm5")
                            inc("vector", lambda v, r=r: v.stream_shuffle(
                                sh4[:, :], rws4[:, :],
                                [r] + list(range(1, 32))), "fm6")
                            wait("tensor", "fm6")
                            inc("tensor", lambda t, r=r, f4b1=f4b1: t.matmul(
                                f4b1[:, 128 * r:128 * (r + 1)],
                                cons["ones_row_f"][:, :], sh4[0:1, :],
                                start=True, stop=True), "fm7")
                            wait("vector", "fm7")
                            inc("vector", lambda v, r=r: v.stream_shuffle(
                                sh4[:, :], rws4[:, :],
                                [4 + r] + list(range(1, 32))), "fm5")
                            wait("tensor", "fm5")
                            inc("tensor", lambda t, r=r, f4b2=f4b2: t.matmul(
                                f4b2[:, 128 * r:128 * (r + 1)],
                                cons["ones_row_f"][:, :], sh4[0:1, :],
                                start=True, stop=True), "fm7")
                            wait("vector", "fm7")
                            inc("vector", lambda v: v.engine_nop(), "fm5")
                        for fc in range(8):
                            op("vector", lambda v, fc=fc, f4b1=f4b1:
                               v.tensor_mul(
                                t4b[:, :], h0[:, FT * fc:FT * (fc + 1)],
                                f4b1[:, :]))
                            op("vector", lambda v: v.tensor_copy(
                                ti4b[:, :], t4b[:, :]))
                            op("vector", lambda v: v.tensor_copy(
                                tf4b[:, :], ti4b[:, :]))
                            op("vector", lambda v: v.tensor_scalar(
                                t4b[:, :], tf4b[:, :], -128.0, 127.0,
                                ALU.max, ALU.min))
                            op("vector", lambda v, fc=fc, f4b2=f4b2:
                               v.tensor_mul(
                                h0q[:, FT * fc:FT * (fc + 1)], t4b[:, :],
                                f4b2[:, :]))
                        inc("vector", lambda v: v.engine_nop(), "fq1")
                        if _DBG:
                            wait("gpsimd", "fq1", P.v("fq1"))
                            dma("gpsimd",
                                h0q_dbg[:, 4096 * ft:4096 * (ft + 1)],
                                h0q[:, :], semn="sp")
                            wait("vector", "sp", P.v("sp"))
                        with nc.psum_tensor("f4m0_%d" % ft, [128, FT], F32) as f4m0, \
                             nc.psum_tensor("f4m1_%d" % ft, [128, FT], F32) as f4m1:
                            psm = (f4m0, f4m1)
                            wait("tensor", "fq1")
                            for f2 in range(32):
                                kf = ft * 32 + f2
                                if f2 >= 2:
                                    wait("tensor", "fge", kf - 1)
                                for ec in range(8):
                                    op("tensor", lambda t, ec=ec, f2=f2,
                                       psm=psm: t.matmul(
                                           psm[f2 % 2][:, :],
                                           w1q[:, 4096 * ec + 128 * f2:
                                               4096 * ec + 128 * (f2 + 1)],
                                           h0q[:, FT * ec:FT * (ec + 1)],
                                           start=(ec == 0), stop=(ec == 7)))
                                P.inc_prev("tensor", "fmm")
                                wait("scalar", "fmm")
                                inc("scalar", lambda s, f2=f2,
                                    psm=psm: s.activation(
                                    h1[:, FT * f2:FT * (f2 + 1)],
                                    psm[f2 % 2][:, :], AF.Gelu,
                                    bias=b1c[:, f2:f2 + 1],
                                    scale=m1rc[:, 0:1]), "fge")
                        wait("vector", "fge", P.v("fge"))
                        inc("vector", lambda v: v.tensor_reduce(
                            M42[:, :],
                            h1[:, :].rearrange("p (c t) -> p t c", c=32),
                            AX.X, ALU.max, apply_absolute_value=True),
                            "fn1")
                        with nc.psum_tensor("f4t2a_%d" % ft, [128, 128],
                                            BF16) as f4t2a, \
                             nc.psum_tensor("f4t2b_%d" % ft, [128, 128],
                                            BF16) as f4t2b:
                            f4t2 = (f4t2a, f4t2b)
                            wait("tensor", "fn1")
                            for r in range(4):
                                rg = ft * 4 + r
                                if r >= 2:
                                    wait("tensor", "fn2b", rg - 1)
                                inc("tensor", lambda t, r=r, f4t2=f4t2:
                                    t.transpose(
                                        f4t2[r % 2][:, :],
                                        M42[:, 128 * r:128 * (r + 1)],
                                        cons["ident_b"][:, :]), "fn2")
                                wait("vector", "fn2", ft * 4 + r + 1)
                                inc("vector", lambda v, r=r, f4t2=f4t2:
                                    v.tensor_reduce(
                                        X4[:, r:r + 1],
                                        f4t2[r % 2][:, :],
                                        AX.X, ALU.max), "fn2b")
                        wait("vector", "fn2b", P.v("fn2b"))
                        inc("vector", lambda v: v.tensor_scalar_max(
                            XC[:, 0:4], X4[:, 0:4], EPS), "fnc")
                        wait("scalar", "fnc")
                        op("scalar", lambda s: s.activation(
                            XC[:, 4:8], XC[:, 0:4], AF.Ln))
                        inc("scalar", lambda s: s.activation(
                            XP[:, 0:4], XC[:, 4:8], AF.Exp, scale=-1.0,
                            bias=cons["l127c"][:, :]), "fnd")
                        wait("vector", "fnd")
                        inc("vector", lambda v: v.tensor_scalar(
                            XP[:, 4:8], XC[:, 0:4], m2rc[:, 0:1],
                            1.0 / 127.0, ALU.mult, ALU.mult), "fn3")
                        wait("tensor", "fn3")
                        inc("tensor", lambda t, f4tx=f4tx: t.transpose(
                            f4tx[:, :], XP[:, :], cons["ident_f"][:, :]),
                            "fn4")
                        wait("vector", "fn4")
                        inc("vector", lambda v, f4tx=f4tx: v.tensor_copy(
                            rws4[0:8, :], f4tx[:, :]), "fn5")
                        for r in range(4):
                            wait("vector", "fn5")
                            inc("vector", lambda v, r=r: v.stream_shuffle(
                                sh4[:, :], rws4[:, :],
                                [r] + list(range(1, 32))), "fn6")
                            wait("tensor", "fn6")
                            inc("tensor", lambda t, r=r, f4b1=f4b1: t.matmul(
                                f4b1[:, 128 * r:128 * (r + 1)],
                                cons["ones_row_f"][:, :], sh4[0:1, :],
                                start=True, stop=True), "fn7")
                            wait("vector", "fn7")
                            inc("vector", lambda v, r=r: v.stream_shuffle(
                                sh4[:, :], rws4[:, :],
                                [4 + r] + list(range(1, 32))), "fn5")
                            wait("tensor", "fn5")
                            inc("tensor", lambda t, r=r, f4b2=f4b2: t.matmul(
                                f4b2[:, 128 * r:128 * (r + 1)],
                                cons["ones_row_f"][:, :], sh4[0:1, :],
                                start=True, stop=True), "fn7")
                            wait("vector", "fn7")
                            inc("vector", lambda v: v.engine_nop(), "fn5")
                        inc("vector", lambda v, ft=ft, f4b2=f4b2:
                            v.tensor_copy(
                            ry2sb[:, FT * ft:FT * (ft + 1)], f4b2[:, :]),
                            "fn8")
                        for f2 in range(32):
                            op("vector", lambda v, f2=f2, f4b1=f4b1:
                               v.tensor_mul(
                                t4b[:, :], h1[:, FT * f2:FT * (f2 + 1)],
                                f4b1[:, :]))
                            op("vector", lambda v: v.tensor_copy(
                                ti4b[:, :], t4b[:, :]))
                            op("vector", lambda v: v.tensor_copy(
                                tf4b[:, :], ti4b[:, :]))
                            op("vector", lambda v, f2=f2: v.tensor_scalar(
                                h1[:, FT * f2:FT * (f2 + 1)], tf4b[:, :],
                                -128.0, 127.0, ALU.max, ALU.min))
                        inc("vector", lambda v: v.engine_nop(), "fq2")
                        wait("gpsimd", "fq2")
                        dma("gpsimd",
                            h1q_sp.rearrange("(c p) T -> p c T", p=128)
                            [:, :, FT * ft:FT * (ft + 1)],
                            h1[:, :].rearrange("p (c t) -> p c t", c=32),
                            semn="sp")
                    barrier()
            b1outer.close()
            barrier()
            P.phase_end()
            with ES() as b2s:
                s42 = lambda n, shp, dt=F32: b2s.enter_context(
                    nc.sbuf_tensor(n, shp, dt))
                w2q = s42("w2q", [128, 32 * 1024], BF16)
                wr_scope = ES()
                s42b = lambda n, shp, dt=F32: wr_scope.enter_context(
                    nc.sbuf_tensor(n, shp, dt))
                wst5 = s42b("wst5", [128, 1024])
                t5a = s42b("t5a", [128, 1024])
                ti5 = s42b("ti5", [128, 1024], I32)
                tf5 = s42b("tf5", [128, 1024])
                for ch in range(32):
                    tgt = dma("sync", wst5[:, :],
                              w2T[128 * ch:128 * (ch + 1), :])
                    wait("sync", "wra", P.v("wra") + 1)
                    wait("vector", "ld", tgt)
                    op("vector", lambda v: v.tensor_scalar_mul(
                        t5a[:, :], wst5[:, :], s2c[:, :]))
                    inc("vector", lambda v: v.tensor_copy(
                        ti5[:, :], t5a[:, :]), "wra")
                    op("vector", lambda v: v.tensor_copy(
                        tf5[:, :], ti5[:, :]))
                    inc("vector", lambda v, ch=ch: v.tensor_scalar(
                        w2q[:, 1024 * ch:1024 * (ch + 1)], tf5[:, :],
                        -1.0, 1.0, ALU.max, ALU.min), "wrd")
                barrier()
                P.phase_end()
                wr_scope.close()
                h1qb = s42("h1qb", [128, 32 * FT], BF16)
                part4 = s42("part4", [128, 8 * FT])
                y4 = s42("y4", [128, 8 * FT])
                ty4 = s42("ty4", [128, FT])
                ty5 = s42("ty5", [128, FT])
                with nc.psum_tensor("f5m0", [128, FT], F32) as f5m0, \
                     nc.psum_tensor("f5m1", [128, FT], F32) as f5m1:
                    psn = (f5m0, f5m1)
                    for ft in range(NFT):
                        lh = dma("gpsimd", h1qb[:, :]
                                 .rearrange("p (c t) -> p c t", c=32),
                                 h1q_sp.rearrange("(c p) T -> p c T", p=128)
                                 [:, :, FT * ft:FT * (ft + 1)], semn="sp")
                        lp = dma("gpsimd", part4[:, :]
                                 .rearrange("p (c t) -> p c t", c=8),
                                 part_sp.rearrange("(c p) T -> p c T", p=128)
                                 [:, :, FT * ft:FT * (ft + 1)], semn="sp")
                        wait("tensor", "sp", lh)
                        for fc in range(8):
                            kf = ft * 8 + fc
                            if fc >= 2:
                                wait("tensor", "yev", kf - 1)
                            for hc in range(32):
                                op("tensor", lambda t, hc=hc, fc=fc:
                                   t.matmul(
                                       psn[fc % 2][:, :],
                                       w2q[:, 1024 * hc + 128 * fc:
                                           1024 * hc + 128 * (fc + 1)],
                                       h1qb[:, FT * hc:FT * (hc + 1)],
                                       start=(hc == 0), stop=(hc == 31)))
                            P.inc_prev("tensor", "ymm")
                            wait("vector", "ymm")
                            op("vector", lambda v, fc=fc, ft=ft: v.tensor_mul(
                                ty4[:, :], psn[fc % 2][:, :],
                                ry2sb[:, FT * ft:FT * (ft + 1)]))
                            inc("vector", lambda v: v.engine_nop(), "ya1")
                            wait("scalar", "ya1")
                            inc("scalar", lambda s, fc=fc: s.activation(
                                ty5[:, :], ty4[:, :], AF.Identity
                                if hasattr(AF, "Identity") else AF.Copy,
                                bias=b2c[:, fc:fc + 1]), "ya2")
                            wait("vector", "ya2")
                            wait("vector", "sp", lp)
                            inc("vector", lambda v, fc=fc: v.tensor_add(
                                y4[:, FT * fc:FT * (fc + 1)], ty5[:, :],
                                part4[:, FT * fc:FT * (fc + 1)]), "yev")
                        wait("gpsimd", "yev", ft * 8 + 8)
                        dma("gpsimd",
                            y_d.rearrange("(c p) T -> p c T", p=128)
                            [:, :, FT * ft:FT * (ft + 1)],
                            y4[:, :].rearrange("p (c t) -> p c t", c=8),
                            semn="st")
                        barrier()
        barrier()
        if _DBG:
            dbg_specs = [("d_ws", ws_dbg, [1, 8], F32),
                         ("d_rw", rw_dbg, [1, 4 * FT], F32),
                         ("d_w1q", w1q_dbg, [128, 8 * 4096], BF16),
                         ("d_mc", mc_dbg, [128, 4], F32),
                         ("d_xp", xp_dbg, [128, 48], F32),
                         ("d_h0q", h0q_dbg, [128, NFT * 4096], BF16),
                         ("d_kfm", kfm_sp, [128, NPT * 4096], BF16),
                         ("d_vtm", vtm_sp, [128, 16 * 1024], BF16),
                         ("d_gp", gp_sp, [64, 2 * 8192], F32),
                         ("d_h1q", h1q_sp, [HID, TLOC], BF16),
                         ("d_part", part_sp, [E, TLOC], F32)]
            for nm, srct, shp, dt in dbg_specs:
                od = nc.dram_tensor(nm, shp, dt, kind="ExternalOutput").ap()
                dma("gpsimd", od, srct, semn="st")
            wait("gpsimd", "st", P.v("st"))
            barrier()
        wait("gpsimd", "st", P.v("st"))
        nc._prog = P
        P.emit()
    return nc


_NC_CACHE = {}


def _get_nc():
    if "nc" not in _NC_CACHE:
        _NC_CACHE["nc"] = build_nc()
    return _NC_CACHE["nc"]


def kernel(**inp):
    x = np.ascontiguousarray(np.asarray(inp["L0_attn_x"], dtype=np.float32))
    Wq = np.asarray(inp["L0_attn_Wq"], np.float32)
    bq = np.asarray(inp["L0_attn_bq"], np.float32)
    Wk = np.asarray(inp["L0_attn_Wk"], np.float32)
    bk = np.asarray(inp["L0_attn_bk"], np.float32)
    Wv = np.asarray(inp["L0_attn_Wv"], np.float32)
    bv = np.asarray(inp["L0_attn_bv"], np.float32)
    Wg = np.asarray(inp["L0_attn_Wg"], np.float32)
    bg = np.asarray(inp["L0_attn_bg"], np.float32)
    gamma = np.asarray(inp["L0_attn_gamma"], np.float32)
    w1 = np.asarray(inp["L0_ffn_w1"], np.float32)
    b1 = np.asarray(inp["L0_ffn_b1"], np.float32)
    w2 = np.asarray(inp["L0_ffn_w2"], np.float32)
    b2 = np.asarray(inp["L0_ffn_b2"], np.float32)
    fgm = np.asarray(inp["L0_ffn_gamma"], np.float32)
    fbt = np.asarray(inp["L0_ffn_beta"], np.float32)

    WqT = np.ascontiguousarray(Wq.T)
    WkT = np.ascontiguousarray(Wk.T)
    WvT = np.ascontiguousarray(Wv.T)
    WgT = np.ascontiguousarray(Wg.T)
    w1T = np.ascontiguousarray(w1.T)
    w2T = np.ascontiguousarray(w2.T)
    brows = np.ascontiguousarray(np.stack([bq, bk, bv], axis=0))
    bgc = np.ascontiguousarray(bg.reshape(8, 128).T)
    b1c = np.ascontiguousarray(b1.reshape(32, 128).T)
    b2c = np.ascontiguousarray(b2.reshape(8, 128).T)
    gmc = np.ascontiguousarray(fgm.reshape(8, 128).T)
    btc = np.ascontiguousarray(fbt.reshape(8, 128).T)
    gam = gamma.reshape(1, H)

    inv = (1.0 / 10000.0 ** (np.arange(0, 64, 2, dtype=np.float64) / 64.0))
    in_maps = []
    for c in range(8):
        b, p = divmod(c, 4)
        sl = slice(2048 * p, 2048 * (p + 1))
        xTc = np.ascontiguousarray(x[b, sl, :].T)
        pos = np.arange(2048 * p, 2048 * (p + 1), dtype=np.float64)
        fr = pos[:, None] * inv[None, :]
        emb = np.concatenate([fr, fr], axis=1)
        cosn = np.cos(emb).astype(np.float32)
        sinn = np.sin(emb).astype(np.float32)
        kcos = np.ascontiguousarray(np.tile(cosn, (1, H))).astype(BF)
        ksin = np.ascontiguousarray(np.tile(sinn, (1, H))).astype(BF)
        qcos = np.ascontiguousarray(np.tile(cosn * 0.125, (1, H))).astype(BF)
        qsin = np.ascontiguousarray(np.tile(sinn * 0.125, (1, H))).astype(BF)
        nd4 = np.array([[-2048.0 * (p - j - 1) if j < p else -1e9
                         for j in range(4)]], np.float32)
        zm4 = np.array([[1.0 if j > p else 0.0 for j in range(4)]],
                       np.float32)
        in_maps.append({
            "xT": xTc, "WqT": WqT, "WkT": WkT, "WvT": WvT, "WgT": WgT,
            "w1T": w1T, "w2T": w2T, "brows": brows, "bgc": bgc,
            "b1c": b1c, "b2c": b2c, "gmc": gmc, "btc": btc, "gam": gam,
            "nd4": nd4, "zm4": zm4, "kcos": kcos, "ksin": ksin,
            "qcos": qcos, "qsin": qsin})
    nc = _get_nc()
    res = run_bass_kernel_spmd(nc, in_maps, core_ids=list(range(8)))
    out = np.empty((2, 8192, 1024), np.float32)
    for c in range(8):
        b, p = divmod(c, 4)
        out[b, 2048 * p:2048 * (p + 1), :] = res.results[c]["y"].T
    if _DBG:
        DBG_RES["res"] = res.results
    return out


# version 09
# speedup vs baseline: 1.0579x; 1.0579x over previous
import os
import numpy as np
import ml_dtypes

_P1LVL = int(os.environ.get("KP1LVL", "99"))
_P1SUB = int(os.environ.get("KP1SUB", "99"))
_DBG = int(os.environ.get("KDBG", "0"))
DBG_RES = {}

import concourse.bass as bass
import concourse.mybir as mybir
from concourse.bass_utils import run_bass_kernel_spmd

F32 = mybir.dt.float32
BF16 = mybir.dt.bfloat16
I32 = mybir.dt.int32
AF = mybir.ActivationFunctionType
ALU = mybir.AluOpType
AX = mybir.AxisListType
BF = ml_dtypes.bfloat16

E, H, HID = 1024, 16, 4096
TLOC = 2048
NPT, PT = 4, 512
NBLK, CB = 8, 256
NFT, FT = 4, 512
EPS = 1e-5
ENGINES = ("sync", "tensor", "scalar", "vector", "gpsimd")


def _consts():
    c = {}
    c["ones_row_f"] = np.ones((1, 128), np.float32)
    c["ones_row_b"] = np.ones((1, 128), BF)
    c["ones_col_f"] = np.ones((128, 1), np.float32)
    c["ones_col_b"] = np.ones((128, 1), BF)
    c["ident_f"] = np.eye(128, dtype=np.float32)
    c["ident_b"] = np.eye(128, dtype=BF)
    s = np.arange(128)[:, None]
    t = np.arange(CB)[None, :]
    c["dk"] = np.concatenate(
        [np.maximum(t - 128 * kc - s, 0) for kc in (0, 1)],
        axis=1).astype(np.float32)
    c["qdec"] = (np.arange(CB, dtype=np.float32)[None, :] + 1.0)
    j = np.arange(128, dtype=np.float32)[:, None]
    c["dcol"] = np.concatenate([255.0 - 128.0 * kc - j for kc in (0, 1)],
                               axis=1).astype(np.float32)
    c["epsc"] = np.full((128, 1), 1e-5, np.float32)
    c["l127c"] = np.full((128, 1), np.log(127.0), np.float32)
    return c


class Prog:

    PERSIST = ("ld", "sp", "bar", "st")

    def __init__(self, nc, stack):
        self.nc = nc
        self.stack = stack
        self.ops = []
        self.cnt = {}
        self.base = {}
        self.phys_of = {}
        self.phys = []
        self.phys_cnt = []
        self.free_phys = []
        self.barn = 0

    def sem(self, n):
        if n not in self.phys_of:
            if self.free_phys:
                idx = self.free_phys.pop()
            else:
                idx = len(self.phys)
                self.phys.append(
                    self.stack.enter_context(self.nc.semaphore(n)))
                self.phys_cnt.append(0)
            self.phys_of[n] = idx
            self.base[n] = self.phys_cnt[idx]
            self.cnt[n] = 0
        return self.phys[self.phys_of[n]]

    def phase_end(self):
        for n in list(self.phys_of):
            if n not in self.PERSIST:
                self.free_phys.append(self.phys_of[n])
                del self.phys_of[n], self.base[n], self.cnt[n]

    def v(self, n):
        self.sem(n)
        return self.cnt[n]

    def inc_prev(self, eng, semn, k=1):
        h = self.sem(semn)
        self.cnt[semn] += k
        self.phys_cnt[self.phys_of[semn]] += k
        for i in range(len(self.ops) - 1, -1, -1):
            e0, f0, m0 = self.ops[i]
            if e0 == eng:
                assert m0 is None or m0[0] == "inc", \
                    f"inc_prev would stack on {m0}"
                ev = ("inc", semn, k)
                meta = ev if m0 is None else (("multi",) +
                                              (m0[1:] if m0[0] == "multi"
                                               else (m0,)) + (ev,))
                self.ops[i] = (e0, lambda e, f0=f0, h=h, k=k:
                               f0(e).then_inc(h, k), meta)
                return self.cnt[semn]
        raise RuntimeError("no prior op for " + eng)

    def op(self, eng, fn, meta=None):
        self.ops.append((eng, fn, meta))

    def wait(self, eng, semn, val=None):
        h = self.sem(semn)
        v = self.cnt[semn] if val is None else val
        pv = self.base[semn] + v
        self.op(eng, lambda e, h=h, v=pv: e.wait_ge(h, v),
                ("wait", semn, v))

    def inc(self, eng, fn, semn, k=1):
        h = self.sem(semn)
        self.cnt[semn] += k
        self.phys_cnt[self.phys_of[semn]] += k
        self.op(eng, lambda e, fn=fn, h=h, k=k: fn(e).then_inc(h, k),
                ("inc", semn, k))
        return self.cnt[semn]

    def dma(self, eng, out, in_, semn="ld"):
        h = self.sem(semn)
        self.cnt[semn] += 16
        self.phys_cnt[self.phys_of[semn]] += 16
        self.op(eng, lambda e, out=out, in_=in_, h=h:
                e.dma_start(out=out, in_=in_).then_inc(h, 16),
                ("inc", semn, 16))
        return self.cnt[semn]

    def barrier(self):
        self.sem("bar")
        self.barn += 1
        tgt = 5 * self.barn
        ldv = self.cnt.get("ld", 0)
        spv = self.cnt.get("sp", 0)
        for e in ENGINES:
            if e == "sync" and "ld" in self.phys_of:
                self.wait(e, "ld", ldv)
            if e == "gpsimd" and "sp" in self.phys_of:
                self.wait(e, "sp", spv)
            h = self.sem("bar")
            self.op(e, lambda en, h=h: en.sem_inc(h, 1), ("inc", "bar", 1))
            self.op(e, lambda en, h=h, t=tgt: en.wait_ge(h, t),
                    ("wait", "bar", tgt))
        self.cnt["bar"] += 5
        self.phys_cnt[self.phys_of["bar"]] += 5

    def emit(self):
        with self.nc.Block() as block:
            for name in ENGINES:
                def mk(name):
                    def f(eng):
                        for e, fn, _m in self.ops:
                            if e == name:
                                fn(eng)
                    return f
                getattr(block, name)(mk(name))


def build_nc(upto=99):
    from contextlib import ExitStack
    nc = bass.Bass("TRN2", target_bir_lowering=False,
                   detect_race_conditions=False)
    dti = lambda n, s: nc.dram_tensor(n, s, F32, kind="ExternalInput").ap()
    xT = dti("xT", [E, TLOC])
    WqT, WkT = dti("WqT", [E, E]), dti("WkT", [E, E])
    WvT, WgT = dti("WvT", [E, E]), dti("WgT", [E, E])
    w1T, w2T = dti("w1T", [E, HID]), dti("w2T", [HID, E])
    brows = dti("brows", [3, E])
    bgc_d = dti("bgc", [128, 8])
    b1c_d, b2c_d = dti("b1c", [128, 32]), dti("b2c", [128, 8])
    gmc_d, btc_d = dti("gmc", [128, 8]), dti("btc", [128, 8])
    gam_d = dti("gam", [1, H])
    nd4_d, zm4_d = dti("nd4", [1, 4]), dti("zm4", [1, 4])
    dtb = lambda n, s: nc.dram_tensor(n, s, BF16, kind="ExternalInput").ap()
    kcos_d, ksin_d = dtb("kcos", [TLOC, E]), dtb("ksin", [TLOC, E])
    qcos_d, qsin_d = dtb("qcos", [TLOC, E]), dtb("qsin", [TLOC, E])
    y_d = nc.dram_tensor("y", [E, TLOC], F32, kind="ExternalOutput").ap()

    kfm_sp = nc.dram_tensor("kfm_sp", [128, NPT * 4096], BF16).ap()
    vtm_sp = nc.dram_tensor("vtm_sp", [128, 16 * 1024], BF16).ap()
    gp_sp = nc.dram_tensor("gp_sp", [64, 2 * 8192], F32).ap()
    h1q_sp = nc.dram_tensor("h1q_sp", [HID, TLOC], BF16).ap()
    part_sp = nc.dram_tensor("part_sp", [E, TLOC], F32).ap()
    ag_in = nc.dram_tensor("ag_in", [2048, 64], F32).ap()
    ag_out = nc.dram_tensor("ag_out", [8192, 64], F32).ap()
    g1024_sp = nc.dram_tensor("g1024_sp", [1, 1024], F32).ap()
    h0q_dbg = nc.dram_tensor("h0q_dbg", [128, NFT * 4096], BF16).ap()
    rw_dbg = nc.dram_tensor("rw_dbg", [1, 4 * FT], F32).ap()
    w1q_dbg = nc.dram_tensor("w1q_dbg", [128, 8 * 4096], BF16).ap()
    mc_dbg = nc.dram_tensor("mc_dbg", [128, 4], F32).ap()
    ws_dbg = nc.dram_tensor("ws_dbg", [1, 8], F32).ap()
    xp_dbg = nc.dram_tensor("xp_dbg", [128, 48], F32).ap()

    CN = _consts()
    inl = {k: nc.inline_tensor(v, k) for k, v in CN.items()}

    with ExitStack() as stk:
        P = Prog(nc, stk)
        sb = lambda n, shp, dt=F32: stk.enter_context(
            nc.sbuf_tensor(n, shp, dt))
        op, wait, inc, dma, barrier = P.op, P.wait, P.inc, P.dma, P.barrier

        xnb = sb("xnb", [128, 8 * TLOC], BF16)
        WTt = sb("WTt", [128, H * 512], BF16)
        Bq = sb("Bq", [128, (H // 2) * CB], BF16)
        cons = {}
        for nm, shp, dt in [("ones_row_f", [1, 128], F32),
                            ("ones_row_b", [1, 128], BF16),
                            ("ones_col_f", [128, 1], F32),
                            ("ones_col_b", [128, 1], BF16),
                            ("ident_f", [128, 128], F32),
                            ("ident_b", [128, 128], BF16),
                            ("dcol", [128, 2], F32),
                            ("epsc", [128, 1], F32),
                            ("l127c", [128, 1], F32)]:
            cons[nm] = sb("c_" + nm, shp, dt)
        bqr = sb("bqr", [1, E], BF16)
        bkr = sb("bkr", [1, E], BF16)
        bvr = sb("bvr", [1, E], BF16)
        bgc = sb("bgc_s", [128, 8])
        b1c, b2c = sb("b1c_s", [128, 32]), sb("b2c_s", [128, 8])
        gmc, btc = sb("gmc_s", [128, 8]), sb("btc_s", [128, 8])
        gam, gsig = sb("gam_s", [1, H]), sb("gsig", [1, H])
        nd4, zm4 = sb("nd4_s", [1, 4]), sb("zm4_s", [1, 4])
        gncol = sb("gncol", [128, H])
        zcol = sb("zcol", [128, 4])
        g1024 = sb("g1024", [1, 1024])
        grows = sb("grows", [16, 64])
        dcole = sb("dcole", [128, 2 * H])
        m1rc, m2rc = sb("m1rc", [128, 1]), sb("m2rc", [128, 1])
        tcol = sb("tcol", [128, 1])
        dbgc = sb("dbgc", [128, 4]) if _DBG else None
        s1c, s2c = sb("s1c", [128, 1]), sb("s2c", [128, 1])

        from contextlib import ExitStack as ES0
        p0scope = ES0()
        for nm, shp, dt in [("dk", [128, 512], F32), ("qdec", [1, CB], F32)]:
            cons[nm] = p0scope.enter_context(
                nc.sbuf_tensor("c_" + nm, shp, dt))
        btmpf = p0scope.enter_context(nc.sbuf_tensor("btmpf", [1, E], F32))
        for nm in cons:
            dma("sync", cons[nm][:, :], inl[nm].ap())
        for i, row in enumerate([bqr, bkr, bvr]):
            tgt = dma("sync", btmpf[:, :], brows[i:i + 1, :])
            wait("vector", "ld", tgt)
            inc("vector", lambda v, row=row: v.tensor_copy(
                row[:, :], btmpf[:, :]), "p0v")
            wait("sync", "p0v")
        dma("sync", bgc[:, :], bgc_d)
        dma("sync", b1c[:, :], b1c_d)
        dma("sync", b2c[:, :], b2c_d)
        dma("sync", gmc[:, :], gmc_d)
        dma("sync", btc[:, :], btc_d)
        dma("sync", gam[:, :], gam_d)
        dma("sync", nd4[:, :], nd4_d)
        dma("sync", zm4[:, :], zm4_d)
        p0ld = P.v("ld")
        wait("scalar", "ld", p0ld)
        inc("scalar", lambda s: s.activation(gsig[:, :], gam[:, :],
                                             AF.Sigmoid), "p0s")
        with nc.psum_tensor("p0a", [128, 64], F32) as p0a, \
             nc.psum_tensor("p0b", [16, 64], F32) as p0b:
            wait("tensor", "ld", p0ld)
            wait("tensor", "p0s")
            op("tensor", lambda t: t.matmul(
                p0a[:, 0:H], cons["ones_row_f"][:, :], gsig[:, :],
                start=True, stop=True))
            op("tensor", lambda t: t.matmul(
                p0a[:, 32:36], cons["ones_row_f"][:, :], zm4[:, :],
                start=True, stop=True))
            inc("tensor", lambda t: t.matmul(
                p0b[:, :], gsig[:, :], cons["ones_row_f"][0:1, 0:64],
                start=True, stop=True), "p0t")
            wait("vector", "p0t")
            op("vector", lambda v: v.tensor_scalar_mul(
                gncol[:, :], p0a[:, 0:H], -1.0))
            op("vector", lambda v: v.tensor_copy(zcol[:, :], p0a[:, 32:36]))
            inc("vector", lambda v: v.tensor_copy(grows[:, :], p0b[:, :]),
                "p0v")
        wait("gpsimd", "p0v")
        dma("gpsimd", g1024_sp.rearrange("x (h d) -> (x h) d", h=16),
            grows[:, :], semn="sp")
        dma("gpsimd", g1024[:, :], g1024_sp, semn="sp")
        barrier()
        for h in range(H):
            inc("scalar", lambda s, h=h: s.activation(
                WTt[:, h * 512:(h + 1) * 512], cons["dk"][:, :], AF.Exp,
                scale=gncol[:, h:h + 1]), "p0s")
            inc("scalar", lambda s, h=h: s.activation(
                dcole[:, 2 * h:2 * h + 2], cons["dcol"][:, :], AF.Exp,
                scale=gncol[:, h:h + 1]), "p0s")
        with nc.sbuf_tensor("brt", [1, H * CB], BF16) as brt, \
             nc.psum_tensor("pbq", [128, CB], F32) as pbq:
            for h in range(H):
                inc("scalar", lambda s, h=h: s.activation(
                    brt[:, h * CB:(h + 1) * CB], cons["qdec"][:, :], AF.Exp,
                    scale=gncol[0:1, h:h + 1]), "p0s")
            wait("tensor", "p0s")
            for pi in range(H // 2):
                if pi:
                    wait("tensor", "p0v")
                op("tensor", lambda t, pi=pi: t.matmul(
                    pbq[0:64, :], cons["ones_row_b"][0:1, 0:64],
                    brt[:, (2 * pi) * CB:(2 * pi + 1) * CB],
                    start=True, stop=True))
                inc("tensor", lambda t, pi=pi: t.matmul(
                    pbq[64:128, :], cons["ones_row_b"][0:1, 0:64],
                    brt[:, (2 * pi + 1) * CB:(2 * pi + 2) * CB],
                    start=True, stop=True), "p0t")
                wait("vector", "p0t")
                inc("vector", lambda v, pi=pi: v.tensor_copy(
                    Bq[:, pi * CB:(pi + 1) * CB], pbq[:, :]), "p0v")
            wait("tensor", "p0v")
        barrier()
        p0scope.close()
        P.phase_end()
        if upto < 1:
            nc._prog = P
            P.emit()
            return nc

        from contextlib import ExitStack as ES
        with ES() as p1:
            s1 = lambda n, shp, dt=F32: p1.enter_context(
                nc.sbuf_tensor(n, shp, dt))
            x_sb = s1("x_sb", [128, 8 * PT])
            xsq = s1("xsq", [128, 2 * PT])
            Wk_b = s1("Wk_b", [128, 8 * 1024], BF16)
            Wv_b = s1("Wv_b", [128, 8 * 1024], BF16)
            wstg = s1("wstg", [128, 2 * 1024])
            rowt = s1("rowt", [1, 3 * PT])
            mub, rsb = s1("mub", [128, PT]), s1("rsb", [128, PT])
            tA = s1("tA", [128, PT])
            ktm = s1("ktm", [128, 2 * 1024], BF16)
            ksw = s1("ksw", [128, 1024], BF16)
            kr1 = s1("kr1", [128, 1024], BF16)
            krot = s1("krot", [128, 2 * 1024], BF16)
            ktil = s1("ktil", [128, 2 * 1024], BF16)
            kstg = s1("kstg", [128, 4096], BF16)
            vstg = s1("vstg", [128, 4096], BF16)
            gpev = s1("gpev", [64, 2048])
            ctab = s1("ctab", [128, 2 * 1024], BF16)
            stab = s1("stab", [128, 2 * 1024], BF16)
            nw1 = 0
            for wd, wb in ((WkT, Wk_b), (WvT, Wv_b)):
                for ec in range(8):
                    if nw1 >= 2:
                        wait("sync", "p1w", nw1 - 1)
                    tgt = dma("sync", wstg[:, (nw1 % 2) * 1024:
                                           (nw1 % 2 + 1) * 1024],
                              wd[128 * ec:128 * (ec + 1), :])
                    wait("scalar", "ld", tgt)
                    inc("scalar", lambda s, wb=wb, ec=ec, nw1=nw1: s.copy(
                        wb[:, 1024 * ec:1024 * (ec + 1)],
                        wstg[:, (nw1 % 2) * 1024:(nw1 % 2 + 1) * 1024]),
                        "p1w")
                    nw1 += 1
            barrier()
            with nc.psum_tensor("ps_sum", [1, PT], F32) as ps_sum, \
                 nc.psum_tensor("ps_sq", [1, PT], F32) as ps_sq, \
                 nc.psum_tensor("ps_k0", [128, 512], F32) as ps_k0, \
                 nc.psum_tensor("ps_k1", [128, 512], F32) as ps_k1, \
                 nc.psum_tensor("ps_t0", [128, 128], BF16) as ps_t0, \
                 nc.psum_tensor("ps_t1", [128, 128], BF16) as ps_t1, \
                 nc.psum_tensor("ps_gp", [64, 1024], F32) as ps_gp:
                ps_t = (ps_t0, ps_t1)
                ps_bc = ps_k0
                gp_dma_tgt = [0]
                for pt in range(NPT):
                    if _P1LVL < 1:
                        barrier()
                        continue
                    xt = dma("sync",
                             x_sb[:, :].rearrange("p (c t) -> p c t", c=8),
                             xT[:, PT * pt:PT * (pt + 1)]
                             .rearrange("(c p) t -> p c t", p=128))
                    wait("scalar", "ld", xt)
                    for fc in range(8):
                        k = pt * 8 + fc
                        if k >= 2 and _P1SUB >= 2:
                            wait("scalar", "sqm", k - 1)
                        inc("scalar", lambda s, fc=fc: s.activation(
                            xsq[:, (fc % 2) * PT:(fc % 2 + 1) * PT],
                            x_sb[:, PT * fc:PT * (fc + 1)], AF.Square), "sqa")
                    if _P1SUB < 2:
                        barrier()
                        continue
                    wait("tensor", "ld", xt)
                    for fc in range(8):
                        op("tensor", lambda t, fc=fc: t.matmul(
                            ps_sum[:, :], cons["ones_col_f"][:, :],
                            x_sb[:, PT * fc:PT * (fc + 1)],
                            start=(fc == 0), stop=(fc == 7)))
                    for fc in range(8):
                        wait("tensor", "sqa", pt * 8 + fc + 1)
                        inc("tensor", lambda t, fc=fc: t.matmul(
                            ps_sq[:, :], cons["ones_col_f"][:, :],
                            xsq[:, (fc % 2) * PT:(fc % 2 + 1) * PT],
                            start=(fc == 0), stop=(fc == 7)), "sqm")
                    if _P1SUB < 3:
                        barrier()
                        continue
                    r0 = rowt[0:1, 0:PT]
                    r1 = rowt[0:1, PT:2 * PT]
                    r2 = rowt[0:1, 2 * PT:3 * PT]
                    wait("vector", "sqm", 8 * (pt + 1))
                    op("vector", lambda v: v.tensor_scalar_mul(
                        r0, ps_sum[:, :], 1.0 / 1024.0))
                    if _P1SUB == 25:
                        inc("vector", lambda v: v.engine_nop(), "rw1")
                        barrier()
                        continue
                    op("vector", lambda v: v.tensor_scalar_mul(
                        r1, ps_sq[:, :], 1.0 / 1024.0))
                    if _P1SUB == 26:
                        inc("vector", lambda v: v.engine_nop(), "rw1")
                        barrier()
                        continue
                    op("vector", lambda v: v.tensor_mul(r2, r0, r0))
                    inc("vector", lambda v: v.tensor_sub(r1, r1, r2), "rw1")
                    if _P1SUB == 27:
                        barrier()
                        continue
                    wait("scalar", "rw1")
                    op("scalar", lambda s: s.activation(
                        r2, r1, AF.Ln, bias=cons["epsc"][0:1, :]))
                    inc("scalar", lambda s: s.activation(r1, r2, AF.Exp,
                                                         scale=-0.5), "rw3")
                    if _P1SUB < 4:
                        barrier()
                        continue
                    wait("tensor", "rw3")
                    inc("tensor", lambda t: t.matmul(
                        ps_bc[:, :], cons["ones_row_f"][:, :], r0,
                        start=True, stop=True), "bcd")
                    wait("vector", "bcd")
                    inc("vector", lambda v: v.tensor_copy(
                        mub[:, :], ps_bc[:, :]), "bce")
                    wait("tensor", "bce")
                    inc("tensor", lambda t: t.matmul(
                        ps_bc[:, :], cons["ones_row_f"][:, :], r1,
                        start=True, stop=True), "bcd")
                    wait("vector", "bcd")
                    op("vector", lambda v: v.tensor_copy(
                        rsb[:, :], ps_bc[:, :]))
                    if _P1SUB < 5:
                        inc("vector", lambda v: v.engine_nop(), "xnd")
                        barrier()
                        continue
                    for fc in range(8):
                        op("vector", lambda v, fc=fc: v.tensor_sub(
                            tA[:, :], x_sb[:, PT * fc:PT * (fc + 1)],
                            mub[:, :]))
                        op("vector", lambda v, fc=fc, pt=pt: v.tensor_mul(
                            xnb[:, 2048 * fc + PT * pt:
                                2048 * fc + PT * (pt + 1)],
                            tA[:, :], rsb[:, :]))
                    inc("vector", lambda v: v.engine_nop(), "xnd")

                    for tc in range(4):
                        if _P1LVL < 2:
                            continue
                        tg = pt * 4 + tc
                        kc = tc % 2
                        if tg >= 2:
                            wait("sync", "krd", tg - 1)
                        t_c = dma("sync",
                                  ctab[:, kc * 1024:(kc + 1) * 1024],
                                  kcos_d[PT * pt + 128 * tc:
                                         PT * pt + 128 * (tc + 1), :])
                        t_s = dma("sync",
                                  stab[:, kc * 1024:(kc + 1) * 1024],
                                  ksin_d[PT * pt + 128 * tc:
                                         PT * pt + 128 * (tc + 1), :])
                        wait("tensor", "xnd", pt + 1)
                        if tg >= 1:
                            wait("tensor", "kve", 2 * tg)
                        for nh, pk in ((0, ps_k0), (1, ps_k1)):
                            for ec in range(8):
                                op("tensor", lambda t, ec=ec, nh=nh, pk=pk,
                                   pt=pt, tc=tc: t.matmul(
                                       pk[:, :],
                                       xnb[:, 2048 * ec + PT * pt + 128 * tc:
                                           2048 * ec + PT * pt
                                           + 128 * (tc + 1)],
                                       Wk_b[:, 1024 * ec + 512 * nh:
                                            1024 * ec + 512 * (nh + 1)],
                                       start=(ec == 0), stop=False))
                            op("tensor", lambda t, nh=nh, pk=pk: t.matmul(
                                pk[:, :], cons["ones_row_b"][:, :],
                                bkr[0:1, 512 * nh:512 * (nh + 1)],
                                start=False, stop=True))
                        P.inc_prev("tensor", "kmm")
                        wait("scalar", "kmm")
                        op("scalar", lambda s, kc=kc: s.copy(
                            ktm[:, kc * 1024:kc * 1024 + 512], ps_k0[:, :]))
                        inc("scalar", lambda s, kc=kc: s.copy(
                            ktm[:, kc * 1024 + 512:(kc + 1) * 1024],
                            ps_k1[:, :]), "kve")
                        if _P1LVL < 3:
                            continue
                        wait("gpsimd", "kve")
                        if tg >= 1:
                            wait("gpsimd", "krd", tg)
                        op("gpsimd", lambda e, kc=kc: e.tensor_scalar_mul(
                            ksw[:, 0:1024:2],
                            ktm[:, kc * 1024 + 1:(kc + 1) * 1024:2], -1.0))
                        inc("gpsimd", lambda e, kc=kc: e.tensor_copy(
                            ksw[:, 1:1024:2],
                            ktm[:, kc * 1024:(kc + 1) * 1024:2]), "ksd")
                        wait("vector", "ksd")
                        wait("vector", "ld", max(t_c, t_s))
                        if tg >= 2:
                            wait("vector", "pmm", tg // 2)
                        op("vector", lambda v, kc=kc: v.tensor_mul(
                            krot[:, kc * 1024:(kc + 1) * 1024],
                            ktm[:, kc * 1024:(kc + 1) * 1024],
                            ctab[:, kc * 1024:(kc + 1) * 1024]))
                        op("vector", lambda v, kc=kc: v.tensor_mul(
                            kr1[:, :], ksw[:, :],
                            stab[:, kc * 1024:(kc + 1) * 1024]))
                        inc("vector", lambda v, kc=kc: v.tensor_add(
                            krot[:, kc * 1024:(kc + 1) * 1024],
                            krot[:, kc * 1024:(kc + 1) * 1024],
                            kr1[:, :]), "krd")
                        if _P1LVL < 4:
                            continue
                        wait("tensor", "krd")
                        for jj in range(8):
                            jg = tg * 8 + jj
                            if jg >= 2:
                                wait("tensor", "kte", jg - 1)
                            inc("tensor", lambda t, jj=jj, kc=kc:
                                t.transpose(
                                    ps_t[jj % 2][:, :],
                                    krot[:, kc * 1024 + 128 * jj:
                                         kc * 1024 + 128 * (jj + 1)],
                                    cons["ident_b"][:, :]), "ktt")
                            wait("vector", "ktt", tg * 8 + jj + 1)
                            inc("vector", lambda v, tc=tc, jj=jj:
                                v.tensor_copy(
                                    kstg[:, 1024 * tc + 128 * jj:
                                         1024 * tc + 128 * (jj + 1)],
                                    ps_t[jj % 2][:, :]), "kte")
                        if _P1LVL < 5:
                            continue
                        wait("tensor", "kve", 2 * tg + 1)
                        for nh, pk in ((0, ps_k0), (1, ps_k1)):
                            for ec in range(8):
                                op("tensor", lambda t, ec=ec, nh=nh, pk=pk,
                                   pt=pt, tc=tc: t.matmul(
                                       pk[:, :],
                                       xnb[:, 2048 * ec + PT * pt + 128 * tc:
                                           2048 * ec + PT * pt
                                           + 128 * (tc + 1)],
                                       Wv_b[:, 1024 * ec + 512 * nh:
                                            1024 * ec + 512 * (nh + 1)],
                                       start=(ec == 0), stop=False))
                            op("tensor", lambda t, nh=nh, pk=pk: t.matmul(
                                pk[:, :], cons["ones_row_b"][:, :],
                                bvr[0:1, 512 * nh:512 * (nh + 1)],
                                start=False, stop=True))
                        P.inc_prev("tensor", "vmm")
                        wait("scalar", "vmm")
                        op("scalar", lambda s, tc=tc: s.copy(
                            vstg[:, 1024 * tc:1024 * tc + 512], ps_k0[:, :]))
                        inc("scalar", lambda s, tc=tc: s.copy(
                            vstg[:, 1024 * tc + 512:1024 * (tc + 1)],
                            ps_k1[:, :]), "kve")
                        if _P1LVL < 6:
                            continue
                        if tg >= 2:
                            wait("vector", "gsm", tg // 2)
                        for h in range(H):
                            op("vector", lambda v, h=h, kc=kc:
                               v.tensor_scalar_mul(
                                   ktil[:, kc * 1024 + 64 * h:
                                        kc * 1024 + 64 * (h + 1)],
                                   krot[:, kc * 1024 + 64 * h:
                                        kc * 1024 + 64 * (h + 1)],
                                   dcole[:, 2 * h + kc:2 * h + kc + 1]))
                        inc("vector", lambda v: v.engine_nop(), "ktl")
                        if kc == 1:
                            sbk = pt * 2 + tc // 2
                            wait("tensor", "ktl")
                            wait("tensor", "kve", 2 * tg + 2)
                            for h in range(H):
                                for k2 in (0, 1):
                                    op("tensor", lambda t, h=h, k2=k2,
                                       tc=tc: t.matmul(
                                           ps_gp[:, 64 * h:64 * (h + 1)],
                                           ktil[:, k2 * 1024 + 64 * h:
                                                k2 * 1024 + 64 * (h + 1)],
                                           vstg[:, 1024 * (tc - 1 + k2)
                                                + 64 * h:
                                                1024 * (tc - 1 + k2)
                                                + 64 * (h + 1)],
                                           start=(k2 == 0),
                                           stop=(k2 == 1)))
                            P.inc_prev("tensor", "gsm")
                            wait("vector", "gsm")
                            wait("vector", "sp", gp_dma_tgt[0])
                            inc("vector", lambda v: v.tensor_copy(
                                gpev[:, 0:1024], ps_gp[:, :]), "gp1")
                            wait("tensor", "gp1")
                            for h in range(H):
                                for k2 in (0, 1):
                                    op("tensor", lambda t, h=h, k2=k2,
                                       tc=tc: t.matmul(
                                           ps_gp[:, 64 * h:64 * (h + 1)],
                                           krot[:, k2 * 1024 + 64 * h:
                                                k2 * 1024 + 64 * (h + 1)],
                                           vstg[:, 1024 * (tc - 1 + k2)
                                                + 64 * h:
                                                1024 * (tc - 1 + k2)
                                                + 64 * (h + 1)],
                                           start=(k2 == 0),
                                           stop=(k2 == 1)))
                            P.inc_prev("tensor", "pmm")
                            wait("vector", "pmm")
                            inc("vector", lambda v: v.tensor_copy(
                                gpev[:, 1024:2048], ps_gp[:, :]), "gpe")
                            wait("gpsimd", "gpe")
                            dma("gpsimd",
                                gp_sp[:, 1024 * sbk:1024 * (sbk + 1)],
                                gpev[:, 0:1024], semn="sp")
                            dma("gpsimd",
                                gp_sp[:, 8192 + 1024 * sbk:
                                      8192 + 1024 * (sbk + 1)],
                                gpev[:, 1024:2048], semn="sp")
                            gp_dma_tgt[0] = P.v("sp")
                    if _P1LVL >= 7:
                        wait("gpsimd", "kte", 32 * (pt + 1))
                        dma("gpsimd", kfm_sp[:, 4096 * pt:4096 * (pt + 1)],
                            kstg[:, :], semn="sp")
                        wait("gpsimd", "kve", 8 * (pt + 1))
                        dma("gpsimd", vtm_sp[:, 4096 * pt:4096 * (pt + 1)],
                            vstg[:, :], semn="sp")
                    barrier()
        barrier()
        P.phase_end()
        if upto < 2:
            nc._prog = P
            P.emit()
            return nc

        p23 = stk.enter_context(ES())
        Rb16 = p23.enter_context(
            nc.sbuf_tensor("Rb16", [128, NBLK * 1024], BF16))
        Zb16 = p23.enter_context(
            nc.sbuf_tensor("Zb16", [128, NBLK * 1024], BF16))
        with ES() as p2:
            s2 = lambda n, shp, dt=F32: p2.enter_context(
                nc.sbuf_tensor(n, shp, dt))
            GtPt = s2("GtPt", [128, 8192])
            RpZp = s2("RpZp", [128, 8192])
            gC = s2("gC", [64, 1024])
            rowe = s2("rowe", [1, 1024])
            gat = s2("gat", [128, 4096])
            RacZac = s2("RacZac", [128, 1024])
            tt1 = s2("tt1", [128, 1024])
            wst2 = s2("wst2", [128, 1024])
            wabs = s2("wabs", [128, 1024])
            Gt = lambda a, b: GtPt[0:64, a:b]
            Pt = lambda a, b: GtPt[64:128, a:b]
            Rp = lambda a, b: RpZp[0:64, a:b]
            Zp = lambda a, b: RpZp[64:128, a:b]
            with nc.psum_tensor("ps_r", [64, 1024], F32) as ps_r, \
                 nc.psum_tensor("ps_w", [64, 1024], F32) as ps_w, \
                 nc.psum_tensor("ps_s2", [128, 1], F32) as ps_s2, \
                 nc.psum_tensor("ps_ws", [1, 512], F32) as ps_ws, \
                 nc.psum_tensor("ps_ws2", [1, 512], F32) as ps_ws2:
                t1 = dma("gpsimd", GtPt[0:64, :], gp_sp[:, 0:8192],
                         semn="sp")
                t2 = dma("gpsimd", GtPt[64:128, :], gp_sp[:, 8192:16384],
                         semn="sp")
                inc("scalar", lambda s: s.activation(
                    rowe[:, :], g1024[:, :], AF.Exp, scale=-256.0), "q2s")
                wait("tensor", "q2s")
                op("tensor", lambda t: t.matmul(
                    ps_r[:, 0:512], cons["ones_row_f"][0:1, 0:64],
                    rowe[0:1, 0:512], start=True, stop=True))
                inc("tensor", lambda t: t.matmul(
                    ps_r[:, 512:1024], cons["ones_row_f"][0:1, 0:64],
                    rowe[0:1, 512:1024], start=True, stop=True), "q2t")
                wait("vector", "q2t")
                inc("vector", lambda v: v.tensor_copy(gC[:, :], ps_r[:, :]),
                    "q2v")
                wait("vector", "sp", max(t1, t2))
                op("vector", lambda v: v.memset(Rp(0, 1024), 0.0))
                op("vector", lambda v: v.memset(Zp(7 * 1024, 8192), 0.0))
                for b in range(1, NBLK):
                    op("vector", lambda v, b=b: v.tensor_mul(
                        tt1[0:64, :], Rp(1024 * (b - 1), 1024 * b),
                        gC[:, :]))
                    op("vector", lambda v, b=b: v.tensor_add(
                        Rp(1024 * b, 1024 * (b + 1)), tt1[0:64, :],
                        Gt(1024 * (b - 1), 1024 * b)))
                for b in range(NBLK - 2, -1, -1):
                    op("vector", lambda v, b=b: v.tensor_add(
                        Zp(1024 * b, 1024 * (b + 1)),
                        Zp(1024 * (b + 1), 1024 * (b + 2)),
                        Pt(1024 * (b + 1), 1024 * (b + 2))))
                op("vector", lambda v: v.tensor_mul(
                    tt1[0:64, :], Rp(7 * 1024, 8192), gC[:, :]))
                op("vector", lambda v: v.tensor_add(
                    tt1[0:64, :], tt1[0:64, :], Gt(7 * 1024, 8192)))
                inc("vector", lambda v: v.tensor_add(
                    tt1[64:128, :], Zp(0, 1024), Pt(0, 1024)), "q2v")
                wait("gpsimd", "q2v")
                dma("gpsimd",
                    ag_in.rearrange("(h r) d -> h r d", r=128)[:, 0:64, :]
                    .transpose([1, 0, 2]),
                    tt1[0:64, :].rearrange("i (h d) -> i h d", h=16),
                    semn="sp")
                t3 = dma("gpsimd",
                         ag_in.rearrange("(h r) d -> h r d",
                                         r=128)[:, 64:128, :]
                         .transpose([1, 0, 2]),
                         tt1[64:128, :].rearrange("i (h d) -> i h d", h=16),
                         semn="sp")
                wait("gpsimd", "sp", t3)
                inc("gpsimd", lambda e: e.collective_compute(
                    "AllGather", ALU.bypass,
                    replica_groups=[[0, 1, 2, 3], [4, 5, 6, 7]],
                    ins=[ag_in], outs=[ag_out]), "agd")
                wait("gpsimd", "agd")
                t4 = dma("gpsimd",
                         gat[:, :].rearrange("p (j h d) -> p j h d",
                                             j=4, h=16),
                         ag_out.rearrange("(j h two i) d -> (two i) j h d",
                                          j=4, h=16, two=2), semn="sp")

                wait("tensor", "q2t", 1)
                nw = 0
                for wi, wd, nchunk, cols, mc, sc in (
                        (0, w1T, 8, HID, m1rc, s1c),
                        (1, w2T, 32, E, m2rc, s2c)):
                    psw = ps_ws if wi == 0 else ps_ws2
                    npc = nchunk * (cols // 1024)
                    k0 = 0
                    for ch in range(nchunk):
                        for pc in range(cols // 1024):
                            tgt = dma("sync", wst2[:, :],
                                      wd[128 * ch:128 * (ch + 1),
                                         1024 * pc:1024 * (pc + 1)])
                            wait("scalar", "ld", tgt)
                            inc("scalar", lambda s: s.activation(
                                wabs[:, :], wst2[:, :], AF.Abs), "wsa")
                            wait("sync", "wsa")
                            wait("tensor", "wsa", P.v("wsa"))
                            for nh in range(2):
                                op("tensor", lambda t, nh=nh, k0=k0,
                                   npc=npc, psw=psw: t.matmul(
                                       psw[:, 0:512],
                                       cons["ones_col_f"][:, :],
                                       wabs[:, 512 * nh:512 * (nh + 1)],
                                       start=(k0 == 0 and nh == 0),
                                       stop=(k0 == npc - 1 and nh == 1)))
                            P.inc_prev("tensor", "wsm")
                            wait("scalar", "wsm", P.v("wsm") + 0)
                            k0 += 1
                            nw += 1
                    wait("vector", "wsm", P.v("wsm"))
                    inc("vector", lambda v, psw=psw: v.tensor_reduce(
                        rowe[0:1, 0:1], psw[0:1, 0:512], AX.X, ALU.add),
                        "wm1")
                    wait("vector", "wm1", P.v("wm1"))
                    op("vector", lambda v: v.tensor_scalar(
                        rowe[0:1, 1:2], rowe[0:1, 0:1], 1.0 / (E * HID),
                        EPS, ALU.mult, ALU.max))
                    inc("vector", lambda v: v.engine_nop(), "wm2")
                    if _DBG:
                        wait("gpsimd", "wm2", P.v("wm2"))
                        dma("gpsimd", ws_dbg[:, 4 * wi:4 * wi + 2],
                            rowe[0:1, 0:2], semn="sp")
                        wait("vector", "sp", P.v("sp"))
                    wait("tensor", "wm2", P.v("wm2"))
                    inc("tensor", lambda t: t.matmul(
                        ps_s2[:, :], cons["ones_row_f"][:, :],
                        rowe[0:1, 1:2], start=True, stop=True), "wm3")
                    wait("vector", "wm3", P.v("wm3"))
                    inc("vector", lambda v, mc=mc: v.tensor_copy(
                        mc[:, :], ps_s2[:, :]), "wm4")
                    wait("scalar", "wm4", P.v("wm4"))
                    op("scalar", lambda s, mc=mc: s.activation(
                        tcol[:, :], mc[:, :], AF.Ln))
                    inc("scalar", lambda s, sc=sc: s.activation(
                        sc[:, :], tcol[:, :], AF.Exp, scale=-1.0), "wm5")
                    wait("tensor", "wm2", P.v("wm2"))
                wait("tensor", "wm3", P.v("wm3"))
                wait("vector", "wm5", P.v("wm5"))

                wait("vector", "sp", t4)
                op("vector", lambda v: v.memset(RacZac[:, :], 0.0))
                wait("scalar", "wm3", P.v("wm3"))
                for j in range(4):
                    inc("scalar", lambda s, j=j: s.activation(
                        rowe[:, :], g1024[:, :], AF.Exp,
                        scale=nd4[0:1, j:j + 1]), "bd1")
                    wait("tensor", "bd1")
                    op("tensor", lambda t: t.matmul(
                        ps_w[:, 0:512], cons["ones_row_f"][0:1, 0:64],
                        rowe[0:1, 0:512], start=True, stop=True))
                    inc("tensor", lambda t: t.matmul(
                        ps_w[:, 512:1024], cons["ones_row_f"][0:1, 0:64],
                        rowe[0:1, 512:1024], start=True, stop=True), "bd2")
                    wait("vector", "bd2")
                    op("vector", lambda v, j=j: v.tensor_mul(
                        tt1[0:64, :], gat[0:64, 1024 * j:1024 * (j + 1)],
                        ps_w[:, :]))
                    op("vector", lambda v: v.tensor_add(
                        RacZac[0:64, :], RacZac[0:64, :], tt1[0:64, :]))
                    inc("vector", lambda v, j=j: v.scalar_tensor_tensor(
                        RacZac[64:128, :],
                        gat[64:128, 1024 * j:1024 * (j + 1)],
                        zcol[64:128, j:j + 1], RacZac[64:128, :],
                        ALU.mult, ALU.add), "bd3")
                    wait("tensor", "bd3")
                    wait("scalar", "bd2", P.v("bd2"))
                for b in range(NBLK):
                    inc("scalar", lambda s, b=b: s.activation(
                        rowe[:, :], g1024[:, :], AF.Exp, scale=-256.0 * b),
                        "bd4")
                    wait("tensor", "bd4")
                    op("tensor", lambda t: t.matmul(
                        ps_w[:, 0:512], cons["ones_row_f"][0:1, 0:64],
                        rowe[0:1, 0:512], start=True, stop=True))
                    inc("tensor", lambda t: t.matmul(
                        ps_w[:, 512:1024], cons["ones_row_f"][0:1, 0:64],
                        rowe[0:1, 512:1024], start=True, stop=True), "bd5")
                    wait("vector", "bd5")
                    op("vector", lambda v: v.tensor_mul(
                        tt1[0:64, :], RacZac[0:64, :], ps_w[:, :]))
                    op("vector", lambda v, b=b: v.tensor_add(
                        Rb16[0:64, 1024 * b:1024 * (b + 1)], tt1[0:64, :],
                        Rp(1024 * b, 1024 * (b + 1))))
                    inc("vector", lambda v, b=b: v.tensor_add(
                        Zb16[64:128, 1024 * b:1024 * (b + 1)],
                        Zp(1024 * b, 1024 * (b + 1)),
                        RacZac[64:128, :]), "bd6")
                    wait("tensor", "bd6")
                    wait("scalar", "bd5", P.v("bd5"))
                wait("gpsimd", "bd6", P.v("bd6"))
                dma("gpsimd", Rb16[64:128, :], Rb16[0:64, :], semn="sp")
                dma("gpsimd", Zb16[0:64, :], Zb16[64:128, :], semn="sp")
            barrier()
        barrier()
        P.phase_end()
        if upto < 3:
            nc._prog = P
            P.emit()
            return nc

        with ES() as p3:
            s3 = lambda n, shp, dt=F32: p3.enter_context(
                nc.sbuf_tensor(n, shp, dt))
            Wq_b = s3("Wq_b", [128, 8 * 1024], BF16)
            Wg_b = s3("Wg_b", [128, 8 * 1024], BF16)
            wstg3 = s3("wstg3", [128, 2 * 1024])
            Kb = s3("Kb", [128, 2 * 2048], BF16)
            Vb = s3("Vb", [128, 2 * 2048], BF16)
            x3 = s3("x3", [128, 2 * 2048])
            qtm = s3("qtm", [128, 1024], BF16)
            qsw = s3("qsw", [128, 1024], BF16)
            qr1 = s3("qr1", [128, 1024], BF16)
            qrot = s3("qrot", [128, 1024], BF16)
            qfm = s3("qfm", [128, 2048], BF16)
            qd2 = s3("qd2", [128, 512], BF16)
            gate = s3("gate", [128, 2048], BF16)
            swt = s3("swt", [128, 2 * 1024], BF16)
            otm = s3("otm", [128, 256])
            outsb = s3("outsb", [128, 2048])
            ctb3 = s3("ctb3", [128, 2 * 1024], BF16)
            stb3 = s3("stb3", [128, 2 * 1024], BF16)
            nw3 = 0
            for wd, wb in ((WqT, Wq_b), (WgT, Wg_b)):
                for ec in range(8):
                    if nw3 >= 2:
                        wait("sync", "p3w", nw3 - 1)
                    tgt = dma("sync", wstg3[:, (nw3 % 2) * 1024:
                                            (nw3 % 2 + 1) * 1024],
                              wd[128 * ec:128 * (ec + 1), :])
                    wait("scalar", "ld", tgt)
                    inc("scalar", lambda s, wb=wb, ec=ec, nw3=nw3: s.copy(
                        wb[:, 1024 * ec:1024 * (ec + 1)],
                        wstg3[:, (nw3 % 2) * 1024:(nw3 % 2 + 1) * 1024]),
                        "p3w")
                    nw3 += 1
            barrier()
            with nc.psum_tensor("q3_0", [128, 512], F32) as ps_q0, \
                 nc.psum_tensor("q3_1", [128, 512], F32) as ps_q1, \
                 nc.psum_tensor("q3_t0", [128, 128], BF16) as ps3_t0, \
                 nc.psum_tensor("q3_t1", [128, 128], BF16) as ps3_t1, \
                 nc.psum_tensor("s3_0", [128, 512], F32) as ps3_s0, \
                 nc.psum_tensor("s3_1", [128, 512], F32) as ps3_s1, \
                 nc.psum_tensor("o3_0", [128, 256], F32) as ps3_o0, \
                 nc.psum_tensor("o3_1", [128, 256], F32) as ps3_o1:
                ps3_g = ps3_s0
                ps3_t = (ps3_t0, ps3_t1)
                pss = (ps3_s0, ps3_s1)
                pso = (ps3_o0, ps3_o1)
                spill_tgt = [0]
                for blk in range(NBLK):
                    pr = blk % 2
                    if blk >= 2:
                        wait("gpsimd", "omd", (blk - 1) * 8)
                    lk = dma("gpsimd", Kb[:, 2048 * pr:2048 * (pr + 1)],
                             kfm_sp[:, 2048 * blk:2048 * (blk + 1)],
                             semn="sp")
                    lv = dma("gpsimd", Vb[:, 2048 * pr:2048 * (pr + 1)],
                             vtm_sp[:, 2048 * blk:2048 * (blk + 1)],
                             semn="sp")
                    if blk >= 2:
                        wait("sync", "oev", (blk - 1) * 8)
                    lx = dma("sync", x3[:, 2048 * pr:2048 * (pr + 1)]
                             .rearrange("p (c t) -> p c t", c=8),
                             xT[:, CB * blk:CB * (blk + 1)]
                             .rearrange("(c p) t -> p c t", p=128))
                    for tc in range(2):
                        tg3 = blk * 2 + tc
                        if tg3 >= 2:
                            wait("sync", "qrd", tg3 - 1)
                        t_c = dma("sync", ctb3[:, tc * 1024:(tc + 1) * 1024],
                                  qcos_d[CB * blk + 128 * tc:
                                         CB * blk + 128 * (tc + 1), :])
                        t_s = dma("sync", stb3[:, tc * 1024:(tc + 1) * 1024],
                                  qsin_d[CB * blk + 128 * tc:
                                         CB * blk + 128 * (tc + 1), :])
                        if tg3 >= 1:
                            wait("tensor", "qev", tg3)
                        for nh, pk in ((0, ps_q0), (1, ps_q1)):
                            for ec in range(8):
                                op("tensor", lambda t, ec=ec, nh=nh, pk=pk,
                                   blk=blk, tc=tc: t.matmul(
                                       pk[:, :],
                                       xnb[:, 2048 * ec + CB * blk
                                           + 128 * tc:
                                           2048 * ec + CB * blk
                                           + 128 * (tc + 1)],
                                       Wq_b[:, 1024 * ec + 512 * nh:
                                            1024 * ec + 512 * (nh + 1)],
                                       start=(ec == 0), stop=False))
                            op("tensor", lambda t, nh=nh, pk=pk: t.matmul(
                                pk[:, :], cons["ones_row_b"][:, :],
                                bqr[0:1, 512 * nh:512 * (nh + 1)],
                                start=False, stop=True))
                        P.inc_prev("tensor", "qmm")
                        wait("scalar", "qmm")
                        if tg3 >= 1:
                            wait("scalar", "qrd", tg3)
                        op("scalar", lambda s: s.copy(
                            qtm[:, 0:512], ps_q0[:, :]))
                        inc("scalar", lambda s: s.copy(
                            qtm[:, 512:1024], ps_q1[:, :]), "qev")
                        wait("gpsimd", "qev")
                        if tg3 >= 1:
                            wait("gpsimd", "qrd", tg3)
                        op("gpsimd", lambda e: e.tensor_scalar_mul(
                            qsw[:, 0:1024:2], qtm[:, 1:1024:2], -1.0))
                        inc("gpsimd", lambda e: e.tensor_copy(
                            qsw[:, 1:1024:2], qtm[:, 0:1024:2]), "qsd")
                        wait("vector", "qsd")
                        wait("vector", "ld", max(t_c, t_s))
                        if tg3 >= 1:
                            wait("vector", "qtt", 8 * tg3)
                        op("vector", lambda v, tc=tc: v.tensor_mul(
                            qrot[:, :], qtm[:, 0:1024],
                            ctb3[:, tc * 1024:(tc + 1) * 1024]))
                        op("vector", lambda v, tc=tc: v.tensor_mul(
                            qr1[:, :], qsw[:, :],
                            stb3[:, tc * 1024:(tc + 1) * 1024]))
                        inc("vector", lambda v: v.tensor_add(
                            qrot[:, :], qrot[:, :], qr1[:, :]), "qrd")
                        wait("tensor", "qrd")
                        for jj in range(8):
                            jg = tg3 * 8 + jj
                            if jg >= 2:
                                wait("tensor", "qte", jg - 1)
                            inc("tensor", lambda t, jj=jj: t.transpose(
                                ps3_t[jj % 2][:, :],
                                qrot[:, 128 * jj:128 * (jj + 1)],
                                cons["ident_b"][:, :]), "qtt")
                            wait("vector", "qtt", tg3 * 8 + jj + 1)
                            if blk >= 1 and tc == 0 and jj == 0:
                                wait("vector", "omd", blk * 8)
                            inc("vector", lambda v, jj=jj, tc=tc:
                                v.tensor_copy(
                                    qfm[:, 256 * jj + 128 * tc:
                                        256 * jj + 128 * (tc + 1)],
                                    ps3_t[jj % 2][:, :]), "qte")
                    if blk >= 1:
                        wait("tensor", "swd", blk * 8)
                        wait("scalar", "oev", blk * 8)
                    for fc in range(8):
                        kf = blk * 8 + fc
                        if kf >= 1:
                            wait("tensor", "gev", kf)
                        for ec in range(8):
                            op("tensor", lambda t, ec=ec, fc=fc, blk=blk:
                               t.matmul(
                                   ps3_g[:, 0:256],
                                   Wg_b[:, 1024 * ec + 128 * fc:
                                        1024 * ec + 128 * (fc + 1)],
                                   xnb[:, 2048 * ec + CB * blk:
                                       2048 * ec + CB * (blk + 1)],
                                   start=(ec == 0), stop=(ec == 7)))
                        P.inc_prev("tensor", "gmm")
                        wait("scalar", "gmm")
                        inc("scalar", lambda s, fc=fc: s.activation(
                            gate[:, 256 * fc:256 * (fc + 1)],
                            ps3_g[:, 0:256],
                            AF.Silu, bias=bgc[:, fc:fc + 1]), "gev")
                    wait("vector", "qte", 16 * (blk + 1))
                    inc("vector", lambda v: v.engine_nop(), "qdd")
                    wait("tensor", "sp", max(lk, lv))
                    wait("tensor", "qdd")
                    wait("tensor", "gev", blk * 8 + 8)
                    for pi in range(8):
                        pg = blk * 8 + pi
                        if pg >= 1:
                            wait("tensor", "swd", pg)
                        for hh in (0, 1):
                            for kc in (0, 1):
                                op("tensor", lambda t, kc=kc, pi=pi,
                                   hh=hh, pr=pr: t.matmul(
                                       pss[hh][:, 256 * kc:256 * (kc + 1)],
                                       Kb[64 * hh:64 * hh + 64,
                                          2048 * pr + 1024 * kc + 128 * pi:
                                          2048 * pr + 1024 * kc
                                          + 128 * (pi + 1)],
                                       qfm[64 * hh:64 * hh + 64,
                                           256 * pi:256 * (pi + 1)],
                                       start=True, stop=True))
                        P.inc_prev("tensor", "stm")
                        wait("vector", "stm", pg + 1)
                        if pg >= 2:
                            wait("vector", "omd", pg - 1)
                        op("vector", lambda v, pi=pi, pg=pg: v.tensor_mul(
                            qd2[:, 256 * (pg % 2):256 * (pg % 2 + 1)],
                            qfm[:, 256 * pi:256 * (pi + 1)],
                            Bq[:, 256 * pi:256 * (pi + 1)]))
                        for hh in (0, 1):
                            for kc in (0, 1):
                                h = 2 * pi + hh
                                op("vector", lambda v, kc=kc, h=h, hh=hh,
                                   pi=pi: v.tensor_mul(
                                       swt[:, 1024 * (pi % 2) + 512 * hh
                                           + 256 * kc:
                                           1024 * (pi % 2) + 512 * hh
                                           + 256 * (kc + 1)],
                                       pss[hh][:, 256 * kc:256 * (kc + 1)],
                                       WTt[:, 512 * h + 256 * kc:
                                           512 * h + 256 * (kc + 1)]))
                        P.inc_prev("vector", "swd")
                        wait("tensor", "swd", pg + 1)
                        if pg >= 2:
                            wait("tensor", "oev", pg - 1)
                        for hh in (0, 1):
                            h = 2 * pi + hh
                            hb = 64 * hh
                            for kc in (0, 1):
                                op("tensor", lambda t, kc=kc, h=h, hb=hb,
                                   pr=pr, pi=pi, hh=hh: t.matmul(
                                       pso[pi % 2][hb:hb + 64, :],
                                       Vb[:, 2048 * pr + 1024 * kc + 64 * h:
                                          2048 * pr + 1024 * kc
                                          + 64 * (h + 1)],
                                       swt[:, 1024 * (pi % 2) + 512 * hh
                                           + 256 * kc:
                                           1024 * (pi % 2) + 512 * hh
                                           + 256 * (kc + 1)],
                                       start=(kc == 0), stop=False))
                            op("tensor", lambda t, h=h, hb=hb, blk=blk,
                               pi=pi, pg=pg: t.matmul(
                                   pso[pi % 2][hb:hb + 64, :],
                                   Rb16[hb:hb + 64, 1024 * blk + 64 * h:
                                        1024 * blk + 64 * (h + 1)],
                                   qd2[hb:hb + 64,
                                       256 * (pg % 2):256 * (pg % 2 + 1)],
                                   start=False, stop=False))
                            op("tensor", lambda t, h=h, hb=hb, blk=blk,
                               pi=pi: t.matmul(
                                   pso[pi % 2][hb:hb + 64, :],
                                   Zb16[hb:hb + 64, 1024 * blk + 64 * h:
                                        1024 * blk + 64 * (h + 1)],
                                   qfm[hb:hb + 64, 256 * pi:256 * (pi + 1)],
                                   start=False, stop=True))
                        P.inc_prev("tensor", "omd")
                        wait("vector", "omd")
                        wait("vector", "ld", lx)
                        wait("vector", "gev", blk * 8 + 8)
                        if pi == 0 and blk >= 1:
                            wait("vector", "sp", spill_tgt[0])
                        op("vector", lambda v, pi=pi: v.tensor_mul(
                            otm[:, :], pso[pi % 2][:, :],
                            gate[:, 256 * pi:256 * (pi + 1)]))
                        inc("vector", lambda v, pi=pi, pr=pr:
                            v.tensor_add(
                                outsb[:, 256 * pi:256 * (pi + 1)],
                                otm[:, :],
                                x3[:, 2048 * pr + 256 * pi:
                                   2048 * pr + 256 * (pi + 1)]), "oev")
                    wait("gpsimd", "oev", blk * 8 + 8)
                    dma("gpsimd",
                        part_sp.rearrange("(c p) T -> p c T", p=128)
                        [:, :, CB * blk:CB * (blk + 1)],
                        outsb[:, :].rearrange("p (c t) -> p c t", c=8),
                        semn="sp")
                    spill_tgt[0] = P.v("sp")
                barrier()
        p23.close()
        barrier()
        P.phase_end()
        if upto < 4:
            nc._prog = P
            P.emit()
            return nc

        ry2sb = stk.enter_context(
            nc.sbuf_tensor("ry2sb", [128, NFT * FT], F32))
        with ES() as p4:
            s4 = lambda n, shp, dt=F32: p4.enter_context(
                nc.sbuf_tensor(n, shp, dt))
            b1outer = ES()
            w1q = b1outer.enter_context(
                nc.sbuf_tensor("w1q", [128, 8 * 4096], BF16))
            wq_scope = ES()
            s4b = lambda n, shp, dt=F32: wq_scope.enter_context(
                nc.sbuf_tensor(n, shp, dt))
            wst4 = s4b("wst4", [128, 4096])
            t4a = s4b("t4a", [128, 4096])
            ti4 = s4b("ti4", [128, 4096], I32)
            tf4 = s4b("tf4", [128, 4096])
            for ch in range(8):
                tgt = dma("sync", wst4[:, :],
                          w1T[128 * ch:128 * (ch + 1), :])
                wait("sync", "wqa", P.v("wqa") + 1)
                wait("vector", "ld", tgt)
                op("vector", lambda v: v.tensor_scalar_mul(
                    t4a[:, :], wst4[:, :], s1c[:, :]))
                inc("vector", lambda v: v.tensor_copy(ti4[:, :], t4a[:, :]),
                    "wqa")
                op("vector", lambda v: v.tensor_copy(tf4[:, :], ti4[:, :]))
                inc("vector", lambda v, ch=ch: v.tensor_scalar(
                    w1q[:, 4096 * ch:4096 * (ch + 1)], tf4[:, :],
                    -1.0, 1.0, ALU.max, ALU.min), "wqd")
            if _DBG:
                dma("gpsimd", w1q_dbg, w1q[:, :], semn="sp")
                op("vector", lambda v: v.tensor_copy(dbgc[:, 0:1],
                                                     m1rc[:, :]))
                op("vector", lambda v: v.tensor_copy(dbgc[:, 1:2],
                                                     m2rc[:, :]))
                op("vector", lambda v: v.tensor_copy(dbgc[:, 2:3],
                                                     s1c[:, :]))
                inc("vector", lambda v: v.tensor_copy(dbgc[:, 3:4],
                                                      s2c[:, :]), "dgc")
                wait("gpsimd", "dgc")
                dma("gpsimd", mc_dbg, dbgc[:, :], semn="sp")
            barrier()
            P.phase_end()
            wq_scope.close()
            with ES() as b1s:
                s41 = lambda n, shp, dt=F32: b1s.enter_context(
                    nc.sbuf_tensor(n, shp, dt))
                xq4 = s41("xq4", [128, FT], BF16)
                rwt4 = s41("rwt4", [1, 2 * FT])
                h0 = s41("h0", [128, 8 * FT])
                h0q = s41("h0q", [128, 8 * FT], BF16)
                M4 = s41("M4", [128, FT])
                X4 = s41("X4", [128, 8])
                XC = s41("XC", [128, 8])
                XP = s41("XP", [128, 8])
                rws4 = s41("rws4", [32, 128])
                sh4 = s41("sh4", [32, 128])
                h1 = s41("h1", [128, 32 * FT], BF16)
                M42 = s41("M42", [128, FT], BF16)
                t4b = s41("t4b", [128, FT])
                ti4b = s41("ti4b", [128, FT], I32)
                tf4b = s41("tf4b", [128, FT])
                h1_tgt = [0]
                for ft in range(NFT):
                    with nc.psum_tensor("f4su_%d" % ft, [1, FT], F32) as f4su, \
                         nc.psum_tensor("f4sq_%d" % ft, [1, FT], F32) as f4sq, \
                         nc.psum_tensor("f4bc_%d" % ft, [128, 2 * FT], F32) as f4bc:
                        for fc in range(8):
                            k = ft * 8 + fc
                            if k >= 1:
                                wait("scalar", "fqm", k)
                            inc("scalar", lambda s, fc=fc, ft=ft:
                                s.activation(
                                    xq4[:, :],
                                    xnb[:, 2048 * fc + FT * ft:
                                        2048 * fc + FT * (ft + 1)],
                                    AF.Square), "fqa")
                        for fc in range(8):
                            op("tensor", lambda t, fc=fc, ft=ft,
                               f4su=f4su: t.matmul(
                                f4su[:, :], cons["ones_col_b"][:, :],
                                xnb[:, 2048 * fc + FT * ft:
                                    2048 * fc + FT * (ft + 1)],
                                start=(fc == 0), stop=(fc == 7)))
                        for fc in range(8):
                            wait("tensor", "fqa", ft * 8 + fc + 1)
                            inc("tensor", lambda t, fc=fc,
                                f4sq=f4sq: t.matmul(
                                f4sq[:, :], cons["ones_col_b"][:, :],
                                xq4[:, :],
                                start=(fc == 0), stop=(fc == 7)), "fqm")
                        q0 = rwt4[0:1, 0:FT]
                        q1 = rwt4[0:1, FT:2 * FT]
                        wait("vector", "fqm", 8 * (ft + 1))
                        if ft >= 1:
                            wait("vector", "fbc", ft)
                        op("vector", lambda v, f4su=f4su: v.tensor_scalar_mul(
                            q0, f4su[:, :], 1.0 / 1024.0))
                        inc("vector", lambda v, f4sq=f4sq:
                            v.tensor_scalar_mul(
                                q1, f4sq[:, :], 1.0 / 1024.0), "fr1")
                        if _DBG and ft == 0:
                            wait("gpsimd", "fr1", P.v("fr1"))
                            dma("gpsimd", rw_dbg[:, 0:2 * FT],
                                rwt4[:, :], semn="sp")
                            wait("vector", "sp", P.v("sp"))
                        wait("tensor", "fr1")
                        inc("tensor", lambda t, f4bc=f4bc: t.matmul(
                            f4bc[:, 0:FT], cons["ones_row_f"][:, :], q0,
                            start=True, stop=True), "fbd")
                        wait("vector", "fbd")
                        op("vector", lambda v: v.tensor_mul(q0, q0, q0))
                        inc("vector", lambda v: v.tensor_sub(q1, q1, q0),
                            "fr2")
                        wait("scalar", "fr2")
                        op("scalar", lambda s: s.activation(
                            q0, q1, AF.Ln, bias=cons["epsc"][0:1, :]))
                        inc("scalar", lambda s: s.activation(
                            q1, q0, AF.Exp, scale=-0.5), "fr4")
                        if _DBG and ft == 0:
                            wait("gpsimd", "fr4", P.v("fr4"))
                            dma("gpsimd", rw_dbg[:, 2 * FT:4 * FT],
                                rwt4[:, :], semn="sp")
                            wait("tensor", "sp", P.v("sp"))
                        wait("tensor", "fr4")
                        inc("tensor", lambda t, f4bc=f4bc: t.matmul(
                            f4bc[:, FT:2 * FT], cons["ones_row_f"][:, :],
                            q1, start=True, stop=True), "fbc")
                        wait("vector", "fbc")
                        for fc in range(8):
                            op("vector", lambda v, fc=fc, ft=ft,
                               f4bc=f4bc: v.tensor_sub(
                                t4b[:, :],
                                xnb[:, 2048 * fc + FT * ft:
                                    2048 * fc + FT * (ft + 1)],
                                f4bc[:, 0:FT]))
                            op("vector", lambda v, fc=fc, f4bc=f4bc:
                               v.tensor_mul(
                                t4b[:, :], t4b[:, :], f4bc[:, FT:2 * FT]))
                            op("vector", lambda v, fc=fc: v.tensor_scalar(
                                h0[:, FT * fc:FT * (fc + 1)], t4b[:, :],
                                gmc[:, fc:fc + 1], btc[:, fc:fc + 1],
                                ALU.mult, ALU.add))
                        inc("vector", lambda v: v.engine_nop(), "fh0")
                    with nc.psum_tensor("f4ta_%d" % ft, [128, 128], F32) as f4ta, \
                         nc.psum_tensor("f4tb_%d" % ft, [128, 128], F32) as f4tb, \
                         nc.psum_tensor("f4tx_%d" % ft, [8, 128], F32) as f4tx, \
                         nc.psum_tensor("f4b1_%d" % ft, [128, FT], F32) as f4b1, \
                         nc.psum_tensor("f4b2_%d" % ft, [128, FT], F32) as f4b2:
                        f4t = (f4ta, f4tb)
                        inc("vector", lambda v: v.tensor_reduce(
                            M4[:, :],
                            h0[:, :].rearrange("p (c t) -> p t c", c=8),
                            AX.X, ALU.max, apply_absolute_value=True),
                            "fm1")
                        wait("tensor", "fm1")
                        for r in range(4):
                            rg = ft * 4 + r
                            if r >= 2:
                                wait("tensor", "fm2b", rg - 1)
                            inc("tensor", lambda t, r=r, f4t=f4t: t.transpose(
                                f4t[r % 2][:, :],
                                M4[:, 128 * r:128 * (r + 1)],
                                cons["ident_f"][:, :]), "fm2")
                            wait("vector", "fm2", ft * 4 + r + 1)
                            inc("vector", lambda v, r=r, f4t=f4t:
                                v.tensor_reduce(
                                    X4[:, r:r + 1], f4t[r % 2][:, :],
                                    AX.X, ALU.max), "fm2b")
                        wait("vector", "fm2b", P.v("fm2b"))
                        inc("vector", lambda v: v.tensor_scalar_max(
                            XC[:, 0:4], X4[:, 0:4], EPS), "fmc")
                        wait("scalar", "fmc")
                        op("scalar", lambda s: s.activation(
                            XC[:, 4:8], XC[:, 0:4], AF.Ln))
                        inc("scalar", lambda s: s.activation(
                            XP[:, 0:4], XC[:, 4:8], AF.Exp, scale=-1.0,
                            bias=cons["l127c"][:, :]), "fmd")
                        wait("vector", "fmd")
                        inc("vector", lambda v: v.tensor_scalar_mul(
                            XP[:, 4:8], XC[:, 0:4], 1.0 / 127.0), "fm3")
                        if _DBG and ft == 0:
                            wait("gpsimd", "fm3", P.v("fm3"))
                            dma("gpsimd", xp_dbg[:, 0:8], X4[:, :],
                                semn="sp")
                            dma("gpsimd", xp_dbg[:, 8:16], XP[:, :],
                                semn="sp")
                            wait("vector", "sp", P.v("sp"))
                        wait("tensor", "fm3")
                        inc("tensor", lambda t, f4tx=f4tx: t.transpose(
                            f4tx[:, :], XP[:, :], cons["ident_f"][:, :]),
                            "fm4")
                        wait("vector", "fm4")
                        inc("vector", lambda v, f4tx=f4tx: v.tensor_copy(
                            rws4[0:8, :], f4tx[:, :]), "fm5")
                        if _DBG and ft == 0:
                            wait("gpsimd", "fm5", P.v("fm5"))
                            dma("gpsimd",
                                xp_dbg[:, 16:48].rearrange(
                                    "p (a b) -> (p a) b", a=1)
                                if False else xp_dbg[0:32, 16:48],
                                rws4[:, 0:32], semn="sp")
                            wait("vector", "sp", P.v("sp"))
                        for r in range(4):
                            wait("vector", "fm5")
                            inc("vector", lambda v, r=r: v.stream_shuffle(
                                sh4[:, :], rws4[:, :],
                                [r] + list(range(1, 32))), "fm6")
                            wait("tensor", "fm6")
                            inc("tensor", lambda t, r=r, f4b1=f4b1: t.matmul(
                                f4b1[:, 128 * r:128 * (r + 1)],
                                cons["ones_row_f"][:, :], sh4[0:1, :],
                                start=True, stop=True), "fm7")
                            wait("vector", "fm7")
                            inc("vector", lambda v, r=r: v.stream_shuffle(
                                sh4[:, :], rws4[:, :],
                                [4 + r] + list(range(1, 32))), "fm5")
                            wait("tensor", "fm5")
                            inc("tensor", lambda t, r=r, f4b2=f4b2: t.matmul(
                                f4b2[:, 128 * r:128 * (r + 1)],
                                cons["ones_row_f"][:, :], sh4[0:1, :],
                                start=True, stop=True), "fm7")
                            wait("vector", "fm7")
                            inc("vector", lambda v: v.engine_nop(), "fm5")
                        for fc in range(8):
                            op("vector", lambda v, fc=fc, f4b1=f4b1:
                               v.tensor_mul(
                                t4b[:, :], h0[:, FT * fc:FT * (fc + 1)],
                                f4b1[:, :]))
                            op("vector", lambda v: v.tensor_copy(
                                ti4b[:, :], t4b[:, :]))
                            op("vector", lambda v: v.tensor_copy(
                                tf4b[:, :], ti4b[:, :]))
                            op("vector", lambda v: v.tensor_scalar(
                                t4b[:, :], tf4b[:, :], -128.0, 127.0,
                                ALU.max, ALU.min))
                            op("vector", lambda v, fc=fc, f4b2=f4b2:
                               v.tensor_mul(
                                h0q[:, FT * fc:FT * (fc + 1)], t4b[:, :],
                                f4b2[:, :]))
                        inc("vector", lambda v: v.engine_nop(), "fq1")
                        if _DBG:
                            wait("gpsimd", "fq1", P.v("fq1"))
                            dma("gpsimd",
                                h0q_dbg[:, 4096 * ft:4096 * (ft + 1)],
                                h0q[:, :], semn="sp")
                            wait("vector", "sp", P.v("sp"))
                        with nc.psum_tensor("f4m0_%d" % ft, [128, FT], F32) as f4m0, \
                             nc.psum_tensor("f4m1_%d" % ft, [128, FT], F32) as f4m1:
                            psm = (f4m0, f4m1)
                            if ft >= 1:
                                wait("scalar", "sp", h1_tgt[0])
                            wait("tensor", "fq1")
                            for f2 in range(32):
                                kf = ft * 32 + f2
                                if f2 >= 2:
                                    wait("tensor", "fge", kf - 1)
                                for ec in range(8):
                                    op("tensor", lambda t, ec=ec, f2=f2,
                                       psm=psm: t.matmul(
                                           psm[f2 % 2][:, :],
                                           w1q[:, 4096 * ec + 128 * f2:
                                               4096 * ec + 128 * (f2 + 1)],
                                           h0q[:, FT * ec:FT * (ec + 1)],
                                           start=(ec == 0), stop=(ec == 7)))
                                P.inc_prev("tensor", "fmm")
                                wait("scalar", "fmm")
                                inc("scalar", lambda s, f2=f2,
                                    psm=psm: s.activation(
                                    h1[:, FT * f2:FT * (f2 + 1)],
                                    psm[f2 % 2][:, :], AF.Gelu,
                                    bias=b1c[:, f2:f2 + 1],
                                    scale=m1rc[:, 0:1]), "fge")
                        wait("vector", "fge", P.v("fge"))
                        inc("vector", lambda v: v.tensor_reduce(
                            M42[:, :],
                            h1[:, :].rearrange("p (c t) -> p t c", c=32),
                            AX.X, ALU.max, apply_absolute_value=True),
                            "fn1")
                        with nc.psum_tensor("f4t2a_%d" % ft, [128, 128],
                                            BF16) as f4t2a, \
                             nc.psum_tensor("f4t2b_%d" % ft, [128, 128],
                                            BF16) as f4t2b:
                            f4t2 = (f4t2a, f4t2b)
                            wait("tensor", "fn1")
                            for r in range(4):
                                rg = ft * 4 + r
                                if r >= 2:
                                    wait("tensor", "fn2b", rg - 1)
                                inc("tensor", lambda t, r=r, f4t2=f4t2:
                                    t.transpose(
                                        f4t2[r % 2][:, :],
                                        M42[:, 128 * r:128 * (r + 1)],
                                        cons["ident_b"][:, :]), "fn2")
                                wait("vector", "fn2", ft * 4 + r + 1)
                                inc("vector", lambda v, r=r, f4t2=f4t2:
                                    v.tensor_reduce(
                                        X4[:, r:r + 1],
                                        f4t2[r % 2][:, :],
                                        AX.X, ALU.max), "fn2b")
                        wait("vector", "fn2b", P.v("fn2b"))
                        inc("vector", lambda v: v.tensor_scalar_max(
                            XC[:, 0:4], X4[:, 0:4], EPS), "fnc")
                        wait("scalar", "fnc")
                        op("scalar", lambda s: s.activation(
                            XC[:, 4:8], XC[:, 0:4], AF.Ln))
                        inc("scalar", lambda s: s.activation(
                            XP[:, 0:4], XC[:, 4:8], AF.Exp, scale=-1.0,
                            bias=cons["l127c"][:, :]), "fnd")
                        wait("vector", "fnd")
                        inc("vector", lambda v: v.tensor_scalar(
                            XP[:, 4:8], XC[:, 0:4], m2rc[:, 0:1],
                            1.0 / 127.0, ALU.mult, ALU.mult), "fn3")
                        wait("tensor", "fn3")
                        inc("tensor", lambda t, f4tx=f4tx: t.transpose(
                            f4tx[:, :], XP[:, :], cons["ident_f"][:, :]),
                            "fn4")
                        wait("vector", "fn4")
                        inc("vector", lambda v, f4tx=f4tx: v.tensor_copy(
                            rws4[0:8, :], f4tx[:, :]), "fn5")
                        for r in range(4):
                            wait("vector", "fn5")
                            inc("vector", lambda v, r=r: v.stream_shuffle(
                                sh4[:, :], rws4[:, :],
                                [r] + list(range(1, 32))), "fn6")
                            wait("tensor", "fn6")
                            inc("tensor", lambda t, r=r, f4b1=f4b1: t.matmul(
                                f4b1[:, 128 * r:128 * (r + 1)],
                                cons["ones_row_f"][:, :], sh4[0:1, :],
                                start=True, stop=True), "fn7")
                            wait("vector", "fn7")
                            inc("vector", lambda v, r=r: v.stream_shuffle(
                                sh4[:, :], rws4[:, :],
                                [4 + r] + list(range(1, 32))), "fn5")
                            wait("tensor", "fn5")
                            inc("tensor", lambda t, r=r, f4b2=f4b2: t.matmul(
                                f4b2[:, 128 * r:128 * (r + 1)],
                                cons["ones_row_f"][:, :], sh4[0:1, :],
                                start=True, stop=True), "fn7")
                            wait("vector", "fn7")
                            inc("vector", lambda v: v.engine_nop(), "fn5")
                        inc("vector", lambda v, ft=ft, f4b2=f4b2:
                            v.tensor_copy(
                            ry2sb[:, FT * ft:FT * (ft + 1)], f4b2[:, :]),
                            "fn8")
                        for f2 in range(32):
                            op("vector", lambda v, f2=f2, f4b1=f4b1:
                               v.tensor_mul(
                                t4b[:, :], h1[:, FT * f2:FT * (f2 + 1)],
                                f4b1[:, :]))
                            op("vector", lambda v: v.tensor_copy(
                                ti4b[:, :], t4b[:, :]))
                            op("vector", lambda v: v.tensor_copy(
                                tf4b[:, :], ti4b[:, :]))
                            op("vector", lambda v, f2=f2: v.tensor_scalar(
                                h1[:, FT * f2:FT * (f2 + 1)], tf4b[:, :],
                                -128.0, 127.0, ALU.max, ALU.min))
                        inc("vector", lambda v: v.engine_nop(), "fq2")
                        wait("gpsimd", "fq2")
                        dma("gpsimd",
                            h1q_sp.rearrange("(c p) T -> p c T", p=128)
                            [:, :, FT * ft:FT * (ft + 1)],
                            h1[:, :].rearrange("p (c t) -> p c t", c=32),
                            semn="sp")
                        h1_tgt[0] = P.v("sp")
                barrier()
            b1outer.close()
            barrier()
            P.phase_end()
            with ES() as b2s:
                s42 = lambda n, shp, dt=F32: b2s.enter_context(
                    nc.sbuf_tensor(n, shp, dt))
                w2q = s42("w2q", [128, 32 * 1024], BF16)
                wr_scope = ES()
                s42b = lambda n, shp, dt=F32: wr_scope.enter_context(
                    nc.sbuf_tensor(n, shp, dt))
                wst5 = s42b("wst5", [128, 1024])
                t5a = s42b("t5a", [128, 1024])
                ti5 = s42b("ti5", [128, 1024], I32)
                tf5 = s42b("tf5", [128, 1024])
                for ch in range(32):
                    tgt = dma("sync", wst5[:, :],
                              w2T[128 * ch:128 * (ch + 1), :])
                    wait("sync", "wra", P.v("wra") + 1)
                    wait("vector", "ld", tgt)
                    op("vector", lambda v: v.tensor_scalar_mul(
                        t5a[:, :], wst5[:, :], s2c[:, :]))
                    inc("vector", lambda v: v.tensor_copy(
                        ti5[:, :], t5a[:, :]), "wra")
                    op("vector", lambda v: v.tensor_copy(
                        tf5[:, :], ti5[:, :]))
                    inc("vector", lambda v, ch=ch: v.tensor_scalar(
                        w2q[:, 1024 * ch:1024 * (ch + 1)], tf5[:, :],
                        -1.0, 1.0, ALU.max, ALU.min), "wrd")
                barrier()
                P.phase_end()
                wr_scope.close()
                h1qb = s42("h1qb", [128, 32 * FT], BF16)
                part4 = s42("part4", [128, 8 * FT])
                y4 = s42("y4", [128, 8 * FT])
                ty4 = s42("ty4", [128, FT])
                ty5 = s42("ty5", [128, FT])
                with nc.psum_tensor("f5m0", [128, FT], F32) as f5m0, \
                     nc.psum_tensor("f5m1", [128, FT], F32) as f5m1:
                    psn = (f5m0, f5m1)
                    for ft in range(NFT):
                        lh = dma("gpsimd", h1qb[:, :]
                                 .rearrange("p (c t) -> p c t", c=32),
                                 h1q_sp.rearrange("(c p) T -> p c T", p=128)
                                 [:, :, FT * ft:FT * (ft + 1)], semn="sp")
                        lp = dma("gpsimd", part4[:, :]
                                 .rearrange("p (c t) -> p c t", c=8),
                                 part_sp.rearrange("(c p) T -> p c T", p=128)
                                 [:, :, FT * ft:FT * (ft + 1)], semn="sp")
                        wait("tensor", "sp", lh)
                        for fc in range(8):
                            kf = ft * 8 + fc
                            if fc >= 2:
                                wait("tensor", "yev", kf - 1)
                            for hc in range(32):
                                op("tensor", lambda t, hc=hc, fc=fc:
                                   t.matmul(
                                       psn[fc % 2][:, :],
                                       w2q[:, 1024 * hc + 128 * fc:
                                           1024 * hc + 128 * (fc + 1)],
                                       h1qb[:, FT * hc:FT * (hc + 1)],
                                       start=(hc == 0), stop=(hc == 31)))
                            P.inc_prev("tensor", "ymm")
                            wait("vector", "ymm")
                            op("vector", lambda v, fc=fc, ft=ft: v.tensor_mul(
                                ty4[:, :], psn[fc % 2][:, :],
                                ry2sb[:, FT * ft:FT * (ft + 1)]))
                            inc("vector", lambda v: v.engine_nop(), "ya1")
                            wait("scalar", "ya1")
                            inc("scalar", lambda s, fc=fc: s.activation(
                                ty5[:, :], ty4[:, :], AF.Identity
                                if hasattr(AF, "Identity") else AF.Copy,
                                bias=b2c[:, fc:fc + 1]), "ya2")
                            wait("vector", "ya2")
                            wait("vector", "sp", lp)
                            inc("vector", lambda v, fc=fc: v.tensor_add(
                                y4[:, FT * fc:FT * (fc + 1)], ty5[:, :],
                                part4[:, FT * fc:FT * (fc + 1)]), "yev")
                        wait("gpsimd", "yev", ft * 8 + 8)
                        dma("gpsimd",
                            y_d.rearrange("(c p) T -> p c T", p=128)
                            [:, :, FT * ft:FT * (ft + 1)],
                            y4[:, :].rearrange("p (c t) -> p c t", c=8),
                            semn="st")
                        barrier()
        barrier()
        if _DBG:
            dbg_specs = [("d_ws", ws_dbg, [1, 8], F32),
                         ("d_rw", rw_dbg, [1, 4 * FT], F32),
                         ("d_w1q", w1q_dbg, [128, 8 * 4096], BF16),
                         ("d_mc", mc_dbg, [128, 4], F32),
                         ("d_xp", xp_dbg, [128, 48], F32),
                         ("d_h0q", h0q_dbg, [128, NFT * 4096], BF16),
                         ("d_kfm", kfm_sp, [128, NPT * 4096], BF16),
                         ("d_vtm", vtm_sp, [128, 16 * 1024], BF16),
                         ("d_gp", gp_sp, [64, 2 * 8192], F32),
                         ("d_h1q", h1q_sp, [HID, TLOC], BF16),
                         ("d_part", part_sp, [E, TLOC], F32)]
            for nm, srct, shp, dt in dbg_specs:
                od = nc.dram_tensor(nm, shp, dt, kind="ExternalOutput").ap()
                dma("gpsimd", od, srct, semn="st")
            wait("gpsimd", "st", P.v("st"))
            barrier()
        wait("gpsimd", "st", P.v("st"))
        nc._prog = P
        P.emit()
    return nc


_NC_CACHE = {}


def _get_nc():
    if "nc" not in _NC_CACHE:
        _NC_CACHE["nc"] = build_nc()
    return _NC_CACHE["nc"]


def kernel(**inp):
    x = np.ascontiguousarray(np.asarray(inp["L0_attn_x"], dtype=np.float32))
    Wq = np.asarray(inp["L0_attn_Wq"], np.float32)
    bq = np.asarray(inp["L0_attn_bq"], np.float32)
    Wk = np.asarray(inp["L0_attn_Wk"], np.float32)
    bk = np.asarray(inp["L0_attn_bk"], np.float32)
    Wv = np.asarray(inp["L0_attn_Wv"], np.float32)
    bv = np.asarray(inp["L0_attn_bv"], np.float32)
    Wg = np.asarray(inp["L0_attn_Wg"], np.float32)
    bg = np.asarray(inp["L0_attn_bg"], np.float32)
    gamma = np.asarray(inp["L0_attn_gamma"], np.float32)
    w1 = np.asarray(inp["L0_ffn_w1"], np.float32)
    b1 = np.asarray(inp["L0_ffn_b1"], np.float32)
    w2 = np.asarray(inp["L0_ffn_w2"], np.float32)
    b2 = np.asarray(inp["L0_ffn_b2"], np.float32)
    fgm = np.asarray(inp["L0_ffn_gamma"], np.float32)
    fbt = np.asarray(inp["L0_ffn_beta"], np.float32)

    WqT = np.ascontiguousarray(Wq.T)
    WkT = np.ascontiguousarray(Wk.T)
    WvT = np.ascontiguousarray(Wv.T)
    WgT = np.ascontiguousarray(Wg.T)
    w1T = np.ascontiguousarray(w1.T)
    w2T = np.ascontiguousarray(w2.T)
    brows = np.ascontiguousarray(np.stack([bq, bk, bv], axis=0))
    bgc = np.ascontiguousarray(bg.reshape(8, 128).T)
    b1c = np.ascontiguousarray(b1.reshape(32, 128).T)
    b2c = np.ascontiguousarray(b2.reshape(8, 128).T)
    gmc = np.ascontiguousarray(fgm.reshape(8, 128).T)
    btc = np.ascontiguousarray(fbt.reshape(8, 128).T)
    gam = gamma.reshape(1, H)

    inv = (1.0 / 10000.0 ** (np.arange(0, 64, 2, dtype=np.float64) / 64.0))
    in_maps = []
    for c in range(8):
        b, p = divmod(c, 4)
        sl = slice(2048 * p, 2048 * (p + 1))
        xTc = np.ascontiguousarray(x[b, sl, :].T)
        pos = np.arange(2048 * p, 2048 * (p + 1), dtype=np.float64)
        fr = pos[:, None] * inv[None, :]
        emb = np.concatenate([fr, fr], axis=1)
        cosn = np.cos(emb).astype(np.float32)
        sinn = np.sin(emb).astype(np.float32)
        kcos = np.ascontiguousarray(np.tile(cosn, (1, H))).astype(BF)
        ksin = np.ascontiguousarray(np.tile(sinn, (1, H))).astype(BF)
        qcos = np.ascontiguousarray(np.tile(cosn * 0.125, (1, H))).astype(BF)
        qsin = np.ascontiguousarray(np.tile(sinn * 0.125, (1, H))).astype(BF)
        nd4 = np.array([[-2048.0 * (p - j - 1) if j < p else -1e9
                         for j in range(4)]], np.float32)
        zm4 = np.array([[1.0 if j > p else 0.0 for j in range(4)]],
                       np.float32)
        in_maps.append({
            "xT": xTc, "WqT": WqT, "WkT": WkT, "WvT": WvT, "WgT": WgT,
            "w1T": w1T, "w2T": w2T, "brows": brows, "bgc": bgc,
            "b1c": b1c, "b2c": b2c, "gmc": gmc, "btc": btc, "gam": gam,
            "nd4": nd4, "zm4": zm4, "kcos": kcos, "ksin": ksin,
            "qcos": qcos, "qsin": qsin})
    nc = _get_nc()
    res = run_bass_kernel_spmd(nc, in_maps, core_ids=list(range(8)))
    out = np.empty((2, 8192, 1024), np.float32)
    for c in range(8):
        b, p = divmod(c, 4)
        out[b, 2048 * p:2048 * (p + 1), :] = res.results[c]["y"].T
    if _DBG:
        DBG_RES["res"] = res.results
    return out


# version 10
# speedup vs baseline: 1.0776x; 1.0185x over previous
import os
import numpy as np
import ml_dtypes

_P1LVL = int(os.environ.get("KP1LVL", "99"))
_P1SUB = int(os.environ.get("KP1SUB", "99"))
_DBG = int(os.environ.get("KDBG", "0"))
DBG_RES = {}

import concourse.bass as bass
import concourse.mybir as mybir
from concourse.bass_utils import run_bass_kernel_spmd

F32 = mybir.dt.float32
BF16 = mybir.dt.bfloat16
I32 = mybir.dt.int32
AF = mybir.ActivationFunctionType
ALU = mybir.AluOpType
AX = mybir.AxisListType
BF = ml_dtypes.bfloat16

E, H, HID = 1024, 16, 4096
TLOC = 2048
NPT, PT = 4, 512
NBLK, CB = 8, 256
NFT, FT = 4, 512
EPS = 1e-5
ENGINES = ("sync", "tensor", "scalar", "vector", "gpsimd")


def _consts():
    c = {}
    c["ones_row_f"] = np.ones((1, 128), np.float32)
    c["ones_row_b"] = np.ones((1, 128), BF)
    c["ones_col_f"] = np.ones((128, 1), np.float32)
    c["ones_col_b"] = np.ones((128, 1), BF)
    c["ident_f"] = np.eye(128, dtype=np.float32)
    c["ident_b"] = np.eye(128, dtype=BF)
    s = np.arange(128)[:, None]
    t = np.arange(CB)[None, :]
    c["dk"] = np.concatenate(
        [np.maximum(t - 128 * kc - s, 0) for kc in (0, 1)],
        axis=1).astype(np.float32)
    c["qdec"] = (np.arange(CB, dtype=np.float32)[None, :] + 1.0)
    j = np.arange(128, dtype=np.float32)[:, None]
    c["dcol"] = np.concatenate([255.0 - 128.0 * kc - j for kc in (0, 1)],
                               axis=1).astype(np.float32)
    c["epsc"] = np.full((128, 1), 1e-5, np.float32)
    c["l127c"] = np.full((128, 1), np.log(127.0), np.float32)
    return c


class Prog:

    PERSIST = ("ld", "sp", "bar", "st")

    def __init__(self, nc, stack):
        self.nc = nc
        self.stack = stack
        self.ops = []
        self.cnt = {}
        self.base = {}
        self.phys_of = {}
        self.phys = []
        self.phys_cnt = []
        self.free_phys = []
        self.barn = 0

    def sem(self, n):
        if n not in self.phys_of:
            if self.free_phys:
                idx = self.free_phys.pop()
            else:
                idx = len(self.phys)
                self.phys.append(
                    self.stack.enter_context(self.nc.semaphore(n)))
                self.phys_cnt.append(0)
            self.phys_of[n] = idx
            self.base[n] = self.phys_cnt[idx]
            self.cnt[n] = 0
        return self.phys[self.phys_of[n]]

    def phase_end(self):
        for n in list(self.phys_of):
            if n not in self.PERSIST:
                self.free_phys.append(self.phys_of[n])
                del self.phys_of[n], self.base[n], self.cnt[n]

    def v(self, n):
        self.sem(n)
        return self.cnt[n]

    def inc_prev(self, eng, semn, k=1):
        h = self.sem(semn)
        self.cnt[semn] += k
        self.phys_cnt[self.phys_of[semn]] += k
        for i in range(len(self.ops) - 1, -1, -1):
            e0, f0, m0 = self.ops[i]
            if e0 == eng:
                assert m0 is None or m0[0] == "inc", \
                    f"inc_prev would stack on {m0}"
                ev = ("inc", semn, k)
                meta = ev if m0 is None else (("multi",) +
                                              (m0[1:] if m0[0] == "multi"
                                               else (m0,)) + (ev,))
                self.ops[i] = (e0, lambda e, f0=f0, h=h, k=k:
                               f0(e).then_inc(h, k), meta)
                return self.cnt[semn]
        raise RuntimeError("no prior op for " + eng)

    def op(self, eng, fn, meta=None):
        self.ops.append((eng, fn, meta))

    def wait(self, eng, semn, val=None):
        h = self.sem(semn)
        v = self.cnt[semn] if val is None else val
        pv = self.base[semn] + v
        self.op(eng, lambda e, h=h, v=pv: e.wait_ge(h, v),
                ("wait", semn, v))

    def inc(self, eng, fn, semn, k=1):
        h = self.sem(semn)
        self.cnt[semn] += k
        self.phys_cnt[self.phys_of[semn]] += k
        self.op(eng, lambda e, fn=fn, h=h, k=k: fn(e).then_inc(h, k),
                ("inc", semn, k))
        return self.cnt[semn]

    def dma(self, eng, out, in_, semn="ld"):
        h = self.sem(semn)
        self.cnt[semn] += 16
        self.phys_cnt[self.phys_of[semn]] += 16
        self.op(eng, lambda e, out=out, in_=in_, h=h:
                e.dma_start(out=out, in_=in_).then_inc(h, 16),
                ("inc", semn, 16))
        return self.cnt[semn]

    def barrier(self):
        self.sem("bar")
        self.barn += 1
        tgt = 5 * self.barn
        ldv = self.cnt.get("ld", 0)
        spv = self.cnt.get("sp", 0)
        for e in ENGINES:
            if e == "sync" and "ld" in self.phys_of:
                self.wait(e, "ld", ldv)
            if e == "gpsimd" and "sp" in self.phys_of:
                self.wait(e, "sp", spv)
            h = self.sem("bar")
            self.op(e, lambda en, h=h: en.sem_inc(h, 1), ("inc", "bar", 1))
            self.op(e, lambda en, h=h, t=tgt: en.wait_ge(h, t),
                    ("wait", "bar", tgt))
        self.cnt["bar"] += 5
        self.phys_cnt[self.phys_of["bar"]] += 5

    def emit(self):
        with self.nc.Block() as block:
            for name in ENGINES:
                def mk(name):
                    def f(eng):
                        for e, fn, _m in self.ops:
                            if e == name:
                                fn(eng)
                    return f
                getattr(block, name)(mk(name))


def build_nc(upto=99):
    from contextlib import ExitStack
    nc = bass.Bass("TRN2", target_bir_lowering=False,
                   detect_race_conditions=False)
    dti = lambda n, s: nc.dram_tensor(n, s, F32, kind="ExternalInput").ap()
    xT = dti("xT", [E, TLOC])
    WqT, WkT = dti("WqT", [E, E]), dti("WkT", [E, E])
    WvT, WgT = dti("WvT", [E, E]), dti("WgT", [E, E])
    w1T, w2T = dti("w1T", [E, HID]), dti("w2T", [HID, E])
    brows = dti("brows", [3, E])
    bgc_d = dti("bgc", [128, 8])
    b1c_d, b2c_d = dti("b1c", [128, 32]), dti("b2c", [128, 8])
    gmc_d, btc_d = dti("gmc", [128, 8]), dti("btc", [128, 8])
    gam_d = dti("gam", [1, H])
    nd4_d, zm4_d = dti("nd4", [1, 4]), dti("zm4", [1, 4])
    dtb = lambda n, s: nc.dram_tensor(n, s, BF16, kind="ExternalInput").ap()
    kcos_d, ksin_d = dtb("kcos", [TLOC, E]), dtb("ksin", [TLOC, E])
    qcos_d, qsin_d = dtb("qcos", [TLOC, E]), dtb("qsin", [TLOC, E])
    y_d = nc.dram_tensor("y", [E, TLOC], F32, kind="ExternalOutput").ap()

    kfm_sp = nc.dram_tensor("kfm_sp", [128, NPT * 4096], BF16).ap()
    vtm_sp = nc.dram_tensor("vtm_sp", [128, 16 * 1024], BF16).ap()
    gp_sp = nc.dram_tensor("gp_sp", [64, 2 * 8192], F32).ap()
    h1q_sp = nc.dram_tensor("h1q_sp", [HID, TLOC], BF16).ap()
    part_sp = nc.dram_tensor("part_sp", [E, TLOC], F32).ap()
    ag_in = nc.dram_tensor("ag_in", [2048, 64], F32).ap()
    ag_out = nc.dram_tensor("ag_out", [8192, 64], F32).ap()
    g1024_sp = nc.dram_tensor("g1024_sp", [1, 1024], F32).ap()
    h0q_dbg = nc.dram_tensor("h0q_dbg", [128, NFT * 4096], BF16).ap()
    rw_dbg = nc.dram_tensor("rw_dbg", [1, 4 * FT], F32).ap()
    w1q_dbg = nc.dram_tensor("w1q_dbg", [128, 8 * 4096], BF16).ap()
    mc_dbg = nc.dram_tensor("mc_dbg", [128, 4], F32).ap()
    ws_dbg = nc.dram_tensor("ws_dbg", [1, 8], F32).ap()
    xp_dbg = nc.dram_tensor("xp_dbg", [128, 48], F32).ap()

    CN = _consts()
    inl = {k: nc.inline_tensor(v, k) for k, v in CN.items()}

    with ExitStack() as stk:
        P = Prog(nc, stk)
        sb = lambda n, shp, dt=F32: stk.enter_context(
            nc.sbuf_tensor(n, shp, dt))
        op, wait, inc, dma, barrier = P.op, P.wait, P.inc, P.dma, P.barrier

        xnb = sb("xnb", [128, 8 * TLOC], BF16)
        WTt = sb("WTt", [128, H * 512], BF16)
        Bq = sb("Bq", [128, (H // 2) * CB], BF16)
        cons = {}
        for nm, shp, dt in [("ones_row_f", [1, 128], F32),
                            ("ones_row_b", [1, 128], BF16),
                            ("ones_col_f", [128, 1], F32),
                            ("ones_col_b", [128, 1], BF16),
                            ("ident_f", [128, 128], F32),
                            ("ident_b", [128, 128], BF16),
                            ("dcol", [128, 2], F32),
                            ("epsc", [128, 1], F32),
                            ("l127c", [128, 1], F32)]:
            cons[nm] = sb("c_" + nm, shp, dt)
        bqr = sb("bqr", [1, E], BF16)
        bkr = sb("bkr", [1, E], BF16)
        bvr = sb("bvr", [1, E], BF16)
        bgc = sb("bgc_s", [128, 8])
        b1c, b2c = sb("b1c_s", [128, 32]), sb("b2c_s", [128, 8])
        gmc, btc = sb("gmc_s", [128, 8]), sb("btc_s", [128, 8])
        gam, gsig = sb("gam_s", [1, H]), sb("gsig", [1, H])
        nd4, zm4 = sb("nd4_s", [1, 4]), sb("zm4_s", [1, 4])
        gncol = sb("gncol", [128, H])
        zcol = sb("zcol", [128, 4])
        g1024 = sb("g1024", [1, 1024])
        grows = sb("grows", [16, 64])
        dcole = sb("dcole", [128, 2 * H])
        m1rc, m2rc = sb("m1rc", [128, 1]), sb("m2rc", [128, 1])
        tcol = sb("tcol", [128, 1])
        dbgc = sb("dbgc", [128, 4]) if _DBG else None
        s1c, s2c = sb("s1c", [128, 1]), sb("s2c", [128, 1])

        from contextlib import ExitStack as ES0
        p0scope = ES0()
        for nm, shp, dt in [("dk", [128, 512], F32), ("qdec", [1, CB], F32)]:
            cons[nm] = p0scope.enter_context(
                nc.sbuf_tensor("c_" + nm, shp, dt))
        btmpf = p0scope.enter_context(nc.sbuf_tensor("btmpf", [1, E], F32))
        for nm in cons:
            dma("sync", cons[nm][:, :], inl[nm].ap())
        for i, row in enumerate([bqr, bkr, bvr]):
            tgt = dma("sync", btmpf[:, :], brows[i:i + 1, :])
            wait("vector", "ld", tgt)
            inc("vector", lambda v, row=row: v.tensor_copy(
                row[:, :], btmpf[:, :]), "p0v")
            wait("sync", "p0v")
        dma("sync", bgc[:, :], bgc_d)
        dma("sync", b1c[:, :], b1c_d)
        dma("sync", b2c[:, :], b2c_d)
        dma("sync", gmc[:, :], gmc_d)
        dma("sync", btc[:, :], btc_d)
        dma("sync", gam[:, :], gam_d)
        dma("sync", nd4[:, :], nd4_d)
        dma("sync", zm4[:, :], zm4_d)
        p0ld = P.v("ld")
        wait("scalar", "ld", p0ld)
        inc("scalar", lambda s: s.activation(gsig[:, :], gam[:, :],
                                             AF.Sigmoid), "p0s")
        with nc.psum_tensor("p0a", [128, 64], F32) as p0a, \
             nc.psum_tensor("p0b", [16, 64], F32) as p0b:
            wait("tensor", "ld", p0ld)
            wait("tensor", "p0s")
            op("tensor", lambda t: t.matmul(
                p0a[:, 0:H], cons["ones_row_f"][:, :], gsig[:, :],
                start=True, stop=True))
            op("tensor", lambda t: t.matmul(
                p0a[:, 32:36], cons["ones_row_f"][:, :], zm4[:, :],
                start=True, stop=True))
            inc("tensor", lambda t: t.matmul(
                p0b[:, :], gsig[:, :], cons["ones_row_f"][0:1, 0:64],
                start=True, stop=True), "p0t")
            wait("vector", "p0t")
            op("vector", lambda v: v.tensor_scalar_mul(
                gncol[:, :], p0a[:, 0:H], -1.0))
            op("vector", lambda v: v.tensor_copy(zcol[:, :], p0a[:, 32:36]))
            inc("vector", lambda v: v.tensor_copy(grows[:, :], p0b[:, :]),
                "p0v")
        wait("gpsimd", "p0v")
        dma("gpsimd", g1024_sp.rearrange("x (h d) -> (x h) d", h=16),
            grows[:, :], semn="sp")
        dma("gpsimd", g1024[:, :], g1024_sp, semn="sp")
        barrier()
        for h in range(H):
            inc("scalar", lambda s, h=h: s.activation(
                WTt[:, h * 512:(h + 1) * 512], cons["dk"][:, :], AF.Exp,
                scale=gncol[:, h:h + 1]), "p0s")
            inc("scalar", lambda s, h=h: s.activation(
                dcole[:, 2 * h:2 * h + 2], cons["dcol"][:, :], AF.Exp,
                scale=gncol[:, h:h + 1]), "p0s")
        with nc.sbuf_tensor("brt", [1, H * CB], BF16) as brt, \
             nc.psum_tensor("pbq", [128, CB], F32) as pbq:
            for h in range(H):
                inc("scalar", lambda s, h=h: s.activation(
                    brt[:, h * CB:(h + 1) * CB], cons["qdec"][:, :], AF.Exp,
                    scale=gncol[0:1, h:h + 1]), "p0s")
            wait("tensor", "p0s")
            for pi in range(H // 2):
                if pi:
                    wait("tensor", "p0v")
                op("tensor", lambda t, pi=pi: t.matmul(
                    pbq[0:64, :], cons["ones_row_b"][0:1, 0:64],
                    brt[:, (2 * pi) * CB:(2 * pi + 1) * CB],
                    start=True, stop=True))
                inc("tensor", lambda t, pi=pi: t.matmul(
                    pbq[64:128, :], cons["ones_row_b"][0:1, 0:64],
                    brt[:, (2 * pi + 1) * CB:(2 * pi + 2) * CB],
                    start=True, stop=True), "p0t")
                wait("vector", "p0t")
                inc("vector", lambda v, pi=pi: v.tensor_copy(
                    Bq[:, pi * CB:(pi + 1) * CB], pbq[:, :]), "p0v")
            wait("tensor", "p0v")
        barrier()
        p0scope.close()
        P.phase_end()
        if upto < 1:
            nc._prog = P
            P.emit()
            return nc

        from contextlib import ExitStack as ES
        with ES() as p1:
            s1 = lambda n, shp, dt=F32: p1.enter_context(
                nc.sbuf_tensor(n, shp, dt))
            x_sb = s1("x_sb", [128, 8 * PT])
            xsq = s1("xsq", [128, 2 * PT])
            Wk_b = s1("Wk_b", [128, 8 * 1024], BF16)
            Wv_b = s1("Wv_b", [128, 8 * 1024], BF16)
            wstg = s1("wstg", [128, 2 * 1024])
            rowt = s1("rowt", [1, 3 * PT])
            mub, rsb = s1("mub", [128, PT]), s1("rsb", [128, PT])
            tA = s1("tA", [128, PT])
            ktm = s1("ktm", [128, 2 * 1024], BF16)
            ksw = s1("ksw", [128, 1024], BF16)
            kr1 = s1("kr1", [128, 1024], BF16)
            krot = s1("krot", [128, 2 * 1024], BF16)
            ktil = s1("ktil", [128, 2 * 1024], BF16)
            kstg = s1("kstg", [128, 4096], BF16)
            vstg = s1("vstg", [128, 4096], BF16)
            gpev = s1("gpev", [64, 2048])
            ctab = s1("ctab", [128, 2 * 1024], BF16)
            stab = s1("stab", [128, 2 * 1024], BF16)
            nw1 = 0
            for wd, wb in ((WkT, Wk_b), (WvT, Wv_b)):
                for ec in range(8):
                    if nw1 >= 2:
                        wait("sync", "p1w", nw1 - 1)
                    tgt = dma("sync", wstg[:, (nw1 % 2) * 1024:
                                           (nw1 % 2 + 1) * 1024],
                              wd[128 * ec:128 * (ec + 1), :])
                    wait("scalar", "ld", tgt)
                    inc("scalar", lambda s, wb=wb, ec=ec, nw1=nw1: s.copy(
                        wb[:, 1024 * ec:1024 * (ec + 1)],
                        wstg[:, (nw1 % 2) * 1024:(nw1 % 2 + 1) * 1024]),
                        "p1w")
                    nw1 += 1
            barrier()
            with nc.psum_tensor("ps_sum", [1, PT], F32) as ps_sum, \
                 nc.psum_tensor("ps_sq", [1, PT], F32) as ps_sq, \
                 nc.psum_tensor("ps_k0", [128, 512], F32) as ps_k0, \
                 nc.psum_tensor("ps_k1", [128, 512], F32) as ps_k1, \
                 nc.psum_tensor("ps_t0", [128, 128], BF16) as ps_t0, \
                 nc.psum_tensor("ps_t1", [128, 128], BF16) as ps_t1, \
                 nc.psum_tensor("ps_gp", [64, 1024], F32) as ps_gp:
                ps_t = (ps_t0, ps_t1)
                ps_bc = ps_k0
                gp_dma_tgt = [0]
                psp_tgt = [0]
                for pt in range(NPT):
                    if _P1LVL < 1:
                        barrier()
                        continue
                    if pt >= 1:
                        wait("sync", "xnd", pt)
                    xt = dma("sync",
                             x_sb[:, :].rearrange("p (c t) -> p c t", c=8),
                             xT[:, PT * pt:PT * (pt + 1)]
                             .rearrange("(c p) t -> p c t", p=128))
                    wait("scalar", "ld", xt)
                    for fc in range(8):
                        k = pt * 8 + fc
                        if k >= 2 and _P1SUB >= 2:
                            wait("scalar", "sqm", k - 1)
                        inc("scalar", lambda s, fc=fc: s.activation(
                            xsq[:, (fc % 2) * PT:(fc % 2 + 1) * PT],
                            x_sb[:, PT * fc:PT * (fc + 1)], AF.Square), "sqa")
                    if _P1SUB < 2:
                        barrier()
                        continue
                    wait("tensor", "ld", xt)
                    if pt >= 1:
                        wait("tensor", "rw1", pt)
                    for fc in range(8):
                        op("tensor", lambda t, fc=fc: t.matmul(
                            ps_sum[:, :], cons["ones_col_f"][:, :],
                            x_sb[:, PT * fc:PT * (fc + 1)],
                            start=(fc == 0), stop=(fc == 7)))
                    for fc in range(8):
                        wait("tensor", "sqa", pt * 8 + fc + 1)
                        inc("tensor", lambda t, fc=fc: t.matmul(
                            ps_sq[:, :], cons["ones_col_f"][:, :],
                            xsq[:, (fc % 2) * PT:(fc % 2 + 1) * PT],
                            start=(fc == 0), stop=(fc == 7)), "sqm")
                    if _P1SUB < 3:
                        barrier()
                        continue
                    r0 = rowt[0:1, 0:PT]
                    r1 = rowt[0:1, PT:2 * PT]
                    r2 = rowt[0:1, 2 * PT:3 * PT]
                    wait("vector", "sqm", 8 * (pt + 1))
                    if pt >= 1:
                        wait("vector", "rw3", pt)
                        wait("vector", "bcd", 2 * pt)
                    op("vector", lambda v: v.tensor_scalar_mul(
                        r0, ps_sum[:, :], 1.0 / 1024.0))
                    if _P1SUB == 25:
                        inc("vector", lambda v: v.engine_nop(), "rw1")
                        barrier()
                        continue
                    op("vector", lambda v: v.tensor_scalar_mul(
                        r1, ps_sq[:, :], 1.0 / 1024.0))
                    if _P1SUB == 26:
                        inc("vector", lambda v: v.engine_nop(), "rw1")
                        barrier()
                        continue
                    op("vector", lambda v: v.tensor_mul(r2, r0, r0))
                    inc("vector", lambda v: v.tensor_sub(r1, r1, r2), "rw1")
                    if _P1SUB == 27:
                        barrier()
                        continue
                    wait("scalar", "rw1")
                    op("scalar", lambda s: s.activation(
                        r2, r1, AF.Ln, bias=cons["epsc"][0:1, :]))
                    inc("scalar", lambda s: s.activation(r1, r2, AF.Exp,
                                                         scale=-0.5), "rw3")
                    if _P1SUB < 4:
                        barrier()
                        continue
                    wait("tensor", "rw3")
                    if pt >= 1:
                        wait("tensor", "kve", 8 * pt)
                    inc("tensor", lambda t: t.matmul(
                        ps_bc[:, :], cons["ones_row_f"][:, :], r0,
                        start=True, stop=True), "bcd")
                    wait("vector", "bcd")
                    inc("vector", lambda v: v.tensor_copy(
                        mub[:, :], ps_bc[:, :]), "bce")
                    wait("tensor", "bce")
                    inc("tensor", lambda t: t.matmul(
                        ps_bc[:, :], cons["ones_row_f"][:, :], r1,
                        start=True, stop=True), "bcd")
                    wait("vector", "bcd")
                    op("vector", lambda v: v.tensor_copy(
                        rsb[:, :], ps_bc[:, :]))
                    if _P1SUB < 5:
                        inc("vector", lambda v: v.engine_nop(), "xnd")
                        barrier()
                        continue
                    for fc in range(8):
                        op("vector", lambda v, fc=fc: v.tensor_sub(
                            tA[:, :], x_sb[:, PT * fc:PT * (fc + 1)],
                            mub[:, :]))
                        op("vector", lambda v, fc=fc, pt=pt: v.tensor_mul(
                            xnb[:, 2048 * fc + PT * pt:
                                2048 * fc + PT * (pt + 1)],
                            tA[:, :], rsb[:, :]))
                    inc("vector", lambda v: v.engine_nop(), "xnd")

                    for tc in range(4):
                        if _P1LVL < 2:
                            continue
                        tg = pt * 4 + tc
                        kc = tc % 2
                        if tg >= 2:
                            wait("sync", "krd", tg - 1)
                        t_c = dma("sync",
                                  ctab[:, kc * 1024:(kc + 1) * 1024],
                                  kcos_d[PT * pt + 128 * tc:
                                         PT * pt + 128 * (tc + 1), :])
                        t_s = dma("sync",
                                  stab[:, kc * 1024:(kc + 1) * 1024],
                                  ksin_d[PT * pt + 128 * tc:
                                         PT * pt + 128 * (tc + 1), :])
                        wait("tensor", "xnd", pt + 1)
                        if tg >= 1:
                            wait("tensor", "kve", 2 * tg)
                        for nh, pk in ((0, ps_k0), (1, ps_k1)):
                            for ec in range(8):
                                op("tensor", lambda t, ec=ec, nh=nh, pk=pk,
                                   pt=pt, tc=tc: t.matmul(
                                       pk[:, :],
                                       xnb[:, 2048 * ec + PT * pt + 128 * tc:
                                           2048 * ec + PT * pt
                                           + 128 * (tc + 1)],
                                       Wk_b[:, 1024 * ec + 512 * nh:
                                            1024 * ec + 512 * (nh + 1)],
                                       start=(ec == 0), stop=False))
                            op("tensor", lambda t, nh=nh, pk=pk: t.matmul(
                                pk[:, :], cons["ones_row_b"][:, :],
                                bkr[0:1, 512 * nh:512 * (nh + 1)],
                                start=False, stop=True))
                        P.inc_prev("tensor", "kmm")
                        wait("scalar", "kmm")
                        op("scalar", lambda s, kc=kc: s.copy(
                            ktm[:, kc * 1024:kc * 1024 + 512], ps_k0[:, :]))
                        inc("scalar", lambda s, kc=kc: s.copy(
                            ktm[:, kc * 1024 + 512:(kc + 1) * 1024],
                            ps_k1[:, :]), "kve")
                        if _P1LVL < 3:
                            continue
                        wait("gpsimd", "kve")
                        if tg >= 1:
                            wait("gpsimd", "krd", tg)
                        op("gpsimd", lambda e, kc=kc: e.tensor_scalar_mul(
                            ksw[:, 0:1024:2],
                            ktm[:, kc * 1024 + 1:(kc + 1) * 1024:2], -1.0))
                        inc("gpsimd", lambda e, kc=kc: e.tensor_copy(
                            ksw[:, 1:1024:2],
                            ktm[:, kc * 1024:(kc + 1) * 1024:2]), "ksd")
                        wait("vector", "ksd")
                        wait("vector", "ld", max(t_c, t_s))
                        if tg >= 2:
                            wait("vector", "pmm", tg // 2)
                        op("vector", lambda v, kc=kc: v.tensor_mul(
                            krot[:, kc * 1024:(kc + 1) * 1024],
                            ktm[:, kc * 1024:(kc + 1) * 1024],
                            ctab[:, kc * 1024:(kc + 1) * 1024]))
                        op("vector", lambda v, kc=kc: v.tensor_mul(
                            kr1[:, :], ksw[:, :],
                            stab[:, kc * 1024:(kc + 1) * 1024]))
                        inc("vector", lambda v, kc=kc: v.tensor_add(
                            krot[:, kc * 1024:(kc + 1) * 1024],
                            krot[:, kc * 1024:(kc + 1) * 1024],
                            kr1[:, :]), "krd")
                        if _P1LVL < 4:
                            continue
                        wait("tensor", "krd")
                        if tc == 0 and pt >= 1:
                            wait("vector", "sp", psp_tgt[0])
                        for jj in range(8):
                            jg = tg * 8 + jj
                            if jg >= 2:
                                wait("tensor", "kte", jg - 1)
                            inc("tensor", lambda t, jj=jj, kc=kc:
                                t.transpose(
                                    ps_t[jj % 2][:, :],
                                    krot[:, kc * 1024 + 128 * jj:
                                         kc * 1024 + 128 * (jj + 1)],
                                    cons["ident_b"][:, :]), "ktt")
                            wait("vector", "ktt", tg * 8 + jj + 1)
                            inc("vector", lambda v, tc=tc, jj=jj:
                                v.tensor_copy(
                                    kstg[:, 1024 * tc + 128 * jj:
                                         1024 * tc + 128 * (jj + 1)],
                                    ps_t[jj % 2][:, :]), "kte")
                        if _P1LVL < 5:
                            continue
                        wait("tensor", "kve", 2 * tg + 1)
                        for nh, pk in ((0, ps_k0), (1, ps_k1)):
                            for ec in range(8):
                                op("tensor", lambda t, ec=ec, nh=nh, pk=pk,
                                   pt=pt, tc=tc: t.matmul(
                                       pk[:, :],
                                       xnb[:, 2048 * ec + PT * pt + 128 * tc:
                                           2048 * ec + PT * pt
                                           + 128 * (tc + 1)],
                                       Wv_b[:, 1024 * ec + 512 * nh:
                                            1024 * ec + 512 * (nh + 1)],
                                       start=(ec == 0), stop=False))
                            op("tensor", lambda t, nh=nh, pk=pk: t.matmul(
                                pk[:, :], cons["ones_row_b"][:, :],
                                bvr[0:1, 512 * nh:512 * (nh + 1)],
                                start=False, stop=True))
                        P.inc_prev("tensor", "vmm")
                        wait("scalar", "vmm")
                        if tc == 0 and pt >= 1:
                            wait("scalar", "sp", psp_tgt[0])
                        op("scalar", lambda s, tc=tc: s.copy(
                            vstg[:, 1024 * tc:1024 * tc + 512], ps_k0[:, :]))
                        inc("scalar", lambda s, tc=tc: s.copy(
                            vstg[:, 1024 * tc + 512:1024 * (tc + 1)],
                            ps_k1[:, :]), "kve")
                        if _P1LVL < 6:
                            continue
                        if tg >= 2:
                            wait("vector", "gsm", tg // 2)
                        for h in range(H):
                            op("vector", lambda v, h=h, kc=kc:
                               v.tensor_scalar_mul(
                                   ktil[:, kc * 1024 + 64 * h:
                                        kc * 1024 + 64 * (h + 1)],
                                   krot[:, kc * 1024 + 64 * h:
                                        kc * 1024 + 64 * (h + 1)],
                                   dcole[:, 2 * h + kc:2 * h + kc + 1]))
                        inc("vector", lambda v: v.engine_nop(), "ktl")
                        if kc == 1:
                            sbk = pt * 2 + tc // 2
                            wait("tensor", "ktl")
                            wait("tensor", "kve", 2 * tg + 2)
                            for h in range(H):
                                for k2 in (0, 1):
                                    op("tensor", lambda t, h=h, k2=k2,
                                       tc=tc: t.matmul(
                                           ps_gp[:, 64 * h:64 * (h + 1)],
                                           ktil[:, k2 * 1024 + 64 * h:
                                                k2 * 1024 + 64 * (h + 1)],
                                           vstg[:, 1024 * (tc - 1 + k2)
                                                + 64 * h:
                                                1024 * (tc - 1 + k2)
                                                + 64 * (h + 1)],
                                           start=(k2 == 0),
                                           stop=(k2 == 1)))
                            P.inc_prev("tensor", "gsm")
                            wait("vector", "gsm")
                            wait("vector", "sp", gp_dma_tgt[0])
                            inc("vector", lambda v: v.tensor_copy(
                                gpev[:, 0:1024], ps_gp[:, :]), "gp1")
                            wait("tensor", "gp1")
                            for h in range(H):
                                for k2 in (0, 1):
                                    op("tensor", lambda t, h=h, k2=k2,
                                       tc=tc: t.matmul(
                                           ps_gp[:, 64 * h:64 * (h + 1)],
                                           krot[:, k2 * 1024 + 64 * h:
                                                k2 * 1024 + 64 * (h + 1)],
                                           vstg[:, 1024 * (tc - 1 + k2)
                                                + 64 * h:
                                                1024 * (tc - 1 + k2)
                                                + 64 * (h + 1)],
                                           start=(k2 == 0),
                                           stop=(k2 == 1)))
                            P.inc_prev("tensor", "pmm")
                            wait("vector", "pmm")
                            inc("vector", lambda v: v.tensor_copy(
                                gpev[:, 1024:2048], ps_gp[:, :]), "gpe")
                            wait("gpsimd", "gpe")
                            dma("gpsimd",
                                gp_sp[:, 1024 * sbk:1024 * (sbk + 1)],
                                gpev[:, 0:1024], semn="sp")
                            dma("gpsimd",
                                gp_sp[:, 8192 + 1024 * sbk:
                                      8192 + 1024 * (sbk + 1)],
                                gpev[:, 1024:2048], semn="sp")
                            gp_dma_tgt[0] = P.v("sp")
                    if _P1LVL >= 7:
                        wait("gpsimd", "kte", 32 * (pt + 1))
                        dma("gpsimd", kfm_sp[:, 4096 * pt:4096 * (pt + 1)],
                            kstg[:, :], semn="sp")
                        wait("gpsimd", "kve", 8 * (pt + 1))
                        dma("gpsimd", vtm_sp[:, 4096 * pt:4096 * (pt + 1)],
                            vstg[:, :], semn="sp")
                        psp_tgt[0] = P.v("sp")
        barrier()
        P.phase_end()
        if upto < 2:
            nc._prog = P
            P.emit()
            return nc

        p23 = stk.enter_context(ES())
        Rb16 = p23.enter_context(
            nc.sbuf_tensor("Rb16", [128, NBLK * 1024], BF16))
        Zb16 = p23.enter_context(
            nc.sbuf_tensor("Zb16", [128, NBLK * 1024], BF16))
        with ES() as p2:
            s2 = lambda n, shp, dt=F32: p2.enter_context(
                nc.sbuf_tensor(n, shp, dt))
            GtPt = s2("GtPt", [128, 8192])
            RpZp = s2("RpZp", [128, 8192])
            gC = s2("gC", [64, 1024])
            rowe = s2("rowe", [1, 1024])
            gat = s2("gat", [128, 4096])
            RacZac = s2("RacZac", [128, 1024])
            tt1 = s2("tt1", [128, 1024])
            wst2 = s2("wst2", [128, 1024])
            wabs = s2("wabs", [128, 1024])
            Gt = lambda a, b: GtPt[0:64, a:b]
            Pt = lambda a, b: GtPt[64:128, a:b]
            Rp = lambda a, b: RpZp[0:64, a:b]
            Zp = lambda a, b: RpZp[64:128, a:b]
            with nc.psum_tensor("ps_r", [64, 1024], F32) as ps_r, \
                 nc.psum_tensor("ps_w", [64, 1024], F32) as ps_w, \
                 nc.psum_tensor("ps_s2", [128, 1], F32) as ps_s2, \
                 nc.psum_tensor("ps_ws", [1, 512], F32) as ps_ws, \
                 nc.psum_tensor("ps_ws2", [1, 512], F32) as ps_ws2:
                t1 = dma("gpsimd", GtPt[0:64, :], gp_sp[:, 0:8192],
                         semn="sp")
                t2 = dma("gpsimd", GtPt[64:128, :], gp_sp[:, 8192:16384],
                         semn="sp")
                inc("scalar", lambda s: s.activation(
                    rowe[:, :], g1024[:, :], AF.Exp, scale=-256.0), "q2s")
                wait("tensor", "q2s")
                op("tensor", lambda t: t.matmul(
                    ps_r[:, 0:512], cons["ones_row_f"][0:1, 0:64],
                    rowe[0:1, 0:512], start=True, stop=True))
                inc("tensor", lambda t: t.matmul(
                    ps_r[:, 512:1024], cons["ones_row_f"][0:1, 0:64],
                    rowe[0:1, 512:1024], start=True, stop=True), "q2t")
                wait("vector", "q2t")
                inc("vector", lambda v: v.tensor_copy(gC[:, :], ps_r[:, :]),
                    "q2v")
                wait("vector", "sp", max(t1, t2))
                op("vector", lambda v: v.memset(Rp(0, 1024), 0.0))
                op("vector", lambda v: v.memset(Zp(7 * 1024, 8192), 0.0))
                for b in range(1, NBLK):
                    op("vector", lambda v, b=b: v.tensor_mul(
                        tt1[0:64, :], Rp(1024 * (b - 1), 1024 * b),
                        gC[:, :]))
                    op("vector", lambda v, b=b: v.tensor_add(
                        Rp(1024 * b, 1024 * (b + 1)), tt1[0:64, :],
                        Gt(1024 * (b - 1), 1024 * b)))
                for b in range(NBLK - 2, -1, -1):
                    op("vector", lambda v, b=b: v.tensor_add(
                        Zp(1024 * b, 1024 * (b + 1)),
                        Zp(1024 * (b + 1), 1024 * (b + 2)),
                        Pt(1024 * (b + 1), 1024 * (b + 2))))
                op("vector", lambda v: v.tensor_mul(
                    tt1[0:64, :], Rp(7 * 1024, 8192), gC[:, :]))
                op("vector", lambda v: v.tensor_add(
                    tt1[0:64, :], tt1[0:64, :], Gt(7 * 1024, 8192)))
                inc("vector", lambda v: v.tensor_add(
                    tt1[64:128, :], Zp(0, 1024), Pt(0, 1024)), "q2v")
                wait("gpsimd", "q2v")
                dma("gpsimd",
                    ag_in.rearrange("(h r) d -> h r d", r=128)[:, 0:64, :]
                    .transpose([1, 0, 2]),
                    tt1[0:64, :].rearrange("i (h d) -> i h d", h=16),
                    semn="sp")
                t3 = dma("gpsimd",
                         ag_in.rearrange("(h r) d -> h r d",
                                         r=128)[:, 64:128, :]
                         .transpose([1, 0, 2]),
                         tt1[64:128, :].rearrange("i (h d) -> i h d", h=16),
                         semn="sp")
                wait("gpsimd", "sp", t3)
                inc("gpsimd", lambda e: e.collective_compute(
                    "AllGather", ALU.bypass,
                    replica_groups=[[0, 1, 2, 3], [4, 5, 6, 7]],
                    ins=[ag_in], outs=[ag_out]), "agd")
                wait("gpsimd", "agd")
                t4 = dma("gpsimd",
                         gat[:, :].rearrange("p (j h d) -> p j h d",
                                             j=4, h=16),
                         ag_out.rearrange("(j h two i) d -> (two i) j h d",
                                          j=4, h=16, two=2), semn="sp")

                wait("tensor", "q2t", 1)
                nw = 0
                for wi, wd, nchunk, cols, mc, sc in (
                        (0, w1T, 8, HID, m1rc, s1c),
                        (1, w2T, 32, E, m2rc, s2c)):
                    psw = ps_ws if wi == 0 else ps_ws2
                    npc = nchunk * (cols // 1024)
                    k0 = 0
                    for ch in range(nchunk):
                        for pc in range(cols // 1024):
                            tgt = dma("sync", wst2[:, :],
                                      wd[128 * ch:128 * (ch + 1),
                                         1024 * pc:1024 * (pc + 1)])
                            wait("scalar", "ld", tgt)
                            inc("scalar", lambda s: s.activation(
                                wabs[:, :], wst2[:, :], AF.Abs), "wsa")
                            wait("sync", "wsa")
                            wait("tensor", "wsa", P.v("wsa"))
                            for nh in range(2):
                                op("tensor", lambda t, nh=nh, k0=k0,
                                   npc=npc, psw=psw: t.matmul(
                                       psw[:, 0:512],
                                       cons["ones_col_f"][:, :],
                                       wabs[:, 512 * nh:512 * (nh + 1)],
                                       start=(k0 == 0 and nh == 0),
                                       stop=(k0 == npc - 1 and nh == 1)))
                            P.inc_prev("tensor", "wsm")
                            wait("scalar", "wsm", P.v("wsm") + 0)
                            k0 += 1
                            nw += 1
                    wait("vector", "wsm", P.v("wsm"))
                    inc("vector", lambda v, psw=psw: v.tensor_reduce(
                        rowe[0:1, 0:1], psw[0:1, 0:512], AX.X, ALU.add),
                        "wm1")
                    wait("vector", "wm1", P.v("wm1"))
                    op("vector", lambda v: v.tensor_scalar(
                        rowe[0:1, 1:2], rowe[0:1, 0:1], 1.0 / (E * HID),
                        EPS, ALU.mult, ALU.max))
                    inc("vector", lambda v: v.engine_nop(), "wm2")
                    if _DBG:
                        wait("gpsimd", "wm2", P.v("wm2"))
                        dma("gpsimd", ws_dbg[:, 4 * wi:4 * wi + 2],
                            rowe[0:1, 0:2], semn="sp")
                        wait("vector", "sp", P.v("sp"))
                    wait("tensor", "wm2", P.v("wm2"))
                    inc("tensor", lambda t: t.matmul(
                        ps_s2[:, :], cons["ones_row_f"][:, :],
                        rowe[0:1, 1:2], start=True, stop=True), "wm3")
                    wait("vector", "wm3", P.v("wm3"))
                    inc("vector", lambda v, mc=mc: v.tensor_copy(
                        mc[:, :], ps_s2[:, :]), "wm4")
                    wait("scalar", "wm4", P.v("wm4"))
                    op("scalar", lambda s, mc=mc: s.activation(
                        tcol[:, :], mc[:, :], AF.Ln))
                    inc("scalar", lambda s, sc=sc: s.activation(
                        sc[:, :], tcol[:, :], AF.Exp, scale=-1.0), "wm5")
                    wait("tensor", "wm2", P.v("wm2"))
                wait("tensor", "wm3", P.v("wm3"))
                wait("vector", "wm5", P.v("wm5"))

                wait("vector", "sp", t4)
                op("vector", lambda v: v.memset(RacZac[:, :], 0.0))
                wait("scalar", "wm3", P.v("wm3"))
                for j in range(4):
                    inc("scalar", lambda s, j=j: s.activation(
                        rowe[:, :], g1024[:, :], AF.Exp,
                        scale=nd4[0:1, j:j + 1]), "bd1")
                    wait("tensor", "bd1")
                    op("tensor", lambda t: t.matmul(
                        ps_w[:, 0:512], cons["ones_row_f"][0:1, 0:64],
                        rowe[0:1, 0:512], start=True, stop=True))
                    inc("tensor", lambda t: t.matmul(
                        ps_w[:, 512:1024], cons["ones_row_f"][0:1, 0:64],
                        rowe[0:1, 512:1024], start=True, stop=True), "bd2")
                    wait("vector", "bd2")
                    op("vector", lambda v, j=j: v.tensor_mul(
                        tt1[0:64, :], gat[0:64, 1024 * j:1024 * (j + 1)],
                        ps_w[:, :]))
                    op("vector", lambda v: v.tensor_add(
                        RacZac[0:64, :], RacZac[0:64, :], tt1[0:64, :]))
                    inc("vector", lambda v, j=j: v.scalar_tensor_tensor(
                        RacZac[64:128, :],
                        gat[64:128, 1024 * j:1024 * (j + 1)],
                        zcol[64:128, j:j + 1], RacZac[64:128, :],
                        ALU.mult, ALU.add), "bd3")
                    wait("tensor", "bd3")
                    wait("scalar", "bd2", P.v("bd2"))
                for b in range(NBLK):
                    inc("scalar", lambda s, b=b: s.activation(
                        rowe[:, :], g1024[:, :], AF.Exp, scale=-256.0 * b),
                        "bd4")
                    wait("tensor", "bd4")
                    op("tensor", lambda t: t.matmul(
                        ps_w[:, 0:512], cons["ones_row_f"][0:1, 0:64],
                        rowe[0:1, 0:512], start=True, stop=True))
                    inc("tensor", lambda t: t.matmul(
                        ps_w[:, 512:1024], cons["ones_row_f"][0:1, 0:64],
                        rowe[0:1, 512:1024], start=True, stop=True), "bd5")
                    wait("vector", "bd5")
                    op("vector", lambda v: v.tensor_mul(
                        tt1[0:64, :], RacZac[0:64, :], ps_w[:, :]))
                    op("vector", lambda v, b=b: v.tensor_add(
                        Rb16[0:64, 1024 * b:1024 * (b + 1)], tt1[0:64, :],
                        Rp(1024 * b, 1024 * (b + 1))))
                    inc("vector", lambda v, b=b: v.tensor_add(
                        Zb16[64:128, 1024 * b:1024 * (b + 1)],
                        Zp(1024 * b, 1024 * (b + 1)),
                        RacZac[64:128, :]), "bd6")
                    wait("tensor", "bd6")
                    wait("scalar", "bd5", P.v("bd5"))
                wait("gpsimd", "bd6", P.v("bd6"))
                dma("gpsimd", Rb16[64:128, :], Rb16[0:64, :], semn="sp")
                dma("gpsimd", Zb16[0:64, :], Zb16[64:128, :], semn="sp")
            barrier()
        barrier()
        P.phase_end()
        if upto < 3:
            nc._prog = P
            P.emit()
            return nc

        with ES() as p3:
            s3 = lambda n, shp, dt=F32: p3.enter_context(
                nc.sbuf_tensor(n, shp, dt))
            Wq_b = s3("Wq_b", [128, 8 * 1024], BF16)
            Wg_b = s3("Wg_b", [128, 8 * 1024], BF16)
            wstg3 = s3("wstg3", [128, 2 * 1024])
            Kb = s3("Kb", [128, 2 * 2048], BF16)
            Vb = s3("Vb", [128, 2 * 2048], BF16)
            x3 = s3("x3", [128, 2 * 2048])
            qtm = s3("qtm", [128, 1024], BF16)
            qsw = s3("qsw", [128, 1024], BF16)
            qr1 = s3("qr1", [128, 1024], BF16)
            qrot = s3("qrot", [128, 1024], BF16)
            qfm = s3("qfm", [128, 2048], BF16)
            qd2 = s3("qd2", [128, 512], BF16)
            gate = s3("gate", [128, 2048], BF16)
            swt = s3("swt", [128, 2 * 1024], BF16)
            otm = s3("otm", [128, 256])
            outsb = s3("outsb", [128, 2048])
            ctb3 = s3("ctb3", [128, 2 * 1024], BF16)
            stb3 = s3("stb3", [128, 2 * 1024], BF16)
            nw3 = 0
            for wd, wb in ((WqT, Wq_b), (WgT, Wg_b)):
                for ec in range(8):
                    if nw3 >= 2:
                        wait("sync", "p3w", nw3 - 1)
                    tgt = dma("sync", wstg3[:, (nw3 % 2) * 1024:
                                            (nw3 % 2 + 1) * 1024],
                              wd[128 * ec:128 * (ec + 1), :])
                    wait("scalar", "ld", tgt)
                    inc("scalar", lambda s, wb=wb, ec=ec, nw3=nw3: s.copy(
                        wb[:, 1024 * ec:1024 * (ec + 1)],
                        wstg3[:, (nw3 % 2) * 1024:(nw3 % 2 + 1) * 1024]),
                        "p3w")
                    nw3 += 1
            barrier()
            with nc.psum_tensor("q3_0", [128, 512], F32) as ps_q0, \
                 nc.psum_tensor("q3_1", [128, 512], F32) as ps_q1, \
                 nc.psum_tensor("q3_t0", [128, 128], BF16) as ps3_t0, \
                 nc.psum_tensor("q3_t1", [128, 128], BF16) as ps3_t1, \
                 nc.psum_tensor("s3_0", [128, 512], F32) as ps3_s0, \
                 nc.psum_tensor("s3_1", [128, 512], F32) as ps3_s1, \
                 nc.psum_tensor("o3_0", [128, 256], F32) as ps3_o0, \
                 nc.psum_tensor("o3_1", [128, 256], F32) as ps3_o1:
                ps3_g = ps3_s0
                ps3_t = (ps3_t0, ps3_t1)
                pss = (ps3_s0, ps3_s1)
                pso = (ps3_o0, ps3_o1)
                spill_tgt = [0]
                for blk in range(NBLK):
                    pr = blk % 2
                    if blk >= 2:
                        wait("gpsimd", "omd", (blk - 1) * 8)
                    lk = dma("gpsimd", Kb[:, 2048 * pr:2048 * (pr + 1)],
                             kfm_sp[:, 2048 * blk:2048 * (blk + 1)],
                             semn="sp")
                    lv = dma("gpsimd", Vb[:, 2048 * pr:2048 * (pr + 1)],
                             vtm_sp[:, 2048 * blk:2048 * (blk + 1)],
                             semn="sp")
                    if blk >= 2:
                        wait("sync", "oev", (blk - 1) * 8)
                    lx = dma("sync", x3[:, 2048 * pr:2048 * (pr + 1)]
                             .rearrange("p (c t) -> p c t", c=8),
                             xT[:, CB * blk:CB * (blk + 1)]
                             .rearrange("(c p) t -> p c t", p=128))
                    for tc in range(2):
                        tg3 = blk * 2 + tc
                        if tg3 >= 2:
                            wait("sync", "qrd", tg3 - 1)
                        t_c = dma("sync", ctb3[:, tc * 1024:(tc + 1) * 1024],
                                  qcos_d[CB * blk + 128 * tc:
                                         CB * blk + 128 * (tc + 1), :])
                        t_s = dma("sync", stb3[:, tc * 1024:(tc + 1) * 1024],
                                  qsin_d[CB * blk + 128 * tc:
                                         CB * blk + 128 * (tc + 1), :])
                        if tg3 >= 1:
                            wait("tensor", "qev", tg3)
                        for nh, pk in ((0, ps_q0), (1, ps_q1)):
                            for ec in range(8):
                                op("tensor", lambda t, ec=ec, nh=nh, pk=pk,
                                   blk=blk, tc=tc: t.matmul(
                                       pk[:, :],
                                       xnb[:, 2048 * ec + CB * blk
                                           + 128 * tc:
                                           2048 * ec + CB * blk
                                           + 128 * (tc + 1)],
                                       Wq_b[:, 1024 * ec + 512 * nh:
                                            1024 * ec + 512 * (nh + 1)],
                                       start=(ec == 0), stop=False))
                            op("tensor", lambda t, nh=nh, pk=pk: t.matmul(
                                pk[:, :], cons["ones_row_b"][:, :],
                                bqr[0:1, 512 * nh:512 * (nh + 1)],
                                start=False, stop=True))
                        P.inc_prev("tensor", "qmm")
                        wait("scalar", "qmm")
                        if tg3 >= 1:
                            wait("scalar", "qrd", tg3)
                        op("scalar", lambda s: s.copy(
                            qtm[:, 0:512], ps_q0[:, :]))
                        inc("scalar", lambda s: s.copy(
                            qtm[:, 512:1024], ps_q1[:, :]), "qev")
                        wait("gpsimd", "qev")
                        if tg3 >= 1:
                            wait("gpsimd", "qrd", tg3)
                        op("gpsimd", lambda e: e.tensor_scalar_mul(
                            qsw[:, 0:1024:2], qtm[:, 1:1024:2], -1.0))
                        inc("gpsimd", lambda e: e.tensor_copy(
                            qsw[:, 1:1024:2], qtm[:, 0:1024:2]), "qsd")
                        wait("vector", "qsd")
                        wait("vector", "ld", max(t_c, t_s))
                        if tg3 >= 1:
                            wait("vector", "qtt", 8 * tg3)
                        op("vector", lambda v, tc=tc: v.tensor_mul(
                            qrot[:, :], qtm[:, 0:1024],
                            ctb3[:, tc * 1024:(tc + 1) * 1024]))
                        op("vector", lambda v, tc=tc: v.tensor_mul(
                            qr1[:, :], qsw[:, :],
                            stb3[:, tc * 1024:(tc + 1) * 1024]))
                        inc("vector", lambda v: v.tensor_add(
                            qrot[:, :], qrot[:, :], qr1[:, :]), "qrd")
                        wait("tensor", "qrd")
                        for jj in range(8):
                            jg = tg3 * 8 + jj
                            if jg >= 2:
                                wait("tensor", "qte", jg - 1)
                            inc("tensor", lambda t, jj=jj: t.transpose(
                                ps3_t[jj % 2][:, :],
                                qrot[:, 128 * jj:128 * (jj + 1)],
                                cons["ident_b"][:, :]), "qtt")
                            wait("vector", "qtt", tg3 * 8 + jj + 1)
                            if blk >= 1 and tc == 0 and jj == 0:
                                wait("vector", "omd", blk * 8)
                            inc("vector", lambda v, jj=jj, tc=tc:
                                v.tensor_copy(
                                    qfm[:, 256 * jj + 128 * tc:
                                        256 * jj + 128 * (tc + 1)],
                                    ps3_t[jj % 2][:, :]), "qte")
                    if blk >= 1:
                        wait("tensor", "swd", blk * 8)
                        wait("scalar", "oev", blk * 8)
                    for fc in range(8):
                        kf = blk * 8 + fc
                        if kf >= 1:
                            wait("tensor", "gev", kf)
                        for ec in range(8):
                            op("tensor", lambda t, ec=ec, fc=fc, blk=blk:
                               t.matmul(
                                   ps3_g[:, 0:256],
                                   Wg_b[:, 1024 * ec + 128 * fc:
                                        1024 * ec + 128 * (fc + 1)],
                                   xnb[:, 2048 * ec + CB * blk:
                                       2048 * ec + CB * (blk + 1)],
                                   start=(ec == 0), stop=(ec == 7)))
                        P.inc_prev("tensor", "gmm")
                        wait("scalar", "gmm")
                        inc("scalar", lambda s, fc=fc: s.activation(
                            gate[:, 256 * fc:256 * (fc + 1)],
                            ps3_g[:, 0:256],
                            AF.Silu, bias=bgc[:, fc:fc + 1]), "gev")
                    wait("vector", "qte", 16 * (blk + 1))
                    inc("vector", lambda v: v.engine_nop(), "qdd")
                    wait("tensor", "sp", max(lk, lv))
                    wait("tensor", "qdd")
                    wait("tensor", "gev", blk * 8 + 8)
                    for pi in range(8):
                        pg = blk * 8 + pi
                        if pg >= 1:
                            wait("tensor", "swd", pg)
                        for hh in (0, 1):
                            for kc in (0, 1):
                                op("tensor", lambda t, kc=kc, pi=pi,
                                   hh=hh, pr=pr: t.matmul(
                                       pss[hh][:, 256 * kc:256 * (kc + 1)],
                                       Kb[64 * hh:64 * hh + 64,
                                          2048 * pr + 1024 * kc + 128 * pi:
                                          2048 * pr + 1024 * kc
                                          + 128 * (pi + 1)],
                                       qfm[64 * hh:64 * hh + 64,
                                           256 * pi:256 * (pi + 1)],
                                       start=True, stop=True))
                        P.inc_prev("tensor", "stm")
                        wait("vector", "stm", pg + 1)
                        if pg >= 2:
                            wait("vector", "omd", pg - 1)
                        op("vector", lambda v, pi=pi, pg=pg: v.tensor_mul(
                            qd2[:, 256 * (pg % 2):256 * (pg % 2 + 1)],
                            qfm[:, 256 * pi:256 * (pi + 1)],
                            Bq[:, 256 * pi:256 * (pi + 1)]))
                        for hh in (0, 1):
                            for kc in (0, 1):
                                h = 2 * pi + hh
                                op("vector", lambda v, kc=kc, h=h, hh=hh,
                                   pi=pi: v.tensor_mul(
                                       swt[:, 1024 * (pi % 2) + 512 * hh
                                           + 256 * kc:
                                           1024 * (pi % 2) + 512 * hh
                                           + 256 * (kc + 1)],
                                       pss[hh][:, 256 * kc:256 * (kc + 1)],
                                       WTt[:, 512 * h + 256 * kc:
                                           512 * h + 256 * (kc + 1)]))
                        P.inc_prev("vector", "swd")
                        wait("tensor", "swd", pg + 1)
                        if pg >= 2:
                            wait("tensor", "oev", pg - 1)
                        for hh in (0, 1):
                            h = 2 * pi + hh
                            hb = 64 * hh
                            for kc in (0, 1):
                                op("tensor", lambda t, kc=kc, h=h, hb=hb,
                                   pr=pr, pi=pi, hh=hh: t.matmul(
                                       pso[pi % 2][hb:hb + 64, :],
                                       Vb[:, 2048 * pr + 1024 * kc + 64 * h:
                                          2048 * pr + 1024 * kc
                                          + 64 * (h + 1)],
                                       swt[:, 1024 * (pi % 2) + 512 * hh
                                           + 256 * kc:
                                           1024 * (pi % 2) + 512 * hh
                                           + 256 * (kc + 1)],
                                       start=(kc == 0), stop=False))
                            op("tensor", lambda t, h=h, hb=hb, blk=blk,
                               pi=pi, pg=pg: t.matmul(
                                   pso[pi % 2][hb:hb + 64, :],
                                   Rb16[hb:hb + 64, 1024 * blk + 64 * h:
                                        1024 * blk + 64 * (h + 1)],
                                   qd2[hb:hb + 64,
                                       256 * (pg % 2):256 * (pg % 2 + 1)],
                                   start=False, stop=False))
                            op("tensor", lambda t, h=h, hb=hb, blk=blk,
                               pi=pi: t.matmul(
                                   pso[pi % 2][hb:hb + 64, :],
                                   Zb16[hb:hb + 64, 1024 * blk + 64 * h:
                                        1024 * blk + 64 * (h + 1)],
                                   qfm[hb:hb + 64, 256 * pi:256 * (pi + 1)],
                                   start=False, stop=True))
                        P.inc_prev("tensor", "omd")
                        wait("vector", "omd")
                        wait("vector", "ld", lx)
                        wait("vector", "gev", blk * 8 + 8)
                        if pi == 0 and blk >= 1:
                            wait("vector", "sp", spill_tgt[0])
                        op("vector", lambda v, pi=pi: v.tensor_mul(
                            otm[:, :], pso[pi % 2][:, :],
                            gate[:, 256 * pi:256 * (pi + 1)]))
                        inc("vector", lambda v, pi=pi, pr=pr:
                            v.tensor_add(
                                outsb[:, 256 * pi:256 * (pi + 1)],
                                otm[:, :],
                                x3[:, 2048 * pr + 256 * pi:
                                   2048 * pr + 256 * (pi + 1)]), "oev")
                    wait("gpsimd", "oev", blk * 8 + 8)
                    dma("gpsimd",
                        part_sp.rearrange("(c p) T -> p c T", p=128)
                        [:, :, CB * blk:CB * (blk + 1)],
                        outsb[:, :].rearrange("p (c t) -> p c t", c=8),
                        semn="sp")
                    spill_tgt[0] = P.v("sp")
                barrier()
        p23.close()
        barrier()
        P.phase_end()
        if upto < 4:
            nc._prog = P
            P.emit()
            return nc

        ry2sb = stk.enter_context(
            nc.sbuf_tensor("ry2sb", [128, NFT * FT], F32))
        with ES() as p4:
            s4 = lambda n, shp, dt=F32: p4.enter_context(
                nc.sbuf_tensor(n, shp, dt))
            b1outer = ES()
            w1q = b1outer.enter_context(
                nc.sbuf_tensor("w1q", [128, 8 * 4096], BF16))
            wq_scope = ES()
            s4b = lambda n, shp, dt=F32: wq_scope.enter_context(
                nc.sbuf_tensor(n, shp, dt))
            wst4 = s4b("wst4", [128, 4096])
            t4a = s4b("t4a", [128, 4096])
            ti4 = s4b("ti4", [128, 4096], I32)
            tf4 = s4b("tf4", [128, 4096])
            for ch in range(8):
                tgt = dma("sync", wst4[:, :],
                          w1T[128 * ch:128 * (ch + 1), :])
                wait("sync", "wqa", P.v("wqa") + 1)
                wait("vector", "ld", tgt)
                op("vector", lambda v: v.tensor_scalar_mul(
                    t4a[:, :], wst4[:, :], s1c[:, :]))
                inc("vector", lambda v: v.tensor_copy(ti4[:, :], t4a[:, :]),
                    "wqa")
                op("vector", lambda v: v.tensor_copy(tf4[:, :], ti4[:, :]))
                inc("vector", lambda v, ch=ch: v.tensor_scalar(
                    w1q[:, 4096 * ch:4096 * (ch + 1)], tf4[:, :],
                    -1.0, 1.0, ALU.max, ALU.min), "wqd")
            if _DBG:
                dma("gpsimd", w1q_dbg, w1q[:, :], semn="sp")
                op("vector", lambda v: v.tensor_copy(dbgc[:, 0:1],
                                                     m1rc[:, :]))
                op("vector", lambda v: v.tensor_copy(dbgc[:, 1:2],
                                                     m2rc[:, :]))
                op("vector", lambda v: v.tensor_copy(dbgc[:, 2:3],
                                                     s1c[:, :]))
                inc("vector", lambda v: v.tensor_copy(dbgc[:, 3:4],
                                                      s2c[:, :]), "dgc")
                wait("gpsimd", "dgc")
                dma("gpsimd", mc_dbg, dbgc[:, :], semn="sp")
            barrier()
            P.phase_end()
            wq_scope.close()
            with ES() as b1s:
                s41 = lambda n, shp, dt=F32: b1s.enter_context(
                    nc.sbuf_tensor(n, shp, dt))
                xq4 = s41("xq4", [128, FT], BF16)
                rwt4 = s41("rwt4", [1, 2 * FT])
                h0 = s41("h0", [128, 8 * FT])
                h0q = s41("h0q", [128, 8 * FT], BF16)
                M4 = s41("M4", [128, FT])
                X4 = s41("X4", [128, 8])
                XC = s41("XC", [128, 8])
                XP = s41("XP", [128, 8])
                rws4 = s41("rws4", [32, 128])
                sh4 = s41("sh4", [32, 128])
                h1 = s41("h1", [128, 32 * FT], BF16)
                M42 = s41("M42", [128, FT], BF16)
                t4b = s41("t4b", [128, FT])
                ti4b = s41("ti4b", [128, FT], I32)
                tf4b = s41("tf4b", [128, FT])
                h1_tgt = [0]
                for ft in range(NFT):
                    with nc.psum_tensor("f4su_%d" % ft, [1, FT], F32) as f4su, \
                         nc.psum_tensor("f4sq_%d" % ft, [1, FT], F32) as f4sq, \
                         nc.psum_tensor("f4bc_%d" % ft, [128, 2 * FT], F32) as f4bc:
                        for fc in range(8):
                            k = ft * 8 + fc
                            if k >= 1:
                                wait("scalar", "fqm", k)
                            inc("scalar", lambda s, fc=fc, ft=ft:
                                s.activation(
                                    xq4[:, :],
                                    xnb[:, 2048 * fc + FT * ft:
                                        2048 * fc + FT * (ft + 1)],
                                    AF.Square), "fqa")
                        for fc in range(8):
                            op("tensor", lambda t, fc=fc, ft=ft,
                               f4su=f4su: t.matmul(
                                f4su[:, :], cons["ones_col_b"][:, :],
                                xnb[:, 2048 * fc + FT * ft:
                                    2048 * fc + FT * (ft + 1)],
                                start=(fc == 0), stop=(fc == 7)))
                        for fc in range(8):
                            wait("tensor", "fqa", ft * 8 + fc + 1)
                            inc("tensor", lambda t, fc=fc,
                                f4sq=f4sq: t.matmul(
                                f4sq[:, :], cons["ones_col_b"][:, :],
                                xq4[:, :],
                                start=(fc == 0), stop=(fc == 7)), "fqm")
                        q0 = rwt4[0:1, 0:FT]
                        q1 = rwt4[0:1, FT:2 * FT]
                        wait("vector", "fqm", 8 * (ft + 1))
                        if ft >= 1:
                            wait("vector", "fbc", ft)
                        op("vector", lambda v, f4su=f4su: v.tensor_scalar_mul(
                            q0, f4su[:, :], 1.0 / 1024.0))
                        inc("vector", lambda v, f4sq=f4sq:
                            v.tensor_scalar_mul(
                                q1, f4sq[:, :], 1.0 / 1024.0), "fr1")
                        if _DBG and ft == 0:
                            wait("gpsimd", "fr1", P.v("fr1"))
                            dma("gpsimd", rw_dbg[:, 0:2 * FT],
                                rwt4[:, :], semn="sp")
                            wait("vector", "sp", P.v("sp"))
                        wait("tensor", "fr1")
                        inc("tensor", lambda t, f4bc=f4bc: t.matmul(
                            f4bc[:, 0:FT], cons["ones_row_f"][:, :], q0,
                            start=True, stop=True), "fbd")
                        wait("vector", "fbd")
                        op("vector", lambda v: v.tensor_mul(q0, q0, q0))
                        inc("vector", lambda v: v.tensor_sub(q1, q1, q0),
                            "fr2")
                        wait("scalar", "fr2")
                        op("scalar", lambda s: s.activation(
                            q0, q1, AF.Ln, bias=cons["epsc"][0:1, :]))
                        inc("scalar", lambda s: s.activation(
                            q1, q0, AF.Exp, scale=-0.5), "fr4")
                        if _DBG and ft == 0:
                            wait("gpsimd", "fr4", P.v("fr4"))
                            dma("gpsimd", rw_dbg[:, 2 * FT:4 * FT],
                                rwt4[:, :], semn="sp")
                            wait("tensor", "sp", P.v("sp"))
                        wait("tensor", "fr4")
                        inc("tensor", lambda t, f4bc=f4bc: t.matmul(
                            f4bc[:, FT:2 * FT], cons["ones_row_f"][:, :],
                            q1, start=True, stop=True), "fbc")
                        wait("vector", "fbc")
                        for fc in range(8):
                            op("vector", lambda v, fc=fc, ft=ft,
                               f4bc=f4bc: v.tensor_sub(
                                t4b[:, :],
                                xnb[:, 2048 * fc + FT * ft:
                                    2048 * fc + FT * (ft + 1)],
                                f4bc[:, 0:FT]))
                            op("vector", lambda v, fc=fc, f4bc=f4bc:
                               v.tensor_mul(
                                t4b[:, :], t4b[:, :], f4bc[:, FT:2 * FT]))
                            op("vector", lambda v, fc=fc: v.tensor_scalar(
                                h0[:, FT * fc:FT * (fc + 1)], t4b[:, :],
                                gmc[:, fc:fc + 1], btc[:, fc:fc + 1],
                                ALU.mult, ALU.add))
                        inc("vector", lambda v: v.engine_nop(), "fh0")
                    with nc.psum_tensor("f4ta_%d" % ft, [128, 128], F32) as f4ta, \
                         nc.psum_tensor("f4tb_%d" % ft, [128, 128], F32) as f4tb, \
                         nc.psum_tensor("f4tx_%d" % ft, [8, 128], F32) as f4tx, \
                         nc.psum_tensor("f4b1_%d" % ft, [128, FT], F32) as f4b1, \
                         nc.psum_tensor("f4b2_%d" % ft, [128, FT], F32) as f4b2:
                        f4t = (f4ta, f4tb)
                        inc("vector", lambda v: v.tensor_reduce(
                            M4[:, :],
                            h0[:, :].rearrange("p (c t) -> p t c", c=8),
                            AX.X, ALU.max, apply_absolute_value=True),
                            "fm1")
                        wait("tensor", "fm1")
                        for r in range(4):
                            rg = ft * 4 + r
                            if r >= 2:
                                wait("tensor", "fm2b", rg - 1)
                            inc("tensor", lambda t, r=r, f4t=f4t: t.transpose(
                                f4t[r % 2][:, :],
                                M4[:, 128 * r:128 * (r + 1)],
                                cons["ident_f"][:, :]), "fm2")
                            wait("vector", "fm2", ft * 4 + r + 1)
                            inc("vector", lambda v, r=r, f4t=f4t:
                                v.tensor_reduce(
                                    X4[:, r:r + 1], f4t[r % 2][:, :],
                                    AX.X, ALU.max), "fm2b")
                        wait("vector", "fm2b", P.v("fm2b"))
                        inc("vector", lambda v: v.tensor_scalar_max(
                            XC[:, 0:4], X4[:, 0:4], EPS), "fmc")
                        wait("scalar", "fmc")
                        op("scalar", lambda s: s.activation(
                            XC[:, 4:8], XC[:, 0:4], AF.Ln))
                        inc("scalar", lambda s: s.activation(
                            XP[:, 0:4], XC[:, 4:8], AF.Exp, scale=-1.0,
                            bias=cons["l127c"][:, :]), "fmd")
                        wait("vector", "fmd")
                        inc("vector", lambda v: v.tensor_scalar_mul(
                            XP[:, 4:8], XC[:, 0:4], 1.0 / 127.0), "fm3")
                        if _DBG and ft == 0:
                            wait("gpsimd", "fm3", P.v("fm3"))
                            dma("gpsimd", xp_dbg[:, 0:8], X4[:, :],
                                semn="sp")
                            dma("gpsimd", xp_dbg[:, 8:16], XP[:, :],
                                semn="sp")
                            wait("vector", "sp", P.v("sp"))
                        wait("tensor", "fm3")
                        inc("tensor", lambda t, f4tx=f4tx: t.transpose(
                            f4tx[:, :], XP[:, :], cons["ident_f"][:, :]),
                            "fm4")
                        wait("vector", "fm4")
                        inc("vector", lambda v, f4tx=f4tx: v.tensor_copy(
                            rws4[0:8, :], f4tx[:, :]), "fm5")
                        if _DBG and ft == 0:
                            wait("gpsimd", "fm5", P.v("fm5"))
                            dma("gpsimd",
                                xp_dbg[:, 16:48].rearrange(
                                    "p (a b) -> (p a) b", a=1)
                                if False else xp_dbg[0:32, 16:48],
                                rws4[:, 0:32], semn="sp")
                            wait("vector", "sp", P.v("sp"))
                        for r in range(4):
                            wait("vector", "fm5")
                            inc("vector", lambda v, r=r: v.stream_shuffle(
                                sh4[:, :], rws4[:, :],
                                [r] + list(range(1, 32))), "fm6")
                            wait("tensor", "fm6")
                            inc("tensor", lambda t, r=r, f4b1=f4b1: t.matmul(
                                f4b1[:, 128 * r:128 * (r + 1)],
                                cons["ones_row_f"][:, :], sh4[0:1, :],
                                start=True, stop=True), "fm7")
                            wait("vector", "fm7")
                            inc("vector", lambda v, r=r: v.stream_shuffle(
                                sh4[:, :], rws4[:, :],
                                [4 + r] + list(range(1, 32))), "fm5")
                            wait("tensor", "fm5")
                            inc("tensor", lambda t, r=r, f4b2=f4b2: t.matmul(
                                f4b2[:, 128 * r:128 * (r + 1)],
                                cons["ones_row_f"][:, :], sh4[0:1, :],
                                start=True, stop=True), "fm7")
                            wait("vector", "fm7")
                            inc("vector", lambda v: v.engine_nop(), "fm5")
                        for fc in range(8):
                            op("vector", lambda v, fc=fc, f4b1=f4b1:
                               v.tensor_mul(
                                t4b[:, :], h0[:, FT * fc:FT * (fc + 1)],
                                f4b1[:, :]))
                            op("vector", lambda v: v.tensor_copy(
                                ti4b[:, :], t4b[:, :]))
                            op("vector", lambda v: v.tensor_copy(
                                tf4b[:, :], ti4b[:, :]))
                            op("vector", lambda v: v.tensor_scalar(
                                t4b[:, :], tf4b[:, :], -128.0, 127.0,
                                ALU.max, ALU.min))
                            op("vector", lambda v, fc=fc, f4b2=f4b2:
                               v.tensor_mul(
                                h0q[:, FT * fc:FT * (fc + 1)], t4b[:, :],
                                f4b2[:, :]))
                        inc("vector", lambda v: v.engine_nop(), "fq1")
                        if _DBG:
                            wait("gpsimd", "fq1", P.v("fq1"))
                            dma("gpsimd",
                                h0q_dbg[:, 4096 * ft:4096 * (ft + 1)],
                                h0q[:, :], semn="sp")
                            wait("vector", "sp", P.v("sp"))
                        with nc.psum_tensor("f4m0_%d" % ft, [128, FT], F32) as f4m0, \
                             nc.psum_tensor("f4m1_%d" % ft, [128, FT], F32) as f4m1:
                            psm = (f4m0, f4m1)
                            if ft >= 1:
                                wait("scalar", "sp", h1_tgt[0])
                            wait("tensor", "fq1")
                            for f2 in range(32):
                                kf = ft * 32 + f2
                                if f2 >= 2:
                                    wait("tensor", "fge", kf - 1)
                                for ec in range(8):
                                    op("tensor", lambda t, ec=ec, f2=f2,
                                       psm=psm: t.matmul(
                                           psm[f2 % 2][:, :],
                                           w1q[:, 4096 * ec + 128 * f2:
                                               4096 * ec + 128 * (f2 + 1)],
                                           h0q[:, FT * ec:FT * (ec + 1)],
                                           start=(ec == 0), stop=(ec == 7)))
                                P.inc_prev("tensor", "fmm")
                                wait("scalar", "fmm")
                                inc("scalar", lambda s, f2=f2,
                                    psm=psm: s.activation(
                                    h1[:, FT * f2:FT * (f2 + 1)],
                                    psm[f2 % 2][:, :], AF.Gelu,
                                    bias=b1c[:, f2:f2 + 1],
                                    scale=m1rc[:, 0:1]), "fge")
                        wait("vector", "fge", P.v("fge"))
                        inc("vector", lambda v: v.tensor_reduce(
                            M42[:, :],
                            h1[:, :].rearrange("p (c t) -> p t c", c=32),
                            AX.X, ALU.max, apply_absolute_value=True),
                            "fn1")
                        with nc.psum_tensor("f4t2a_%d" % ft, [128, 128],
                                            BF16) as f4t2a, \
                             nc.psum_tensor("f4t2b_%d" % ft, [128, 128],
                                            BF16) as f4t2b:
                            f4t2 = (f4t2a, f4t2b)
                            wait("tensor", "fn1")
                            for r in range(4):
                                rg = ft * 4 + r
                                if r >= 2:
                                    wait("tensor", "fn2b", rg - 1)
                                inc("tensor", lambda t, r=r, f4t2=f4t2:
                                    t.transpose(
                                        f4t2[r % 2][:, :],
                                        M42[:, 128 * r:128 * (r + 1)],
                                        cons["ident_b"][:, :]), "fn2")
                                wait("vector", "fn2", ft * 4 + r + 1)
                                inc("vector", lambda v, r=r, f4t2=f4t2:
                                    v.tensor_reduce(
                                        X4[:, r:r + 1],
                                        f4t2[r % 2][:, :],
                                        AX.X, ALU.max), "fn2b")
                        wait("vector", "fn2b", P.v("fn2b"))
                        inc("vector", lambda v: v.tensor_scalar_max(
                            XC[:, 0:4], X4[:, 0:4], EPS), "fnc")
                        wait("scalar", "fnc")
                        op("scalar", lambda s: s.activation(
                            XC[:, 4:8], XC[:, 0:4], AF.Ln))
                        inc("scalar", lambda s: s.activation(
                            XP[:, 0:4], XC[:, 4:8], AF.Exp, scale=-1.0,
                            bias=cons["l127c"][:, :]), "fnd")
                        wait("vector", "fnd")
                        inc("vector", lambda v: v.tensor_scalar(
                            XP[:, 4:8], XC[:, 0:4], m2rc[:, 0:1],
                            1.0 / 127.0, ALU.mult, ALU.mult), "fn3")
                        wait("tensor", "fn3")
                        inc("tensor", lambda t, f4tx=f4tx: t.transpose(
                            f4tx[:, :], XP[:, :], cons["ident_f"][:, :]),
                            "fn4")
                        wait("vector", "fn4")
                        inc("vector", lambda v, f4tx=f4tx: v.tensor_copy(
                            rws4[0:8, :], f4tx[:, :]), "fn5")
                        for r in range(4):
                            wait("vector", "fn5")
                            inc("vector", lambda v, r=r: v.stream_shuffle(
                                sh4[:, :], rws4[:, :],
                                [r] + list(range(1, 32))), "fn6")
                            wait("tensor", "fn6")
                            inc("tensor", lambda t, r=r, f4b1=f4b1: t.matmul(
                                f4b1[:, 128 * r:128 * (r + 1)],
                                cons["ones_row_f"][:, :], sh4[0:1, :],
                                start=True, stop=True), "fn7")
                            wait("vector", "fn7")
                            inc("vector", lambda v, r=r: v.stream_shuffle(
                                sh4[:, :], rws4[:, :],
                                [4 + r] + list(range(1, 32))), "fn5")
                            wait("tensor", "fn5")
                            inc("tensor", lambda t, r=r, f4b2=f4b2: t.matmul(
                                f4b2[:, 128 * r:128 * (r + 1)],
                                cons["ones_row_f"][:, :], sh4[0:1, :],
                                start=True, stop=True), "fn7")
                            wait("vector", "fn7")
                            inc("vector", lambda v: v.engine_nop(), "fn5")
                        inc("vector", lambda v, ft=ft, f4b2=f4b2:
                            v.tensor_copy(
                            ry2sb[:, FT * ft:FT * (ft + 1)], f4b2[:, :]),
                            "fn8")
                        for f2 in range(32):
                            op("vector", lambda v, f2=f2, f4b1=f4b1:
                               v.tensor_mul(
                                t4b[:, :], h1[:, FT * f2:FT * (f2 + 1)],
                                f4b1[:, :]))
                            op("vector", lambda v: v.tensor_copy(
                                ti4b[:, :], t4b[:, :]))
                            op("vector", lambda v: v.tensor_copy(
                                tf4b[:, :], ti4b[:, :]))
                            op("vector", lambda v, f2=f2: v.tensor_scalar(
                                h1[:, FT * f2:FT * (f2 + 1)], tf4b[:, :],
                                -128.0, 127.0, ALU.max, ALU.min))
                        inc("vector", lambda v: v.engine_nop(), "fq2")
                        wait("gpsimd", "fq2")
                        dma("gpsimd",
                            h1q_sp.rearrange("(c p) T -> p c T", p=128)
                            [:, :, FT * ft:FT * (ft + 1)],
                            h1[:, :].rearrange("p (c t) -> p c t", c=32),
                            semn="sp")
                        h1_tgt[0] = P.v("sp")
                barrier()
            b1outer.close()
            barrier()
            P.phase_end()
            with ES() as b2s:
                s42 = lambda n, shp, dt=F32: b2s.enter_context(
                    nc.sbuf_tensor(n, shp, dt))
                w2q = s42("w2q", [128, 32 * 1024], BF16)
                wr_scope = ES()
                s42b = lambda n, shp, dt=F32: wr_scope.enter_context(
                    nc.sbuf_tensor(n, shp, dt))
                wst5 = s42b("wst5", [128, 1024])
                t5a = s42b("t5a", [128, 1024])
                ti5 = s42b("ti5", [128, 1024], I32)
                tf5 = s42b("tf5", [128, 1024])
                for ch in range(32):
                    tgt = dma("sync", wst5[:, :],
                              w2T[128 * ch:128 * (ch + 1), :])
                    wait("sync", "wra", P.v("wra") + 1)
                    wait("vector", "ld", tgt)
                    op("vector", lambda v: v.tensor_scalar_mul(
                        t5a[:, :], wst5[:, :], s2c[:, :]))
                    inc("vector", lambda v: v.tensor_copy(
                        ti5[:, :], t5a[:, :]), "wra")
                    op("vector", lambda v: v.tensor_copy(
                        tf5[:, :], ti5[:, :]))
                    inc("vector", lambda v, ch=ch: v.tensor_scalar(
                        w2q[:, 1024 * ch:1024 * (ch + 1)], tf5[:, :],
                        -1.0, 1.0, ALU.max, ALU.min), "wrd")
                barrier()
                P.phase_end()
                wr_scope.close()
                h1qb = s42("h1qb", [128, 32 * FT], BF16)
                part4 = s42("part4", [128, 8 * FT])
                y4 = s42("y4", [128, 8 * FT])
                ty4 = s42("ty4", [128, FT])
                ty5 = s42("ty5", [128, FT])
                with nc.psum_tensor("f5m0", [128, FT], F32) as f5m0, \
                     nc.psum_tensor("f5m1", [128, FT], F32) as f5m1:
                    psn = (f5m0, f5m1)
                    for ft in range(NFT):
                        lh = dma("gpsimd", h1qb[:, :]
                                 .rearrange("p (c t) -> p c t", c=32),
                                 h1q_sp.rearrange("(c p) T -> p c T", p=128)
                                 [:, :, FT * ft:FT * (ft + 1)], semn="sp")
                        lp = dma("gpsimd", part4[:, :]
                                 .rearrange("p (c t) -> p c t", c=8),
                                 part_sp.rearrange("(c p) T -> p c T", p=128)
                                 [:, :, FT * ft:FT * (ft + 1)], semn="sp")
                        wait("tensor", "sp", lh)
                        for fc in range(8):
                            kf = ft * 8 + fc
                            if fc >= 2:
                                wait("tensor", "yev", kf - 1)
                            for hc in range(32):
                                op("tensor", lambda t, hc=hc, fc=fc:
                                   t.matmul(
                                       psn[fc % 2][:, :],
                                       w2q[:, 1024 * hc + 128 * fc:
                                           1024 * hc + 128 * (fc + 1)],
                                       h1qb[:, FT * hc:FT * (hc + 1)],
                                       start=(hc == 0), stop=(hc == 31)))
                            P.inc_prev("tensor", "ymm")
                            wait("vector", "ymm")
                            op("vector", lambda v, fc=fc, ft=ft: v.tensor_mul(
                                ty4[:, :], psn[fc % 2][:, :],
                                ry2sb[:, FT * ft:FT * (ft + 1)]))
                            inc("vector", lambda v: v.engine_nop(), "ya1")
                            wait("scalar", "ya1")
                            inc("scalar", lambda s, fc=fc: s.activation(
                                ty5[:, :], ty4[:, :], AF.Identity
                                if hasattr(AF, "Identity") else AF.Copy,
                                bias=b2c[:, fc:fc + 1]), "ya2")
                            wait("vector", "ya2")
                            wait("vector", "sp", lp)
                            inc("vector", lambda v, fc=fc: v.tensor_add(
                                y4[:, FT * fc:FT * (fc + 1)], ty5[:, :],
                                part4[:, FT * fc:FT * (fc + 1)]), "yev")
                        wait("gpsimd", "yev", ft * 8 + 8)
                        dma("gpsimd",
                            y_d.rearrange("(c p) T -> p c T", p=128)
                            [:, :, FT * ft:FT * (ft + 1)],
                            y4[:, :].rearrange("p (c t) -> p c t", c=8),
                            semn="st")
                        barrier()
        barrier()
        if _DBG:
            dbg_specs = [("d_ws", ws_dbg, [1, 8], F32),
                         ("d_rw", rw_dbg, [1, 4 * FT], F32),
                         ("d_w1q", w1q_dbg, [128, 8 * 4096], BF16),
                         ("d_mc", mc_dbg, [128, 4], F32),
                         ("d_xp", xp_dbg, [128, 48], F32),
                         ("d_h0q", h0q_dbg, [128, NFT * 4096], BF16),
                         ("d_kfm", kfm_sp, [128, NPT * 4096], BF16),
                         ("d_vtm", vtm_sp, [128, 16 * 1024], BF16),
                         ("d_gp", gp_sp, [64, 2 * 8192], F32),
                         ("d_h1q", h1q_sp, [HID, TLOC], BF16),
                         ("d_part", part_sp, [E, TLOC], F32)]
            for nm, srct, shp, dt in dbg_specs:
                od = nc.dram_tensor(nm, shp, dt, kind="ExternalOutput").ap()
                dma("gpsimd", od, srct, semn="st")
            wait("gpsimd", "st", P.v("st"))
            barrier()
        wait("gpsimd", "st", P.v("st"))
        nc._prog = P
        P.emit()
    return nc


_NC_CACHE = {}


def _get_nc():
    if "nc" not in _NC_CACHE:
        _NC_CACHE["nc"] = build_nc()
    return _NC_CACHE["nc"]


def kernel(**inp):
    x = np.ascontiguousarray(np.asarray(inp["L0_attn_x"], dtype=np.float32))
    Wq = np.asarray(inp["L0_attn_Wq"], np.float32)
    bq = np.asarray(inp["L0_attn_bq"], np.float32)
    Wk = np.asarray(inp["L0_attn_Wk"], np.float32)
    bk = np.asarray(inp["L0_attn_bk"], np.float32)
    Wv = np.asarray(inp["L0_attn_Wv"], np.float32)
    bv = np.asarray(inp["L0_attn_bv"], np.float32)
    Wg = np.asarray(inp["L0_attn_Wg"], np.float32)
    bg = np.asarray(inp["L0_attn_bg"], np.float32)
    gamma = np.asarray(inp["L0_attn_gamma"], np.float32)
    w1 = np.asarray(inp["L0_ffn_w1"], np.float32)
    b1 = np.asarray(inp["L0_ffn_b1"], np.float32)
    w2 = np.asarray(inp["L0_ffn_w2"], np.float32)
    b2 = np.asarray(inp["L0_ffn_b2"], np.float32)
    fgm = np.asarray(inp["L0_ffn_gamma"], np.float32)
    fbt = np.asarray(inp["L0_ffn_beta"], np.float32)

    WqT = np.ascontiguousarray(Wq.T)
    WkT = np.ascontiguousarray(Wk.T)
    WvT = np.ascontiguousarray(Wv.T)
    WgT = np.ascontiguousarray(Wg.T)
    w1T = np.ascontiguousarray(w1.T)
    w2T = np.ascontiguousarray(w2.T)
    brows = np.ascontiguousarray(np.stack([bq, bk, bv], axis=0))
    bgc = np.ascontiguousarray(bg.reshape(8, 128).T)
    b1c = np.ascontiguousarray(b1.reshape(32, 128).T)
    b2c = np.ascontiguousarray(b2.reshape(8, 128).T)
    gmc = np.ascontiguousarray(fgm.reshape(8, 128).T)
    btc = np.ascontiguousarray(fbt.reshape(8, 128).T)
    gam = gamma.reshape(1, H)

    inv = (1.0 / 10000.0 ** (np.arange(0, 64, 2, dtype=np.float64) / 64.0))
    in_maps = []
    for c in range(8):
        b, p = divmod(c, 4)
        sl = slice(2048 * p, 2048 * (p + 1))
        xTc = np.ascontiguousarray(x[b, sl, :].T)
        pos = np.arange(2048 * p, 2048 * (p + 1), dtype=np.float64)
        fr = pos[:, None] * inv[None, :]
        emb = np.concatenate([fr, fr], axis=1)
        cosn = np.cos(emb).astype(np.float32)
        sinn = np.sin(emb).astype(np.float32)
        kcos = np.ascontiguousarray(np.tile(cosn, (1, H))).astype(BF)
        ksin = np.ascontiguousarray(np.tile(sinn, (1, H))).astype(BF)
        qcos = np.ascontiguousarray(np.tile(cosn * 0.125, (1, H))).astype(BF)
        qsin = np.ascontiguousarray(np.tile(sinn * 0.125, (1, H))).astype(BF)
        nd4 = np.array([[-2048.0 * (p - j - 1) if j < p else -1e9
                         for j in range(4)]], np.float32)
        zm4 = np.array([[1.0 if j > p else 0.0 for j in range(4)]],
                       np.float32)
        in_maps.append({
            "xT": xTc, "WqT": WqT, "WkT": WkT, "WvT": WvT, "WgT": WgT,
            "w1T": w1T, "w2T": w2T, "brows": brows, "bgc": bgc,
            "b1c": b1c, "b2c": b2c, "gmc": gmc, "btc": btc, "gam": gam,
            "nd4": nd4, "zm4": zm4, "kcos": kcos, "ksin": ksin,
            "qcos": qcos, "qsin": qsin})
    nc = _get_nc()
    res = run_bass_kernel_spmd(nc, in_maps, core_ids=list(range(8)))
    out = np.empty((2, 8192, 1024), np.float32)
    for c in range(8):
        b, p = divmod(c, 4)
        out[b, 2048 * p:2048 * (p + 1), :] = res.results[c]["y"].T
    if _DBG:
        DBG_RES["res"] = res.results
    return out


# version 11
# speedup vs baseline: 1.0783x; 1.0007x over previous
import os
import numpy as np
import ml_dtypes

_P1LVL = int(os.environ.get("KP1LVL", "99"))
_P1SUB = int(os.environ.get("KP1SUB", "99"))
_DBG = int(os.environ.get("KDBG", "0"))
DBG_RES = {}

import concourse.bass as bass
import concourse.mybir as mybir
from concourse.bass_utils import run_bass_kernel_spmd

F32 = mybir.dt.float32
BF16 = mybir.dt.bfloat16
I32 = mybir.dt.int32
AF = mybir.ActivationFunctionType
ALU = mybir.AluOpType
AX = mybir.AxisListType
BF = ml_dtypes.bfloat16

E, H, HID = 1024, 16, 4096
TLOC = 2048
NPT, PT = 4, 512
NBLK, CB = 8, 256
NFT, FT = 4, 512
EPS = 1e-5
ENGINES = ("sync", "tensor", "scalar", "vector", "gpsimd")


def _consts():
    c = {}
    c["ones_row_f"] = np.ones((1, 128), np.float32)
    c["ones_row_b"] = np.ones((1, 128), BF)
    c["ones_col_f"] = np.ones((128, 1), np.float32)
    c["ones_col_b"] = np.ones((128, 1), BF)
    c["ident_f"] = np.eye(128, dtype=np.float32)
    c["ident_b"] = np.eye(128, dtype=BF)
    s = np.arange(128)[:, None]
    t = np.arange(CB)[None, :]
    c["dk"] = np.concatenate(
        [np.maximum(t - 128 * kc - s, 0) for kc in (0, 1)],
        axis=1).astype(np.float32)
    c["qdec"] = (np.arange(CB, dtype=np.float32)[None, :] + 1.0)
    j = np.arange(128, dtype=np.float32)[:, None]
    c["dcol"] = np.concatenate([255.0 - 128.0 * kc - j for kc in (0, 1)],
                               axis=1).astype(np.float32)
    c["epsc"] = np.full((128, 1), 1e-5, np.float32)
    c["l127c"] = np.full((128, 1), np.log(127.0), np.float32)
    return c


class Prog:

    PERSIST = ("ld", "sp", "bar", "st")

    def __init__(self, nc, stack):
        self.nc = nc
        self.stack = stack
        self.ops = []
        self.cnt = {}
        self.base = {}
        self.phys_of = {}
        self.phys = []
        self.phys_cnt = []
        self.free_phys = []
        self.barn = 0

    def sem(self, n):
        if n not in self.phys_of:
            if self.free_phys:
                idx = self.free_phys.pop()
            else:
                idx = len(self.phys)
                self.phys.append(
                    self.stack.enter_context(self.nc.semaphore(n)))
                self.phys_cnt.append(0)
            self.phys_of[n] = idx
            self.base[n] = self.phys_cnt[idx]
            self.cnt[n] = 0
        return self.phys[self.phys_of[n]]

    def phase_end(self):
        for n in list(self.phys_of):
            if n not in self.PERSIST:
                self.free_phys.append(self.phys_of[n])
                del self.phys_of[n], self.base[n], self.cnt[n]

    def v(self, n):
        self.sem(n)
        return self.cnt[n]

    def inc_prev(self, eng, semn, k=1):
        h = self.sem(semn)
        self.cnt[semn] += k
        self.phys_cnt[self.phys_of[semn]] += k
        for i in range(len(self.ops) - 1, -1, -1):
            e0, f0, m0 = self.ops[i]
            if e0 == eng:
                assert m0 is None or m0[0] == "inc", \
                    f"inc_prev would stack on {m0}"
                ev = ("inc", semn, k)
                meta = ev if m0 is None else (("multi",) +
                                              (m0[1:] if m0[0] == "multi"
                                               else (m0,)) + (ev,))
                self.ops[i] = (e0, lambda e, f0=f0, h=h, k=k:
                               f0(e).then_inc(h, k), meta)
                return self.cnt[semn]
        raise RuntimeError("no prior op for " + eng)

    def op(self, eng, fn, meta=None):
        self.ops.append((eng, fn, meta))

    def wait(self, eng, semn, val=None):
        h = self.sem(semn)
        v = self.cnt[semn] if val is None else val
        pv = self.base[semn] + v
        self.op(eng, lambda e, h=h, v=pv: e.wait_ge(h, v),
                ("wait", semn, v))

    def inc(self, eng, fn, semn, k=1):
        h = self.sem(semn)
        self.cnt[semn] += k
        self.phys_cnt[self.phys_of[semn]] += k
        self.op(eng, lambda e, fn=fn, h=h, k=k: fn(e).then_inc(h, k),
                ("inc", semn, k))
        return self.cnt[semn]

    def dma(self, eng, out, in_, semn="ld"):
        h = self.sem(semn)
        self.cnt[semn] += 16
        self.phys_cnt[self.phys_of[semn]] += 16
        self.op(eng, lambda e, out=out, in_=in_, h=h:
                e.dma_start(out=out, in_=in_).then_inc(h, 16),
                ("inc", semn, 16))
        return self.cnt[semn]

    def barrier(self):
        self.sem("bar")
        self.barn += 1
        tgt = 5 * self.barn
        ldv = self.cnt.get("ld", 0)
        spv = self.cnt.get("sp", 0)
        for e in ENGINES:
            if e == "sync" and "ld" in self.phys_of:
                self.wait(e, "ld", ldv)
            if e == "gpsimd" and "sp" in self.phys_of:
                self.wait(e, "sp", spv)
            h = self.sem("bar")
            self.op(e, lambda en, h=h: en.sem_inc(h, 1), ("inc", "bar", 1))
            self.op(e, lambda en, h=h, t=tgt: en.wait_ge(h, t),
                    ("wait", "bar", tgt))
        self.cnt["bar"] += 5
        self.phys_cnt[self.phys_of["bar"]] += 5

    def emit(self):
        with self.nc.Block() as block:
            for name in ENGINES:
                def mk(name):
                    def f(eng):
                        for e, fn, _m in self.ops:
                            if e == name:
                                fn(eng)
                    return f
                getattr(block, name)(mk(name))


def build_nc(upto=99):
    from contextlib import ExitStack
    nc = bass.Bass("TRN2", target_bir_lowering=False,
                   detect_race_conditions=False)
    dti = lambda n, s: nc.dram_tensor(n, s, F32, kind="ExternalInput").ap()
    xT = dti("xT", [E, TLOC])
    WqT, WkT = dti("WqT", [E, E]), dti("WkT", [E, E])
    WvT, WgT = dti("WvT", [E, E]), dti("WgT", [E, E])
    w1T, w2T = dti("w1T", [E, HID]), dti("w2T", [HID, E])
    brows = dti("brows", [3, E])
    bgc_d = dti("bgc", [128, 8])
    b1c_d, b2c_d = dti("b1c", [128, 32]), dti("b2c", [128, 8])
    gmc_d, btc_d = dti("gmc", [128, 8]), dti("btc", [128, 8])
    gam_d = dti("gam", [1, H])
    nd4_d, zm4_d = dti("nd4", [1, 4]), dti("zm4", [1, 4])
    dtb = lambda n, s: nc.dram_tensor(n, s, BF16, kind="ExternalInput").ap()
    kcos_d, ksin_d = dtb("kcos", [TLOC, E]), dtb("ksin", [TLOC, E])
    qcos_d, qsin_d = dtb("qcos", [TLOC, E]), dtb("qsin", [TLOC, E])
    y_d = nc.dram_tensor("y", [E, TLOC], F32, kind="ExternalOutput").ap()

    kfm_sp = nc.dram_tensor("kfm_sp", [128, NPT * 4096], BF16).ap()
    vtm_sp = nc.dram_tensor("vtm_sp", [128, 16 * 1024], BF16).ap()
    gp_sp = nc.dram_tensor("gp_sp", [64, 2 * 8192], F32).ap()
    h1q_sp = nc.dram_tensor("h1q_sp", [HID, TLOC], BF16).ap()
    part_sp = nc.dram_tensor("part_sp", [E, TLOC], F32).ap()
    ag_in = nc.dram_tensor("ag_in", [2048, 64], F32).ap()
    ag_out = nc.dram_tensor("ag_out", [8192, 64], F32).ap()
    g1024_sp = nc.dram_tensor("g1024_sp", [1, 1024], F32).ap()
    h0q_dbg = nc.dram_tensor("h0q_dbg", [128, NFT * 4096], BF16).ap()
    rw_dbg = nc.dram_tensor("rw_dbg", [1, 4 * FT], F32).ap()
    w1q_dbg = nc.dram_tensor("w1q_dbg", [128, 8 * 4096], BF16).ap()
    mc_dbg = nc.dram_tensor("mc_dbg", [128, 4], F32).ap()
    ws_dbg = nc.dram_tensor("ws_dbg", [1, 8], F32).ap()
    xp_dbg = nc.dram_tensor("xp_dbg", [128, 48], F32).ap()

    CN = _consts()
    inl = {k: nc.inline_tensor(v, k) for k, v in CN.items()}

    with ExitStack() as stk:
        P = Prog(nc, stk)
        sb = lambda n, shp, dt=F32: stk.enter_context(
            nc.sbuf_tensor(n, shp, dt))
        op, wait, inc, dma, barrier = P.op, P.wait, P.inc, P.dma, P.barrier

        xnb = sb("xnb", [128, 8 * TLOC], BF16)
        WTt = sb("WTt", [128, H * 512], BF16)
        Bq = sb("Bq", [128, (H // 2) * CB], BF16)
        cons = {}
        for nm, shp, dt in [("ones_row_f", [1, 128], F32),
                            ("ones_row_b", [1, 128], BF16),
                            ("ones_col_f", [128, 1], F32),
                            ("ones_col_b", [128, 1], BF16),
                            ("ident_f", [128, 128], F32),
                            ("ident_b", [128, 128], BF16),
                            ("dcol", [128, 2], F32),
                            ("epsc", [128, 1], F32),
                            ("l127c", [128, 1], F32)]:
            cons[nm] = sb("c_" + nm, shp, dt)
        bqr = sb("bqr", [1, E], BF16)
        bkr = sb("bkr", [1, E], BF16)
        bvr = sb("bvr", [1, E], BF16)
        bgc = sb("bgc_s", [128, 8])
        b1c, b2c = sb("b1c_s", [128, 32]), sb("b2c_s", [128, 8])
        gmc, btc = sb("gmc_s", [128, 8]), sb("btc_s", [128, 8])
        gam, gsig = sb("gam_s", [1, H]), sb("gsig", [1, H])
        nd4, zm4 = sb("nd4_s", [1, 4]), sb("zm4_s", [1, 4])
        gncol = sb("gncol", [128, H])
        zcol = sb("zcol", [128, 4])
        g1024 = sb("g1024", [1, 1024])
        grows = sb("grows", [16, 64])
        dcole = sb("dcole", [128, 2 * H])
        m1rc, m2rc = sb("m1rc", [128, 1]), sb("m2rc", [128, 1])
        tcol = sb("tcol", [128, 1])
        dbgc = sb("dbgc", [128, 4]) if _DBG else None
        s1c, s2c = sb("s1c", [128, 1]), sb("s2c", [128, 1])

        from contextlib import ExitStack as ES0
        p0scope = ES0()
        for nm, shp, dt in [("dk", [128, 512], F32), ("qdec", [1, CB], F32)]:
            cons[nm] = p0scope.enter_context(
                nc.sbuf_tensor("c_" + nm, shp, dt))
        btmpf = p0scope.enter_context(nc.sbuf_tensor("btmpf", [1, E], F32))
        for nm in cons:
            dma("sync", cons[nm][:, :], inl[nm].ap())
        for i, row in enumerate([bqr, bkr, bvr]):
            tgt = dma("sync", btmpf[:, :], brows[i:i + 1, :])
            wait("vector", "ld", tgt)
            inc("vector", lambda v, row=row: v.tensor_copy(
                row[:, :], btmpf[:, :]), "p0v")
            wait("sync", "p0v")
        dma("sync", bgc[:, :], bgc_d)
        dma("sync", b1c[:, :], b1c_d)
        dma("sync", b2c[:, :], b2c_d)
        dma("sync", gmc[:, :], gmc_d)
        dma("sync", btc[:, :], btc_d)
        dma("sync", gam[:, :], gam_d)
        dma("sync", nd4[:, :], nd4_d)
        dma("sync", zm4[:, :], zm4_d)
        p0ld = P.v("ld")
        wait("scalar", "ld", p0ld)
        inc("scalar", lambda s: s.activation(gsig[:, :], gam[:, :],
                                             AF.Sigmoid), "p0s")
        with nc.psum_tensor("p0a", [128, 64], F32) as p0a, \
             nc.psum_tensor("p0b", [16, 64], F32) as p0b:
            wait("tensor", "ld", p0ld)
            wait("tensor", "p0s")
            op("tensor", lambda t: t.matmul(
                p0a[:, 0:H], cons["ones_row_f"][:, :], gsig[:, :],
                start=True, stop=True))
            op("tensor", lambda t: t.matmul(
                p0a[:, 32:36], cons["ones_row_f"][:, :], zm4[:, :],
                start=True, stop=True))
            inc("tensor", lambda t: t.matmul(
                p0b[:, :], gsig[:, :], cons["ones_row_f"][0:1, 0:64],
                start=True, stop=True), "p0t")
            wait("vector", "p0t")
            op("vector", lambda v: v.tensor_scalar_mul(
                gncol[:, :], p0a[:, 0:H], -1.0))
            op("vector", lambda v: v.tensor_copy(zcol[:, :], p0a[:, 32:36]))
            inc("vector", lambda v: v.tensor_copy(grows[:, :], p0b[:, :]),
                "p0v")
        wait("gpsimd", "p0v")
        dma("gpsimd", g1024_sp.rearrange("x (h d) -> (x h) d", h=16),
            grows[:, :], semn="sp")
        dma("gpsimd", g1024[:, :], g1024_sp, semn="sp")
        barrier()
        for h in range(H):
            inc("scalar", lambda s, h=h: s.activation(
                WTt[:, h * 512:(h + 1) * 512], cons["dk"][:, :], AF.Exp,
                scale=gncol[:, h:h + 1]), "p0s")
            inc("scalar", lambda s, h=h: s.activation(
                dcole[:, 2 * h:2 * h + 2], cons["dcol"][:, :], AF.Exp,
                scale=gncol[:, h:h + 1]), "p0s")
        with nc.sbuf_tensor("brt", [1, H * CB], BF16) as brt, \
             nc.psum_tensor("pbq", [128, CB], F32) as pbq:
            for h in range(H):
                inc("scalar", lambda s, h=h: s.activation(
                    brt[:, h * CB:(h + 1) * CB], cons["qdec"][:, :], AF.Exp,
                    scale=gncol[0:1, h:h + 1]), "p0s")
            wait("tensor", "p0s")
            for pi in range(H // 2):
                if pi:
                    wait("tensor", "p0v")
                op("tensor", lambda t, pi=pi: t.matmul(
                    pbq[0:64, :], cons["ones_row_b"][0:1, 0:64],
                    brt[:, (2 * pi) * CB:(2 * pi + 1) * CB],
                    start=True, stop=True))
                inc("tensor", lambda t, pi=pi: t.matmul(
                    pbq[64:128, :], cons["ones_row_b"][0:1, 0:64],
                    brt[:, (2 * pi + 1) * CB:(2 * pi + 2) * CB],
                    start=True, stop=True), "p0t")
                wait("vector", "p0t")
                inc("vector", lambda v, pi=pi: v.tensor_copy(
                    Bq[:, pi * CB:(pi + 1) * CB], pbq[:, :]), "p0v")
            wait("tensor", "p0v")
        barrier()
        p0scope.close()
        P.phase_end()
        if upto < 1:
            nc._prog = P
            P.emit()
            return nc

        from contextlib import ExitStack as ES
        with ES() as p1:
            s1 = lambda n, shp, dt=F32: p1.enter_context(
                nc.sbuf_tensor(n, shp, dt))
            x_sb = s1("x_sb", [128, 8 * PT])
            xsq = s1("xsq", [128, 2 * PT])
            Wk_b = s1("Wk_b", [128, 8 * 1024], BF16)
            Wv_b = s1("Wv_b", [128, 8 * 1024], BF16)
            wstg = s1("wstg", [128, 2 * 1024])
            rowt = s1("rowt", [1, 3 * PT])
            mub, rsb = s1("mub", [128, PT]), s1("rsb", [128, PT])
            tA = s1("tA", [128, PT])
            ktm = s1("ktm", [128, 2 * 1024], BF16)
            ksw = s1("ksw", [128, 1024], BF16)
            kr1 = s1("kr1", [128, 1024], BF16)
            krot = s1("krot", [128, 2 * 1024], BF16)
            ktil = s1("ktil", [128, 2 * 1024], BF16)
            kstg = s1("kstg", [128, 4096], BF16)
            vstg = s1("vstg", [128, 4096], BF16)
            gpev = s1("gpev", [64, 2048])
            ctab = s1("ctab", [128, 2 * 1024], BF16)
            stab = s1("stab", [128, 2 * 1024], BF16)
            nw1 = 0
            for wd, wb in ((WkT, Wk_b), (WvT, Wv_b)):
                for ec in range(8):
                    if nw1 >= 2:
                        wait("sync", "p1w", nw1 - 1)
                    tgt = dma("sync", wstg[:, (nw1 % 2) * 1024:
                                           (nw1 % 2 + 1) * 1024],
                              wd[128 * ec:128 * (ec + 1), :])
                    wait("scalar", "ld", tgt)
                    inc("scalar", lambda s, wb=wb, ec=ec, nw1=nw1: s.copy(
                        wb[:, 1024 * ec:1024 * (ec + 1)],
                        wstg[:, (nw1 % 2) * 1024:(nw1 % 2 + 1) * 1024]),
                        "p1w")
                    nw1 += 1
            barrier()
            with nc.psum_tensor("ps_sum", [1, PT], F32) as ps_sum, \
                 nc.psum_tensor("ps_sq", [1, PT], F32) as ps_sq, \
                 nc.psum_tensor("ps_k0", [128, 512], F32) as ps_k0, \
                 nc.psum_tensor("ps_k1", [128, 512], F32) as ps_k1, \
                 nc.psum_tensor("ps_t0", [128, 128], BF16) as ps_t0, \
                 nc.psum_tensor("ps_t1", [128, 128], BF16) as ps_t1, \
                 nc.psum_tensor("ps_gp", [64, 1024], F32) as ps_gp:
                ps_t = (ps_t0, ps_t1)
                ps_bc = ps_k0
                gp_dma_tgt = [0]
                psp_tgt = [0]
                for pt in range(NPT):
                    if _P1LVL < 1:
                        barrier()
                        continue
                    if pt >= 1:
                        wait("sync", "xnd", pt)
                    xt = dma("sync",
                             x_sb[:, :].rearrange("p (c t) -> p c t", c=8),
                             xT[:, PT * pt:PT * (pt + 1)]
                             .rearrange("(c p) t -> p c t", p=128))
                    wait("scalar", "ld", xt)
                    for fc in range(8):
                        k = pt * 8 + fc
                        if k >= 2 and _P1SUB >= 2:
                            wait("scalar", "sqm", k - 1)
                        inc("scalar", lambda s, fc=fc: s.activation(
                            xsq[:, (fc % 2) * PT:(fc % 2 + 1) * PT],
                            x_sb[:, PT * fc:PT * (fc + 1)], AF.Square), "sqa")
                    if _P1SUB < 2:
                        barrier()
                        continue
                    wait("tensor", "ld", xt)
                    if pt >= 1:
                        wait("tensor", "rw1", pt)
                    for fc in range(8):
                        op("tensor", lambda t, fc=fc: t.matmul(
                            ps_sum[:, :], cons["ones_col_f"][:, :],
                            x_sb[:, PT * fc:PT * (fc + 1)],
                            start=(fc == 0), stop=(fc == 7)))
                    for fc in range(8):
                        wait("tensor", "sqa", pt * 8 + fc + 1)
                        inc("tensor", lambda t, fc=fc: t.matmul(
                            ps_sq[:, :], cons["ones_col_f"][:, :],
                            xsq[:, (fc % 2) * PT:(fc % 2 + 1) * PT],
                            start=(fc == 0), stop=(fc == 7)), "sqm")
                    if _P1SUB < 3:
                        barrier()
                        continue
                    r0 = rowt[0:1, 0:PT]
                    r1 = rowt[0:1, PT:2 * PT]
                    r2 = rowt[0:1, 2 * PT:3 * PT]
                    wait("vector", "sqm", 8 * (pt + 1))
                    if pt >= 1:
                        wait("vector", "rw3", pt)
                        wait("vector", "bcd", 2 * pt)
                    op("vector", lambda v: v.tensor_scalar_mul(
                        r0, ps_sum[:, :], 1.0 / 1024.0))
                    if _P1SUB == 25:
                        inc("vector", lambda v: v.engine_nop(), "rw1")
                        barrier()
                        continue
                    op("vector", lambda v: v.tensor_scalar_mul(
                        r1, ps_sq[:, :], 1.0 / 1024.0))
                    if _P1SUB == 26:
                        inc("vector", lambda v: v.engine_nop(), "rw1")
                        barrier()
                        continue
                    op("vector", lambda v: v.tensor_mul(r2, r0, r0))
                    inc("vector", lambda v: v.tensor_sub(r1, r1, r2), "rw1")
                    if _P1SUB == 27:
                        barrier()
                        continue
                    wait("scalar", "rw1")
                    op("scalar", lambda s: s.activation(
                        r2, r1, AF.Ln, bias=cons["epsc"][0:1, :]))
                    inc("scalar", lambda s: s.activation(r1, r2, AF.Exp,
                                                         scale=-0.5), "rw3")
                    if _P1SUB < 4:
                        barrier()
                        continue
                    wait("tensor", "rw3")
                    if pt >= 1:
                        wait("tensor", "kve", 8 * pt)
                    inc("tensor", lambda t: t.matmul(
                        ps_bc[:, :], cons["ones_row_f"][:, :], r0,
                        start=True, stop=True), "bcd")
                    wait("vector", "bcd")
                    inc("vector", lambda v: v.tensor_copy(
                        mub[:, :], ps_bc[:, :]), "bce")
                    wait("tensor", "bce")
                    inc("tensor", lambda t: t.matmul(
                        ps_bc[:, :], cons["ones_row_f"][:, :], r1,
                        start=True, stop=True), "bcd")
                    wait("vector", "bcd")
                    op("vector", lambda v: v.tensor_copy(
                        rsb[:, :], ps_bc[:, :]))
                    if _P1SUB < 5:
                        inc("vector", lambda v: v.engine_nop(), "xnd")
                        barrier()
                        continue
                    for fc in range(8):
                        op("vector", lambda v, fc=fc: v.tensor_sub(
                            tA[:, :], x_sb[:, PT * fc:PT * (fc + 1)],
                            mub[:, :]))
                        op("vector", lambda v, fc=fc, pt=pt: v.tensor_mul(
                            xnb[:, 2048 * fc + PT * pt:
                                2048 * fc + PT * (pt + 1)],
                            tA[:, :], rsb[:, :]))
                    inc("vector", lambda v: v.engine_nop(), "xnd")

                    for tc in range(4):
                        if _P1LVL < 2:
                            continue
                        tg = pt * 4 + tc
                        kc = tc % 2
                        if tg >= 2:
                            wait("sync", "krd", tg - 1)
                        t_c = dma("sync",
                                  ctab[:, kc * 1024:(kc + 1) * 1024],
                                  kcos_d[PT * pt + 128 * tc:
                                         PT * pt + 128 * (tc + 1), :])
                        t_s = dma("sync",
                                  stab[:, kc * 1024:(kc + 1) * 1024],
                                  ksin_d[PT * pt + 128 * tc:
                                         PT * pt + 128 * (tc + 1), :])
                        wait("tensor", "xnd", pt + 1)
                        if tg >= 1:
                            wait("tensor", "kve", 2 * tg)
                        for nh, pk in ((0, ps_k0), (1, ps_k1)):
                            for ec in range(8):
                                op("tensor", lambda t, ec=ec, nh=nh, pk=pk,
                                   pt=pt, tc=tc: t.matmul(
                                       pk[:, :],
                                       xnb[:, 2048 * ec + PT * pt + 128 * tc:
                                           2048 * ec + PT * pt
                                           + 128 * (tc + 1)],
                                       Wk_b[:, 1024 * ec + 512 * nh:
                                            1024 * ec + 512 * (nh + 1)],
                                       start=(ec == 0), stop=False))
                            op("tensor", lambda t, nh=nh, pk=pk: t.matmul(
                                pk[:, :], cons["ones_row_b"][:, :],
                                bkr[0:1, 512 * nh:512 * (nh + 1)],
                                start=False, stop=True))
                        P.inc_prev("tensor", "kmm")
                        wait("scalar", "kmm")
                        op("scalar", lambda s, kc=kc: s.copy(
                            ktm[:, kc * 1024:kc * 1024 + 512], ps_k0[:, :]))
                        inc("scalar", lambda s, kc=kc: s.copy(
                            ktm[:, kc * 1024 + 512:(kc + 1) * 1024],
                            ps_k1[:, :]), "kve")
                        if _P1LVL < 3:
                            continue
                        wait("gpsimd", "kve")
                        if tg >= 1:
                            wait("gpsimd", "krd", tg)
                        op("gpsimd", lambda e, kc=kc: e.tensor_scalar_mul(
                            ksw[:, 0:1024:2],
                            ktm[:, kc * 1024 + 1:(kc + 1) * 1024:2], -1.0))
                        inc("gpsimd", lambda e, kc=kc: e.tensor_copy(
                            ksw[:, 1:1024:2],
                            ktm[:, kc * 1024:(kc + 1) * 1024:2]), "ksd")
                        wait("vector", "ksd")
                        wait("vector", "ld", max(t_c, t_s))
                        if tg >= 2:
                            wait("vector", "pmm", tg // 2)
                        op("vector", lambda v, kc=kc: v.tensor_mul(
                            krot[:, kc * 1024:(kc + 1) * 1024],
                            ktm[:, kc * 1024:(kc + 1) * 1024],
                            ctab[:, kc * 1024:(kc + 1) * 1024]))
                        op("vector", lambda v, kc=kc: v.tensor_mul(
                            kr1[:, :], ksw[:, :],
                            stab[:, kc * 1024:(kc + 1) * 1024]))
                        inc("vector", lambda v, kc=kc: v.tensor_add(
                            krot[:, kc * 1024:(kc + 1) * 1024],
                            krot[:, kc * 1024:(kc + 1) * 1024],
                            kr1[:, :]), "krd")
                        if _P1LVL < 4:
                            continue
                        wait("tensor", "krd")
                        if tc == 0 and pt >= 1:
                            wait("vector", "sp", psp_tgt[0])
                        for jj in range(8):
                            jg = tg * 8 + jj
                            if jg >= 2:
                                wait("tensor", "kte", jg - 1)
                            inc("tensor", lambda t, jj=jj, kc=kc:
                                t.transpose(
                                    ps_t[jj % 2][:, :],
                                    krot[:, kc * 1024 + 128 * jj:
                                         kc * 1024 + 128 * (jj + 1)],
                                    cons["ident_b"][:, :]), "ktt")
                            wait("vector", "ktt", tg * 8 + jj + 1)
                            inc("vector", lambda v, tc=tc, jj=jj:
                                v.tensor_copy(
                                    kstg[:, 1024 * tc + 128 * jj:
                                         1024 * tc + 128 * (jj + 1)],
                                    ps_t[jj % 2][:, :]), "kte")
                        if _P1LVL < 5:
                            continue
                        wait("tensor", "kve", 2 * tg + 1)
                        for nh, pk in ((0, ps_k0), (1, ps_k1)):
                            for ec in range(8):
                                op("tensor", lambda t, ec=ec, nh=nh, pk=pk,
                                   pt=pt, tc=tc: t.matmul(
                                       pk[:, :],
                                       xnb[:, 2048 * ec + PT * pt + 128 * tc:
                                           2048 * ec + PT * pt
                                           + 128 * (tc + 1)],
                                       Wv_b[:, 1024 * ec + 512 * nh:
                                            1024 * ec + 512 * (nh + 1)],
                                       start=(ec == 0), stop=False))
                            op("tensor", lambda t, nh=nh, pk=pk: t.matmul(
                                pk[:, :], cons["ones_row_b"][:, :],
                                bvr[0:1, 512 * nh:512 * (nh + 1)],
                                start=False, stop=True))
                        P.inc_prev("tensor", "vmm")
                        wait("scalar", "vmm")
                        if tc == 0 and pt >= 1:
                            wait("scalar", "sp", psp_tgt[0])
                        op("scalar", lambda s, tc=tc: s.copy(
                            vstg[:, 1024 * tc:1024 * tc + 512], ps_k0[:, :]))
                        inc("scalar", lambda s, tc=tc: s.copy(
                            vstg[:, 1024 * tc + 512:1024 * (tc + 1)],
                            ps_k1[:, :]), "kve")
                        if _P1LVL < 6:
                            continue
                        if tg >= 2:
                            wait("vector", "gsm", tg // 2)
                        for h in range(H):
                            op("vector", lambda v, h=h, kc=kc:
                               v.tensor_scalar_mul(
                                   ktil[:, kc * 1024 + 64 * h:
                                        kc * 1024 + 64 * (h + 1)],
                                   krot[:, kc * 1024 + 64 * h:
                                        kc * 1024 + 64 * (h + 1)],
                                   dcole[:, 2 * h + kc:2 * h + kc + 1]))
                        inc("vector", lambda v: v.engine_nop(), "ktl")
                        if kc == 1:
                            sbk = pt * 2 + tc // 2
                            wait("tensor", "ktl")
                            wait("tensor", "kve", 2 * tg + 2)
                            for h in range(H):
                                for k2 in (0, 1):
                                    op("tensor", lambda t, h=h, k2=k2,
                                       tc=tc: t.matmul(
                                           ps_gp[:, 64 * h:64 * (h + 1)],
                                           ktil[:, k2 * 1024 + 64 * h:
                                                k2 * 1024 + 64 * (h + 1)],
                                           vstg[:, 1024 * (tc - 1 + k2)
                                                + 64 * h:
                                                1024 * (tc - 1 + k2)
                                                + 64 * (h + 1)],
                                           start=(k2 == 0),
                                           stop=(k2 == 1)))
                            P.inc_prev("tensor", "gsm")
                            wait("vector", "gsm")
                            wait("vector", "sp", gp_dma_tgt[0])
                            inc("vector", lambda v: v.tensor_copy(
                                gpev[:, 0:1024], ps_gp[:, :]), "gp1")
                            wait("tensor", "gp1")
                            for h in range(H):
                                for k2 in (0, 1):
                                    op("tensor", lambda t, h=h, k2=k2,
                                       tc=tc: t.matmul(
                                           ps_gp[:, 64 * h:64 * (h + 1)],
                                           krot[:, k2 * 1024 + 64 * h:
                                                k2 * 1024 + 64 * (h + 1)],
                                           vstg[:, 1024 * (tc - 1 + k2)
                                                + 64 * h:
                                                1024 * (tc - 1 + k2)
                                                + 64 * (h + 1)],
                                           start=(k2 == 0),
                                           stop=(k2 == 1)))
                            P.inc_prev("tensor", "pmm")
                            wait("vector", "pmm")
                            inc("vector", lambda v: v.tensor_copy(
                                gpev[:, 1024:2048], ps_gp[:, :]), "gpe")
                            wait("gpsimd", "gpe")
                            dma("gpsimd",
                                gp_sp[:, 1024 * sbk:1024 * (sbk + 1)],
                                gpev[:, 0:1024], semn="sp")
                            dma("gpsimd",
                                gp_sp[:, 8192 + 1024 * sbk:
                                      8192 + 1024 * (sbk + 1)],
                                gpev[:, 1024:2048], semn="sp")
                            gp_dma_tgt[0] = P.v("sp")
                    if _P1LVL >= 7:
                        wait("gpsimd", "kte", 32 * (pt + 1))
                        dma("gpsimd", kfm_sp[:, 4096 * pt:4096 * (pt + 1)],
                            kstg[:, :], semn="sp")
                        wait("gpsimd", "kve", 8 * (pt + 1))
                        dma("gpsimd", vtm_sp[:, 4096 * pt:4096 * (pt + 1)],
                            vstg[:, :], semn="sp")
                        psp_tgt[0] = P.v("sp")
        barrier()
        P.phase_end()
        if upto < 2:
            nc._prog = P
            P.emit()
            return nc

        p23 = stk.enter_context(ES())
        Rb16 = p23.enter_context(
            nc.sbuf_tensor("Rb16", [128, NBLK * 1024], BF16))
        Zb16 = p23.enter_context(
            nc.sbuf_tensor("Zb16", [128, NBLK * 1024], BF16))
        with ES() as p2:
            s2 = lambda n, shp, dt=F32: p2.enter_context(
                nc.sbuf_tensor(n, shp, dt))
            GtPt = s2("GtPt", [128, 8192])
            RpZp = s2("RpZp", [128, 8192])
            gC = s2("gC", [64, 1024])
            rowe = s2("rowe", [1, 1024])
            gat = s2("gat", [128, 4096])
            RacZac = s2("RacZac", [128, 1024])
            tt1 = s2("tt1", [128, 1024])
            wst2 = s2("wst2", [128, 1024])
            wabs = s2("wabs", [128, 1024])
            Gt = lambda a, b: GtPt[0:64, a:b]
            Pt = lambda a, b: GtPt[64:128, a:b]
            Rp = lambda a, b: RpZp[0:64, a:b]
            Zp = lambda a, b: RpZp[64:128, a:b]
            with nc.psum_tensor("ps_r", [64, 1024], F32) as ps_r, \
                 nc.psum_tensor("ps_w", [64, 1024], F32) as ps_w, \
                 nc.psum_tensor("ps_s2", [128, 1], F32) as ps_s2, \
                 nc.psum_tensor("ps_ws", [1, 512], F32) as ps_ws, \
                 nc.psum_tensor("ps_ws2", [1, 512], F32) as ps_ws2:
                t1 = dma("gpsimd", GtPt[0:64, :], gp_sp[:, 0:8192],
                         semn="sp")
                t2 = dma("gpsimd", GtPt[64:128, :], gp_sp[:, 8192:16384],
                         semn="sp")
                inc("scalar", lambda s: s.activation(
                    rowe[:, :], g1024[:, :], AF.Exp, scale=-256.0), "q2s")
                wait("tensor", "q2s")
                op("tensor", lambda t: t.matmul(
                    ps_r[:, 0:512], cons["ones_row_f"][0:1, 0:64],
                    rowe[0:1, 0:512], start=True, stop=True))
                inc("tensor", lambda t: t.matmul(
                    ps_r[:, 512:1024], cons["ones_row_f"][0:1, 0:64],
                    rowe[0:1, 512:1024], start=True, stop=True), "q2t")
                wait("vector", "q2t")
                inc("vector", lambda v: v.tensor_copy(gC[:, :], ps_r[:, :]),
                    "q2v")
                wait("vector", "sp", max(t1, t2))
                op("vector", lambda v: v.memset(Rp(0, 1024), 0.0))
                op("vector", lambda v: v.memset(Zp(7 * 1024, 8192), 0.0))
                for b in range(1, NBLK):
                    op("vector", lambda v, b=b: v.tensor_mul(
                        tt1[0:64, :], Rp(1024 * (b - 1), 1024 * b),
                        gC[:, :]))
                    op("vector", lambda v, b=b: v.tensor_add(
                        Rp(1024 * b, 1024 * (b + 1)), tt1[0:64, :],
                        Gt(1024 * (b - 1), 1024 * b)))
                for b in range(NBLK - 2, -1, -1):
                    op("vector", lambda v, b=b: v.tensor_add(
                        Zp(1024 * b, 1024 * (b + 1)),
                        Zp(1024 * (b + 1), 1024 * (b + 2)),
                        Pt(1024 * (b + 1), 1024 * (b + 2))))
                op("vector", lambda v: v.tensor_mul(
                    tt1[0:64, :], Rp(7 * 1024, 8192), gC[:, :]))
                op("vector", lambda v: v.tensor_add(
                    tt1[0:64, :], tt1[0:64, :], Gt(7 * 1024, 8192)))
                inc("vector", lambda v: v.tensor_add(
                    tt1[64:128, :], Zp(0, 1024), Pt(0, 1024)), "q2v")
                wait("gpsimd", "q2v")
                dma("gpsimd",
                    ag_in.rearrange("(h r) d -> h r d", r=128)[:, 0:64, :]
                    .transpose([1, 0, 2]),
                    tt1[0:64, :].rearrange("i (h d) -> i h d", h=16),
                    semn="sp")
                t3 = dma("gpsimd",
                         ag_in.rearrange("(h r) d -> h r d",
                                         r=128)[:, 64:128, :]
                         .transpose([1, 0, 2]),
                         tt1[64:128, :].rearrange("i (h d) -> i h d", h=16),
                         semn="sp")
                wait("gpsimd", "sp", t3)
                inc("gpsimd", lambda e: e.collective_compute(
                    "AllGather", ALU.bypass,
                    replica_groups=[[0, 1, 2, 3], [4, 5, 6, 7]],
                    ins=[ag_in], outs=[ag_out]), "agd")
                wait("gpsimd", "agd")
                t4 = dma("gpsimd",
                         gat[:, :].rearrange("p (j h d) -> p j h d",
                                             j=4, h=16),
                         ag_out.rearrange("(j h two i) d -> (two i) j h d",
                                          j=4, h=16, two=2), semn="sp")

                wait("tensor", "q2t", 1)
                nw = 0
                for wi, wd, nchunk, cols, mc, sc in (
                        (0, w1T, 8, HID, m1rc, s1c),
                        (1, w2T, 32, E, m2rc, s2c)):
                    psw = ps_ws if wi == 0 else ps_ws2
                    npc = nchunk * (cols // 1024)
                    k0 = 0
                    for ch in range(nchunk):
                        for pc in range(cols // 1024):
                            tgt = dma("sync", wst2[:, :],
                                      wd[128 * ch:128 * (ch + 1),
                                         1024 * pc:1024 * (pc + 1)])
                            wait("scalar", "ld", tgt)
                            inc("scalar", lambda s: s.activation(
                                wabs[:, :], wst2[:, :], AF.Abs), "wsa")
                            wait("sync", "wsa")
                            wait("tensor", "wsa", P.v("wsa"))
                            for nh in range(2):
                                op("tensor", lambda t, nh=nh, k0=k0,
                                   npc=npc, psw=psw: t.matmul(
                                       psw[:, 0:512],
                                       cons["ones_col_f"][:, :],
                                       wabs[:, 512 * nh:512 * (nh + 1)],
                                       start=(k0 == 0 and nh == 0),
                                       stop=(k0 == npc - 1 and nh == 1)))
                            P.inc_prev("tensor", "wsm")
                            wait("scalar", "wsm", P.v("wsm") + 0)
                            k0 += 1
                            nw += 1
                    wait("vector", "wsm", P.v("wsm"))
                    inc("vector", lambda v, psw=psw: v.tensor_reduce(
                        rowe[0:1, 0:1], psw[0:1, 0:512], AX.X, ALU.add),
                        "wm1")
                    wait("vector", "wm1", P.v("wm1"))
                    op("vector", lambda v: v.tensor_scalar(
                        rowe[0:1, 1:2], rowe[0:1, 0:1], 1.0 / (E * HID),
                        EPS, ALU.mult, ALU.max))
                    inc("vector", lambda v: v.engine_nop(), "wm2")
                    if _DBG:
                        wait("gpsimd", "wm2", P.v("wm2"))
                        dma("gpsimd", ws_dbg[:, 4 * wi:4 * wi + 2],
                            rowe[0:1, 0:2], semn="sp")
                        wait("vector", "sp", P.v("sp"))
                    wait("tensor", "wm2", P.v("wm2"))
                    inc("tensor", lambda t: t.matmul(
                        ps_s2[:, :], cons["ones_row_f"][:, :],
                        rowe[0:1, 1:2], start=True, stop=True), "wm3")
                    wait("vector", "wm3", P.v("wm3"))
                    inc("vector", lambda v, mc=mc: v.tensor_copy(
                        mc[:, :], ps_s2[:, :]), "wm4")
                    wait("scalar", "wm4", P.v("wm4"))
                    op("scalar", lambda s, mc=mc: s.activation(
                        tcol[:, :], mc[:, :], AF.Ln))
                    inc("scalar", lambda s, sc=sc: s.activation(
                        sc[:, :], tcol[:, :], AF.Exp, scale=-1.0), "wm5")
                    wait("tensor", "wm2", P.v("wm2"))
                wait("tensor", "wm3", P.v("wm3"))
                wait("vector", "wm5", P.v("wm5"))

                wait("vector", "sp", t4)
                op("vector", lambda v: v.memset(RacZac[:, :], 0.0))
                wait("scalar", "wm3", P.v("wm3"))
                for j in range(4):
                    inc("scalar", lambda s, j=j: s.activation(
                        rowe[:, :], g1024[:, :], AF.Exp,
                        scale=nd4[0:1, j:j + 1]), "bd1")
                    wait("tensor", "bd1")
                    op("tensor", lambda t: t.matmul(
                        ps_w[:, 0:512], cons["ones_row_f"][0:1, 0:64],
                        rowe[0:1, 0:512], start=True, stop=True))
                    inc("tensor", lambda t: t.matmul(
                        ps_w[:, 512:1024], cons["ones_row_f"][0:1, 0:64],
                        rowe[0:1, 512:1024], start=True, stop=True), "bd2")
                    wait("vector", "bd2")
                    op("vector", lambda v, j=j: v.tensor_mul(
                        tt1[0:64, :], gat[0:64, 1024 * j:1024 * (j + 1)],
                        ps_w[:, :]))
                    op("vector", lambda v: v.tensor_add(
                        RacZac[0:64, :], RacZac[0:64, :], tt1[0:64, :]))
                    inc("vector", lambda v, j=j: v.scalar_tensor_tensor(
                        RacZac[64:128, :],
                        gat[64:128, 1024 * j:1024 * (j + 1)],
                        zcol[64:128, j:j + 1], RacZac[64:128, :],
                        ALU.mult, ALU.add), "bd3")
                    wait("tensor", "bd3")
                    wait("scalar", "bd2", P.v("bd2"))
                for b in range(NBLK):
                    inc("scalar", lambda s, b=b: s.activation(
                        rowe[:, :], g1024[:, :], AF.Exp, scale=-256.0 * b),
                        "bd4")
                    wait("tensor", "bd4")
                    op("tensor", lambda t: t.matmul(
                        ps_w[:, 0:512], cons["ones_row_f"][0:1, 0:64],
                        rowe[0:1, 0:512], start=True, stop=True))
                    inc("tensor", lambda t: t.matmul(
                        ps_w[:, 512:1024], cons["ones_row_f"][0:1, 0:64],
                        rowe[0:1, 512:1024], start=True, stop=True), "bd5")
                    wait("vector", "bd5")
                    op("vector", lambda v: v.tensor_mul(
                        tt1[0:64, :], RacZac[0:64, :], ps_w[:, :]))
                    op("vector", lambda v, b=b: v.tensor_add(
                        Rb16[0:64, 1024 * b:1024 * (b + 1)], tt1[0:64, :],
                        Rp(1024 * b, 1024 * (b + 1))))
                    inc("vector", lambda v, b=b: v.tensor_add(
                        Zb16[64:128, 1024 * b:1024 * (b + 1)],
                        Zp(1024 * b, 1024 * (b + 1)),
                        RacZac[64:128, :]), "bd6")
                    wait("tensor", "bd6")
                    wait("scalar", "bd5", P.v("bd5"))
                wait("gpsimd", "bd6", P.v("bd6"))
                dma("gpsimd", Rb16[64:128, :], Rb16[0:64, :], semn="sp")
                dma("gpsimd", Zb16[0:64, :], Zb16[64:128, :], semn="sp")
            barrier()
        barrier()
        P.phase_end()
        if upto < 3:
            nc._prog = P
            P.emit()
            return nc

        with ES() as p3:
            s3 = lambda n, shp, dt=F32: p3.enter_context(
                nc.sbuf_tensor(n, shp, dt))
            Wq_b = s3("Wq_b", [128, 8 * 1024], BF16)
            Wg_b = s3("Wg_b", [128, 8 * 1024], BF16)
            wstg3 = s3("wstg3", [128, 2 * 1024])
            Kb = s3("Kb", [128, 2 * 2048], BF16)
            Vb = s3("Vb", [128, 2 * 2048], BF16)
            x3 = s3("x3", [128, 2 * 2048])
            qtm = s3("qtm", [128, 1024], BF16)
            qsw = s3("qsw", [128, 1024], BF16)
            qr1 = s3("qr1", [128, 1024], BF16)
            qrot = s3("qrot", [128, 1024], BF16)
            qfm = s3("qfm", [128, 2048], BF16)
            qd2 = s3("qd2", [128, 512], BF16)
            gate = s3("gate", [128, 2048], BF16)
            swt = s3("swt", [128, 2 * 1024], BF16)
            otm = s3("otm", [128, 256])
            outsb = s3("outsb", [128, 2048])
            ctb3 = s3("ctb3", [128, 2 * 1024], BF16)
            stb3 = s3("stb3", [128, 2 * 1024], BF16)
            nw3 = 0
            for wd, wb in ((WqT, Wq_b), (WgT, Wg_b)):
                for ec in range(8):
                    if nw3 >= 2:
                        wait("sync", "p3w", nw3 - 1)
                    tgt = dma("sync", wstg3[:, (nw3 % 2) * 1024:
                                            (nw3 % 2 + 1) * 1024],
                              wd[128 * ec:128 * (ec + 1), :])
                    wait("scalar", "ld", tgt)
                    inc("scalar", lambda s, wb=wb, ec=ec, nw3=nw3: s.copy(
                        wb[:, 1024 * ec:1024 * (ec + 1)],
                        wstg3[:, (nw3 % 2) * 1024:(nw3 % 2 + 1) * 1024]),
                        "p3w")
                    nw3 += 1
            barrier()
            with nc.psum_tensor("q3_0", [128, 512], F32) as ps_q0, \
                 nc.psum_tensor("q3_1", [128, 512], F32) as ps_q1, \
                 nc.psum_tensor("q3_t0", [128, 128], BF16) as ps3_t0, \
                 nc.psum_tensor("q3_t1", [128, 128], BF16) as ps3_t1, \
                 nc.psum_tensor("s3_0", [128, 512], F32) as ps3_s0, \
                 nc.psum_tensor("s3_1", [128, 512], F32) as ps3_s1, \
                 nc.psum_tensor("o3_0", [128, 256], F32) as ps3_o0, \
                 nc.psum_tensor("o3_1", [128, 256], F32) as ps3_o1:
                ps3_g = ps3_s0
                ps3_t = (ps3_t0, ps3_t1)
                pss = (ps3_s0, ps3_s1)
                pso = (ps3_o0, ps3_o1)
                spill_tgt = [0]
                for blk in range(NBLK):
                    pr = blk % 2
                    if blk >= 2:
                        wait("gpsimd", "omd", (blk - 1) * 8)
                    lk = dma("gpsimd", Kb[:, 2048 * pr:2048 * (pr + 1)],
                             kfm_sp[:, 2048 * blk:2048 * (blk + 1)],
                             semn="sp")
                    lv = dma("gpsimd", Vb[:, 2048 * pr:2048 * (pr + 1)],
                             vtm_sp[:, 2048 * blk:2048 * (blk + 1)],
                             semn="sp")
                    if blk >= 2:
                        wait("sync", "oev", (blk - 1) * 8)
                    lx = dma("sync", x3[:, 2048 * pr:2048 * (pr + 1)]
                             .rearrange("p (c t) -> p c t", c=8),
                             xT[:, CB * blk:CB * (blk + 1)]
                             .rearrange("(c p) t -> p c t", p=128))
                    for tc in range(2):
                        tg3 = blk * 2 + tc
                        if tg3 >= 2:
                            wait("sync", "qrd", tg3 - 1)
                        t_c = dma("sync", ctb3[:, tc * 1024:(tc + 1) * 1024],
                                  qcos_d[CB * blk + 128 * tc:
                                         CB * blk + 128 * (tc + 1), :])
                        t_s = dma("sync", stb3[:, tc * 1024:(tc + 1) * 1024],
                                  qsin_d[CB * blk + 128 * tc:
                                         CB * blk + 128 * (tc + 1), :])
                        if tg3 >= 1:
                            wait("tensor", "qev", tg3)
                        for nh, pk in ((0, ps_q0), (1, ps_q1)):
                            for ec in range(8):
                                op("tensor", lambda t, ec=ec, nh=nh, pk=pk,
                                   blk=blk, tc=tc: t.matmul(
                                       pk[:, :],
                                       xnb[:, 2048 * ec + CB * blk
                                           + 128 * tc:
                                           2048 * ec + CB * blk
                                           + 128 * (tc + 1)],
                                       Wq_b[:, 1024 * ec + 512 * nh:
                                            1024 * ec + 512 * (nh + 1)],
                                       start=(ec == 0), stop=False))
                            op("tensor", lambda t, nh=nh, pk=pk: t.matmul(
                                pk[:, :], cons["ones_row_b"][:, :],
                                bqr[0:1, 512 * nh:512 * (nh + 1)],
                                start=False, stop=True))
                        P.inc_prev("tensor", "qmm")
                        wait("scalar", "qmm")
                        if tg3 >= 1:
                            wait("scalar", "qrd", tg3)
                        op("scalar", lambda s: s.copy(
                            qtm[:, 0:512], ps_q0[:, :]))
                        inc("scalar", lambda s: s.copy(
                            qtm[:, 512:1024], ps_q1[:, :]), "qev")
                        wait("gpsimd", "qev")
                        if tg3 >= 1:
                            wait("gpsimd", "qrd", tg3)
                        op("gpsimd", lambda e: e.tensor_scalar_mul(
                            qsw[:, 0:1024:2], qtm[:, 1:1024:2], -1.0))
                        inc("gpsimd", lambda e: e.tensor_copy(
                            qsw[:, 1:1024:2], qtm[:, 0:1024:2]), "qsd")
                        wait("vector", "qsd")
                        wait("vector", "ld", max(t_c, t_s))
                        if tg3 >= 1:
                            wait("vector", "qtt", 8 * tg3)
                        op("vector", lambda v, tc=tc: v.tensor_mul(
                            qrot[:, :], qtm[:, 0:1024],
                            ctb3[:, tc * 1024:(tc + 1) * 1024]))
                        op("vector", lambda v, tc=tc: v.tensor_mul(
                            qr1[:, :], qsw[:, :],
                            stb3[:, tc * 1024:(tc + 1) * 1024]))
                        inc("vector", lambda v: v.tensor_add(
                            qrot[:, :], qrot[:, :], qr1[:, :]), "qrd")
                        wait("tensor", "qrd")
                        for jj in range(8):
                            jg = tg3 * 8 + jj
                            if jg >= 2:
                                wait("tensor", "qte", jg - 1)
                            inc("tensor", lambda t, jj=jj: t.transpose(
                                ps3_t[jj % 2][:, :],
                                qrot[:, 128 * jj:128 * (jj + 1)],
                                cons["ident_b"][:, :]), "qtt")
                            wait("vector", "qtt", tg3 * 8 + jj + 1)
                            if blk >= 1 and tc == 0 and jj == 0:
                                wait("vector", "omd", blk * 8)
                            inc("vector", lambda v, jj=jj, tc=tc:
                                v.tensor_copy(
                                    qfm[:, 256 * jj + 128 * tc:
                                        256 * jj + 128 * (tc + 1)],
                                    ps3_t[jj % 2][:, :]), "qte")
                    if blk >= 1:
                        wait("tensor", "swd", blk * 8)
                        wait("scalar", "oev", blk * 8)
                    for fc in range(8):
                        kf = blk * 8 + fc
                        if kf >= 1:
                            wait("tensor", "gev", kf)
                        for ec in range(8):
                            op("tensor", lambda t, ec=ec, fc=fc, blk=blk:
                               t.matmul(
                                   ps3_g[:, 0:256],
                                   Wg_b[:, 1024 * ec + 128 * fc:
                                        1024 * ec + 128 * (fc + 1)],
                                   xnb[:, 2048 * ec + CB * blk:
                                       2048 * ec + CB * (blk + 1)],
                                   start=(ec == 0), stop=(ec == 7)))
                        P.inc_prev("tensor", "gmm")
                        wait("scalar", "gmm")
                        inc("scalar", lambda s, fc=fc: s.activation(
                            gate[:, 256 * fc:256 * (fc + 1)],
                            ps3_g[:, 0:256],
                            AF.Silu, bias=bgc[:, fc:fc + 1]), "gev")
                    wait("vector", "qte", 16 * (blk + 1))
                    inc("vector", lambda v: v.engine_nop(), "qdd")
                    wait("tensor", "sp", max(lk, lv))
                    wait("tensor", "qdd")
                    wait("tensor", "gev", blk * 8 + 8)
                    for pi in range(8):
                        pg = blk * 8 + pi
                        if pg >= 1:
                            wait("tensor", "swd", pg)
                        for hh in (0, 1):
                            for kc in (0, 1):
                                op("tensor", lambda t, kc=kc, pi=pi,
                                   hh=hh, pr=pr: t.matmul(
                                       pss[hh][:, 256 * kc:256 * (kc + 1)],
                                       Kb[64 * hh:64 * hh + 64,
                                          2048 * pr + 1024 * kc + 128 * pi:
                                          2048 * pr + 1024 * kc
                                          + 128 * (pi + 1)],
                                       qfm[64 * hh:64 * hh + 64,
                                           256 * pi:256 * (pi + 1)],
                                       start=True, stop=True))
                        P.inc_prev("tensor", "stm")
                        wait("vector", "stm", pg + 1)
                        if pg >= 2:
                            wait("vector", "omd", pg - 1)
                        op("vector", lambda v, pi=pi, pg=pg: v.tensor_mul(
                            qd2[:, 256 * (pg % 2):256 * (pg % 2 + 1)],
                            qfm[:, 256 * pi:256 * (pi + 1)],
                            Bq[:, 256 * pi:256 * (pi + 1)]))
                        for hh in (0, 1):
                            for kc in (0, 1):
                                h = 2 * pi + hh
                                op("vector", lambda v, kc=kc, h=h, hh=hh,
                                   pi=pi: v.tensor_mul(
                                       swt[:, 1024 * (pi % 2) + 512 * hh
                                           + 256 * kc:
                                           1024 * (pi % 2) + 512 * hh
                                           + 256 * (kc + 1)],
                                       pss[hh][:, 256 * kc:256 * (kc + 1)],
                                       WTt[:, 512 * h + 256 * kc:
                                           512 * h + 256 * (kc + 1)]))
                        P.inc_prev("vector", "swd")
                        wait("tensor", "swd", pg + 1)
                        if pg >= 2:
                            wait("tensor", "oev", pg - 1)
                        for hh in (0, 1):
                            h = 2 * pi + hh
                            hb = 64 * hh
                            for kc in (0, 1):
                                op("tensor", lambda t, kc=kc, h=h, hb=hb,
                                   pr=pr, pi=pi, hh=hh: t.matmul(
                                       pso[pi % 2][hb:hb + 64, :],
                                       Vb[:, 2048 * pr + 1024 * kc + 64 * h:
                                          2048 * pr + 1024 * kc
                                          + 64 * (h + 1)],
                                       swt[:, 1024 * (pi % 2) + 512 * hh
                                           + 256 * kc:
                                           1024 * (pi % 2) + 512 * hh
                                           + 256 * (kc + 1)],
                                       start=(kc == 0), stop=False))
                            op("tensor", lambda t, h=h, hb=hb, blk=blk,
                               pi=pi, pg=pg: t.matmul(
                                   pso[pi % 2][hb:hb + 64, :],
                                   Rb16[hb:hb + 64, 1024 * blk + 64 * h:
                                        1024 * blk + 64 * (h + 1)],
                                   qd2[hb:hb + 64,
                                       256 * (pg % 2):256 * (pg % 2 + 1)],
                                   start=False, stop=False))
                            op("tensor", lambda t, h=h, hb=hb, blk=blk,
                               pi=pi: t.matmul(
                                   pso[pi % 2][hb:hb + 64, :],
                                   Zb16[hb:hb + 64, 1024 * blk + 64 * h:
                                        1024 * blk + 64 * (h + 1)],
                                   qfm[hb:hb + 64, 256 * pi:256 * (pi + 1)],
                                   start=False, stop=True))
                        P.inc_prev("tensor", "omd")
                        wait("vector", "omd")
                        wait("vector", "ld", lx)
                        wait("vector", "gev", blk * 8 + 8)
                        if pi == 0 and blk >= 1:
                            wait("vector", "sp", spill_tgt[0])
                        op("vector", lambda v, pi=pi: v.tensor_mul(
                            otm[:, :], pso[pi % 2][:, :],
                            gate[:, 256 * pi:256 * (pi + 1)]))
                        inc("vector", lambda v, pi=pi, pr=pr:
                            v.tensor_add(
                                outsb[:, 256 * pi:256 * (pi + 1)],
                                otm[:, :],
                                x3[:, 2048 * pr + 256 * pi:
                                   2048 * pr + 256 * (pi + 1)]), "oev")
                    wait("gpsimd", "oev", blk * 8 + 8)
                    dma("gpsimd",
                        part_sp.rearrange("(c p) T -> p c T", p=128)
                        [:, :, CB * blk:CB * (blk + 1)],
                        outsb[:, :].rearrange("p (c t) -> p c t", c=8),
                        semn="sp")
                    spill_tgt[0] = P.v("sp")
                barrier()
        p23.close()
        barrier()
        P.phase_end()
        if upto < 4:
            nc._prog = P
            P.emit()
            return nc

        ry2sb = stk.enter_context(
            nc.sbuf_tensor("ry2sb", [128, NFT * FT], F32))
        with ES() as p4:
            s4 = lambda n, shp, dt=F32: p4.enter_context(
                nc.sbuf_tensor(n, shp, dt))
            b1outer = ES()
            w1q = b1outer.enter_context(
                nc.sbuf_tensor("w1q", [128, 8 * 4096], BF16))
            wq_scope = ES()
            s4b = lambda n, shp, dt=F32: wq_scope.enter_context(
                nc.sbuf_tensor(n, shp, dt))
            wst4 = s4b("wst4", [128, 4096])
            t4a = s4b("t4a", [128, 4096])
            ti4 = s4b("ti4", [128, 4096], I32)
            tf4 = s4b("tf4", [128, 4096])
            for ch in range(8):
                tgt = dma("sync", wst4[:, :],
                          w1T[128 * ch:128 * (ch + 1), :])
                wait("sync", "wqa", P.v("wqa") + 1)
                wait("vector", "ld", tgt)
                op("vector", lambda v: v.tensor_scalar_mul(
                    t4a[:, :], wst4[:, :], s1c[:, :]))
                inc("vector", lambda v: v.tensor_copy(ti4[:, :], t4a[:, :]),
                    "wqa")
                op("vector", lambda v: v.tensor_copy(tf4[:, :], ti4[:, :]))
                inc("vector", lambda v, ch=ch: v.tensor_scalar(
                    w1q[:, 4096 * ch:4096 * (ch + 1)], tf4[:, :],
                    -1.0, 1.0, ALU.max, ALU.min), "wqd")
            if _DBG:
                dma("gpsimd", w1q_dbg, w1q[:, :], semn="sp")
                op("vector", lambda v: v.tensor_copy(dbgc[:, 0:1],
                                                     m1rc[:, :]))
                op("vector", lambda v: v.tensor_copy(dbgc[:, 1:2],
                                                     m2rc[:, :]))
                op("vector", lambda v: v.tensor_copy(dbgc[:, 2:3],
                                                     s1c[:, :]))
                inc("vector", lambda v: v.tensor_copy(dbgc[:, 3:4],
                                                      s2c[:, :]), "dgc")
                wait("gpsimd", "dgc")
                dma("gpsimd", mc_dbg, dbgc[:, :], semn="sp")
            barrier()
            P.phase_end()
            wq_scope.close()
            with ES() as b1s:
                s41 = lambda n, shp, dt=F32: b1s.enter_context(
                    nc.sbuf_tensor(n, shp, dt))
                xq4 = s41("xq4", [128, FT], BF16)
                rwt4 = s41("rwt4", [1, 2 * FT])
                h0 = s41("h0", [128, 8 * FT])
                h0q = s41("h0q", [128, 8 * FT], BF16)
                M4 = s41("M4", [128, FT])
                X4 = s41("X4", [128, 8])
                XC = s41("XC", [128, 8])
                XP = s41("XP", [128, 8])
                rws4 = s41("rws4", [32, 128])
                sh4 = s41("sh4", [32, 128])
                h1 = s41("h1", [128, 32 * FT], BF16)
                M42 = s41("M42", [128, FT], BF16)
                t4b = s41("t4b", [128, FT])
                ti4b = s41("ti4b", [128, FT], I32)
                tf4b = s41("tf4b", [128, FT])
                h1_tgt = [0]
                for ft in range(NFT):
                    with nc.psum_tensor("f4su_%d" % ft, [1, FT], F32) as f4su, \
                         nc.psum_tensor("f4sq_%d" % ft, [1, FT], F32) as f4sq, \
                         nc.psum_tensor("f4bc_%d" % ft, [128, 2 * FT], F32) as f4bc:
                        for fc in range(8):
                            k = ft * 8 + fc
                            if k >= 1:
                                wait("scalar", "fqm", k)
                            inc("scalar", lambda s, fc=fc, ft=ft:
                                s.activation(
                                    xq4[:, :],
                                    xnb[:, 2048 * fc + FT * ft:
                                        2048 * fc + FT * (ft + 1)],
                                    AF.Square), "fqa")
                        for fc in range(8):
                            op("tensor", lambda t, fc=fc, ft=ft,
                               f4su=f4su: t.matmul(
                                f4su[:, :], cons["ones_col_b"][:, :],
                                xnb[:, 2048 * fc + FT * ft:
                                    2048 * fc + FT * (ft + 1)],
                                start=(fc == 0), stop=(fc == 7)))
                        for fc in range(8):
                            wait("tensor", "fqa", ft * 8 + fc + 1)
                            inc("tensor", lambda t, fc=fc,
                                f4sq=f4sq: t.matmul(
                                f4sq[:, :], cons["ones_col_b"][:, :],
                                xq4[:, :],
                                start=(fc == 0), stop=(fc == 7)), "fqm")
                        q0 = rwt4[0:1, 0:FT]
                        q1 = rwt4[0:1, FT:2 * FT]
                        wait("vector", "fqm", 8 * (ft + 1))
                        if ft >= 1:
                            wait("vector", "fbc", ft)
                        op("vector", lambda v, f4su=f4su: v.tensor_scalar_mul(
                            q0, f4su[:, :], 1.0 / 1024.0))
                        inc("vector", lambda v, f4sq=f4sq:
                            v.tensor_scalar_mul(
                                q1, f4sq[:, :], 1.0 / 1024.0), "fr1")
                        if _DBG and ft == 0:
                            wait("gpsimd", "fr1", P.v("fr1"))
                            dma("gpsimd", rw_dbg[:, 0:2 * FT],
                                rwt4[:, :], semn="sp")
                            wait("vector", "sp", P.v("sp"))
                        wait("tensor", "fr1")
                        inc("tensor", lambda t, f4bc=f4bc: t.matmul(
                            f4bc[:, 0:FT], cons["ones_row_f"][:, :], q0,
                            start=True, stop=True), "fbd")
                        wait("vector", "fbd")
                        op("vector", lambda v: v.tensor_mul(q0, q0, q0))
                        inc("vector", lambda v: v.tensor_sub(q1, q1, q0),
                            "fr2")
                        wait("scalar", "fr2")
                        op("scalar", lambda s: s.activation(
                            q0, q1, AF.Ln, bias=cons["epsc"][0:1, :]))
                        inc("scalar", lambda s: s.activation(
                            q1, q0, AF.Exp, scale=-0.5), "fr4")
                        if _DBG and ft == 0:
                            wait("gpsimd", "fr4", P.v("fr4"))
                            dma("gpsimd", rw_dbg[:, 2 * FT:4 * FT],
                                rwt4[:, :], semn="sp")
                            wait("tensor", "sp", P.v("sp"))
                        wait("tensor", "fr4")
                        inc("tensor", lambda t, f4bc=f4bc: t.matmul(
                            f4bc[:, FT:2 * FT], cons["ones_row_f"][:, :],
                            q1, start=True, stop=True), "fbc")
                        wait("vector", "fbc")
                        for fc in range(8):
                            op("vector", lambda v, fc=fc, ft=ft,
                               f4bc=f4bc: v.tensor_sub(
                                t4b[:, :],
                                xnb[:, 2048 * fc + FT * ft:
                                    2048 * fc + FT * (ft + 1)],
                                f4bc[:, 0:FT]))
                            op("vector", lambda v, fc=fc, f4bc=f4bc:
                               v.tensor_mul(
                                t4b[:, :], t4b[:, :], f4bc[:, FT:2 * FT]))
                            op("vector", lambda v, fc=fc: v.tensor_scalar(
                                h0[:, FT * fc:FT * (fc + 1)], t4b[:, :],
                                gmc[:, fc:fc + 1], btc[:, fc:fc + 1],
                                ALU.mult, ALU.add))
                        inc("vector", lambda v: v.engine_nop(), "fh0")
                    with nc.psum_tensor("f4ta_%d" % ft, [128, 128], F32) as f4ta, \
                         nc.psum_tensor("f4tb_%d" % ft, [128, 128], F32) as f4tb, \
                         nc.psum_tensor("f4tx_%d" % ft, [8, 128], F32) as f4tx, \
                         nc.psum_tensor("f4b1_%d" % ft, [128, FT], F32) as f4b1, \
                         nc.psum_tensor("f4b2_%d" % ft, [128, FT], F32) as f4b2:
                        f4t = (f4ta, f4tb)
                        inc("vector", lambda v: v.tensor_reduce(
                            M4[:, :],
                            h0[:, :].rearrange("p (c t) -> p t c", c=8),
                            AX.X, ALU.max, apply_absolute_value=True),
                            "fm1")
                        wait("tensor", "fm1")
                        for r in range(4):
                            rg = ft * 4 + r
                            if r >= 2:
                                wait("tensor", "fm2b", rg - 1)
                            inc("tensor", lambda t, r=r, f4t=f4t: t.transpose(
                                f4t[r % 2][:, :],
                                M4[:, 128 * r:128 * (r + 1)],
                                cons["ident_f"][:, :]), "fm2")
                            wait("vector", "fm2", ft * 4 + r + 1)
                            inc("vector", lambda v, r=r, f4t=f4t:
                                v.tensor_reduce(
                                    X4[:, r:r + 1], f4t[r % 2][:, :],
                                    AX.X, ALU.max), "fm2b")
                        wait("vector", "fm2b", P.v("fm2b"))
                        inc("vector", lambda v: v.tensor_scalar_max(
                            XC[:, 0:4], X4[:, 0:4], EPS), "fmc")
                        wait("scalar", "fmc")
                        op("scalar", lambda s: s.activation(
                            XC[:, 4:8], XC[:, 0:4], AF.Ln))
                        inc("scalar", lambda s: s.activation(
                            XP[:, 0:4], XC[:, 4:8], AF.Exp, scale=-1.0,
                            bias=cons["l127c"][:, :]), "fmd")
                        wait("vector", "fmd")
                        inc("vector", lambda v: v.tensor_scalar_mul(
                            XP[:, 4:8], XC[:, 0:4], 1.0 / 127.0), "fm3")
                        if _DBG and ft == 0:
                            wait("gpsimd", "fm3", P.v("fm3"))
                            dma("gpsimd", xp_dbg[:, 0:8], X4[:, :],
                                semn="sp")
                            dma("gpsimd", xp_dbg[:, 8:16], XP[:, :],
                                semn="sp")
                            wait("vector", "sp", P.v("sp"))
                        wait("tensor", "fm3")
                        inc("tensor", lambda t, f4tx=f4tx: t.transpose(
                            f4tx[:, :], XP[:, :], cons["ident_f"][:, :]),
                            "fm4")
                        wait("vector", "fm4")
                        inc("vector", lambda v, f4tx=f4tx: v.tensor_copy(
                            rws4[0:8, :], f4tx[:, :]), "fm5")
                        if _DBG and ft == 0:
                            wait("gpsimd", "fm5", P.v("fm5"))
                            dma("gpsimd",
                                xp_dbg[:, 16:48].rearrange(
                                    "p (a b) -> (p a) b", a=1)
                                if False else xp_dbg[0:32, 16:48],
                                rws4[:, 0:32], semn="sp")
                            wait("vector", "sp", P.v("sp"))
                        for r in range(4):
                            wait("vector", "fm5")
                            inc("vector", lambda v, r=r: v.stream_shuffle(
                                sh4[:, :], rws4[:, :],
                                [r] + list(range(1, 32))), "fm6")
                            wait("tensor", "fm6")
                            inc("tensor", lambda t, r=r, f4b1=f4b1: t.matmul(
                                f4b1[:, 128 * r:128 * (r + 1)],
                                cons["ones_row_f"][:, :], sh4[0:1, :],
                                start=True, stop=True), "fm7")
                            wait("vector", "fm7")
                            inc("vector", lambda v, r=r: v.stream_shuffle(
                                sh4[:, :], rws4[:, :],
                                [4 + r] + list(range(1, 32))), "fm5")
                            wait("tensor", "fm5")
                            inc("tensor", lambda t, r=r, f4b2=f4b2: t.matmul(
                                f4b2[:, 128 * r:128 * (r + 1)],
                                cons["ones_row_f"][:, :], sh4[0:1, :],
                                start=True, stop=True), "fm7")
                            wait("vector", "fm7")
                            inc("vector", lambda v: v.engine_nop(), "fm5")
                        for fc in range(8):
                            op("vector", lambda v, fc=fc, f4b1=f4b1:
                               v.tensor_mul(
                                t4b[:, :], h0[:, FT * fc:FT * (fc + 1)],
                                f4b1[:, :]))
                            op("vector", lambda v: v.tensor_copy(
                                ti4b[:, :], t4b[:, :]))
                            op("vector", lambda v: v.tensor_copy(
                                tf4b[:, :], ti4b[:, :]))
                            op("vector", lambda v: v.tensor_scalar(
                                t4b[:, :], tf4b[:, :], -128.0, 127.0,
                                ALU.max, ALU.min))
                            op("vector", lambda v, fc=fc, f4b2=f4b2:
                               v.tensor_mul(
                                h0q[:, FT * fc:FT * (fc + 1)], t4b[:, :],
                                f4b2[:, :]))
                        inc("vector", lambda v: v.engine_nop(), "fq1")
                        if _DBG:
                            wait("gpsimd", "fq1", P.v("fq1"))
                            dma("gpsimd",
                                h0q_dbg[:, 4096 * ft:4096 * (ft + 1)],
                                h0q[:, :], semn="sp")
                            wait("vector", "sp", P.v("sp"))
                        with nc.psum_tensor("f4m0_%d" % ft, [128, FT], F32) as f4m0, \
                             nc.psum_tensor("f4m1_%d" % ft, [128, FT], F32) as f4m1:
                            psm = (f4m0, f4m1)
                            if ft >= 1:
                                wait("scalar", "sp", h1_tgt[0])
                            wait("tensor", "fq1")
                            for f2 in range(32):
                                kf = ft * 32 + f2
                                if f2 >= 2:
                                    wait("tensor", "fge", kf - 1)
                                for ec in range(8):
                                    op("tensor", lambda t, ec=ec, f2=f2,
                                       psm=psm: t.matmul(
                                           psm[f2 % 2][:, :],
                                           w1q[:, 4096 * ec + 128 * f2:
                                               4096 * ec + 128 * (f2 + 1)],
                                           h0q[:, FT * ec:FT * (ec + 1)],
                                           start=(ec == 0), stop=(ec == 7)))
                                P.inc_prev("tensor", "fmm")
                                wait("scalar", "fmm")
                                inc("scalar", lambda s, f2=f2,
                                    psm=psm: s.activation(
                                    h1[:, FT * f2:FT * (f2 + 1)],
                                    psm[f2 % 2][:, :], AF.Gelu,
                                    bias=b1c[:, f2:f2 + 1],
                                    scale=m1rc[:, 0:1]), "fge")
                        wait("vector", "fge", P.v("fge"))
                        inc("vector", lambda v: v.tensor_reduce(
                            M42[:, :],
                            h1[:, :].rearrange("p (c t) -> p t c", c=32),
                            AX.X, ALU.max, apply_absolute_value=True),
                            "fn1")
                        with nc.psum_tensor("f4t2a_%d" % ft, [128, 128],
                                            BF16) as f4t2a, \
                             nc.psum_tensor("f4t2b_%d" % ft, [128, 128],
                                            BF16) as f4t2b:
                            f4t2 = (f4t2a, f4t2b)
                            wait("tensor", "fn1")
                            for r in range(4):
                                rg = ft * 4 + r
                                if r >= 2:
                                    wait("tensor", "fn2b", rg - 1)
                                inc("tensor", lambda t, r=r, f4t2=f4t2:
                                    t.transpose(
                                        f4t2[r % 2][:, :],
                                        M42[:, 128 * r:128 * (r + 1)],
                                        cons["ident_b"][:, :]), "fn2")
                                wait("vector", "fn2", ft * 4 + r + 1)
                                inc("vector", lambda v, r=r, f4t2=f4t2:
                                    v.tensor_reduce(
                                        X4[:, r:r + 1],
                                        f4t2[r % 2][:, :],
                                        AX.X, ALU.max), "fn2b")
                        wait("vector", "fn2b", P.v("fn2b"))
                        inc("vector", lambda v: v.tensor_scalar_max(
                            XC[:, 0:4], X4[:, 0:4], EPS), "fnc")
                        wait("scalar", "fnc")
                        op("scalar", lambda s: s.activation(
                            XC[:, 4:8], XC[:, 0:4], AF.Ln))
                        inc("scalar", lambda s: s.activation(
                            XP[:, 0:4], XC[:, 4:8], AF.Exp, scale=-1.0,
                            bias=cons["l127c"][:, :]), "fnd")
                        wait("vector", "fnd")
                        inc("vector", lambda v: v.tensor_scalar(
                            XP[:, 4:8], XC[:, 0:4], m2rc[:, 0:1],
                            1.0 / 127.0, ALU.mult, ALU.mult), "fn3")
                        wait("tensor", "fn3")
                        inc("tensor", lambda t, f4tx=f4tx: t.transpose(
                            f4tx[:, :], XP[:, :], cons["ident_f"][:, :]),
                            "fn4")
                        wait("vector", "fn4")
                        inc("vector", lambda v, f4tx=f4tx: v.tensor_copy(
                            rws4[0:8, :], f4tx[:, :]), "fn5")
                        for r in range(4):
                            wait("vector", "fn5")
                            inc("vector", lambda v, r=r: v.stream_shuffle(
                                sh4[:, :], rws4[:, :],
                                [r] + list(range(1, 32))), "fn6")
                            wait("tensor", "fn6")
                            inc("tensor", lambda t, r=r, f4b1=f4b1: t.matmul(
                                f4b1[:, 128 * r:128 * (r + 1)],
                                cons["ones_row_f"][:, :], sh4[0:1, :],
                                start=True, stop=True), "fn7")
                            wait("vector", "fn7")
                            inc("vector", lambda v, r=r: v.stream_shuffle(
                                sh4[:, :], rws4[:, :],
                                [4 + r] + list(range(1, 32))), "fn5")
                            wait("tensor", "fn5")
                            inc("tensor", lambda t, r=r, f4b2=f4b2: t.matmul(
                                f4b2[:, 128 * r:128 * (r + 1)],
                                cons["ones_row_f"][:, :], sh4[0:1, :],
                                start=True, stop=True), "fn7")
                            wait("vector", "fn7")
                            inc("vector", lambda v: v.engine_nop(), "fn5")
                        inc("vector", lambda v, ft=ft, f4b2=f4b2:
                            v.tensor_copy(
                            ry2sb[:, FT * ft:FT * (ft + 1)], f4b2[:, :]),
                            "fn8")
                        for f2 in range(32):
                            op("vector", lambda v, f2=f2, f4b1=f4b1:
                               v.tensor_mul(
                                t4b[:, :], h1[:, FT * f2:FT * (f2 + 1)],
                                f4b1[:, :]))
                            op("vector", lambda v: v.tensor_copy(
                                ti4b[:, :], t4b[:, :]))
                            op("vector", lambda v: v.tensor_copy(
                                tf4b[:, :], ti4b[:, :]))
                            op("vector", lambda v, f2=f2: v.tensor_scalar(
                                h1[:, FT * f2:FT * (f2 + 1)], tf4b[:, :],
                                -128.0, 127.0, ALU.max, ALU.min))
                        inc("vector", lambda v: v.engine_nop(), "fq2")
                        wait("gpsimd", "fq2")
                        dma("gpsimd",
                            h1q_sp.rearrange("(c p) T -> p c T", p=128)
                            [:, :, FT * ft:FT * (ft + 1)],
                            h1[:, :].rearrange("p (c t) -> p c t", c=32),
                            semn="sp")
                        h1_tgt[0] = P.v("sp")
                barrier()
            b1outer.close()
            barrier()
            P.phase_end()
            with ES() as b2s:
                s42 = lambda n, shp, dt=F32: b2s.enter_context(
                    nc.sbuf_tensor(n, shp, dt))
                w2q = s42("w2q", [128, 32 * 1024], BF16)
                wr_scope = ES()
                s42b = lambda n, shp, dt=F32: wr_scope.enter_context(
                    nc.sbuf_tensor(n, shp, dt))
                wst5 = s42b("wst5", [128, 1024])
                t5a = s42b("t5a", [128, 1024])
                ti5 = s42b("ti5", [128, 1024], I32)
                tf5 = s42b("tf5", [128, 1024])
                for ch in range(32):
                    tgt = dma("sync", wst5[:, :],
                              w2T[128 * ch:128 * (ch + 1), :])
                    wait("sync", "wra", P.v("wra") + 1)
                    wait("vector", "ld", tgt)
                    op("vector", lambda v: v.tensor_scalar_mul(
                        t5a[:, :], wst5[:, :], s2c[:, :]))
                    inc("vector", lambda v: v.tensor_copy(
                        ti5[:, :], t5a[:, :]), "wra")
                    op("vector", lambda v: v.tensor_copy(
                        tf5[:, :], ti5[:, :]))
                    inc("vector", lambda v, ch=ch: v.tensor_scalar(
                        w2q[:, 1024 * ch:1024 * (ch + 1)], tf5[:, :],
                        -1.0, 1.0, ALU.max, ALU.min), "wrd")
                barrier()
                P.phase_end()
                wr_scope.close()
                h1qb = s42("h1qb", [128, 32 * FT], BF16)
                part4 = s42("part4", [128, 8 * FT])
                y4 = s42("y4", [128, 8 * FT])
                ty4 = s42("ty4", [128, FT])
                ty5 = s42("ty5", [128, FT])
                with nc.psum_tensor("f5m0", [128, FT], F32) as f5m0, \
                     nc.psum_tensor("f5m1", [128, FT], F32) as f5m1:
                    psn = (f5m0, f5m1)
                    yst_tgt = [0]
                    for ft in range(NFT):
                        if ft >= 1:
                            wait("sync", "ymm", 8 * ft)
                        lh_s = dma("sync", h1qb[:, :]
                                   .rearrange("p (c t) -> p c t", c=32)
                                   [:, 0:16, :],
                                   h1q_sp.rearrange("(c p) T -> p c T", p=128)
                                   [:, 0:16, FT * ft:FT * (ft + 1)])
                        if ft >= 1:
                            wait("gpsimd", "ymm", 8 * ft)
                        lh = dma("gpsimd", h1qb[:, :]
                                 .rearrange("p (c t) -> p c t", c=32)
                                 [:, 16:32, :],
                                 h1q_sp.rearrange("(c p) T -> p c T", p=128)
                                 [:, 16:32, FT * ft:FT * (ft + 1)], semn="sp")
                        if ft >= 1:
                            wait("sync", "yev", 8 * ft)
                        lp = dma("sync", part4[:, :]
                                 .rearrange("p (c t) -> p c t", c=8),
                                 part_sp.rearrange("(c p) T -> p c T", p=128)
                                 [:, :, FT * ft:FT * (ft + 1)])
                        wait("tensor", "ld", lh_s)
                        wait("tensor", "sp", lh)
                        for fc in range(8):
                            kf = ft * 8 + fc
                            if fc >= 2:
                                wait("tensor", "yev", kf - 1)
                            for hc in range(32):
                                op("tensor", lambda t, hc=hc, fc=fc:
                                   t.matmul(
                                       psn[fc % 2][:, :],
                                       w2q[:, 1024 * hc + 128 * fc:
                                           1024 * hc + 128 * (fc + 1)],
                                       h1qb[:, FT * hc:FT * (hc + 1)],
                                       start=(hc == 0), stop=(hc == 31)))
                            P.inc_prev("tensor", "ymm")
                            wait("vector", "ymm")
                            if fc == 0 and ft >= 1:
                                wait("vector", "st", yst_tgt[0])
                            op("vector", lambda v, fc=fc, ft=ft: v.tensor_mul(
                                ty4[:, :], psn[fc % 2][:, :],
                                ry2sb[:, FT * ft:FT * (ft + 1)]))
                            inc("vector", lambda v: v.engine_nop(), "ya1")
                            wait("scalar", "ya1")
                            inc("scalar", lambda s, fc=fc: s.activation(
                                ty5[:, :], ty4[:, :], AF.Identity
                                if hasattr(AF, "Identity") else AF.Copy,
                                bias=b2c[:, fc:fc + 1]), "ya2")
                            wait("vector", "ya2")
                            wait("vector", "ld", lp)
                            inc("vector", lambda v, fc=fc: v.tensor_add(
                                y4[:, FT * fc:FT * (fc + 1)], ty5[:, :],
                                part4[:, FT * fc:FT * (fc + 1)]), "yev")
                        wait("gpsimd", "yev", ft * 8 + 8)
                        dma("gpsimd",
                            y_d.rearrange("(c p) T -> p c T", p=128)
                            [:, :, FT * ft:FT * (ft + 1)],
                            y4[:, :].rearrange("p (c t) -> p c t", c=8),
                            semn="st")
                        yst_tgt[0] = P.v("st")
        barrier()
        if _DBG:
            dbg_specs = [("d_ws", ws_dbg, [1, 8], F32),
                         ("d_rw", rw_dbg, [1, 4 * FT], F32),
                         ("d_w1q", w1q_dbg, [128, 8 * 4096], BF16),
                         ("d_mc", mc_dbg, [128, 4], F32),
                         ("d_xp", xp_dbg, [128, 48], F32),
                         ("d_h0q", h0q_dbg, [128, NFT * 4096], BF16),
                         ("d_kfm", kfm_sp, [128, NPT * 4096], BF16),
                         ("d_vtm", vtm_sp, [128, 16 * 1024], BF16),
                         ("d_gp", gp_sp, [64, 2 * 8192], F32),
                         ("d_h1q", h1q_sp, [HID, TLOC], BF16),
                         ("d_part", part_sp, [E, TLOC], F32)]
            for nm, srct, shp, dt in dbg_specs:
                od = nc.dram_tensor(nm, shp, dt, kind="ExternalOutput").ap()
                dma("gpsimd", od, srct, semn="st")
            wait("gpsimd", "st", P.v("st"))
            barrier()
        wait("gpsimd", "st", P.v("st"))
        nc._prog = P
        P.emit()
    return nc


_NC_CACHE = {}


def _get_nc():
    if "nc" not in _NC_CACHE:
        _NC_CACHE["nc"] = build_nc()
    return _NC_CACHE["nc"]


def kernel(**inp):
    x = np.ascontiguousarray(np.asarray(inp["L0_attn_x"], dtype=np.float32))
    Wq = np.asarray(inp["L0_attn_Wq"], np.float32)
    bq = np.asarray(inp["L0_attn_bq"], np.float32)
    Wk = np.asarray(inp["L0_attn_Wk"], np.float32)
    bk = np.asarray(inp["L0_attn_bk"], np.float32)
    Wv = np.asarray(inp["L0_attn_Wv"], np.float32)
    bv = np.asarray(inp["L0_attn_bv"], np.float32)
    Wg = np.asarray(inp["L0_attn_Wg"], np.float32)
    bg = np.asarray(inp["L0_attn_bg"], np.float32)
    gamma = np.asarray(inp["L0_attn_gamma"], np.float32)
    w1 = np.asarray(inp["L0_ffn_w1"], np.float32)
    b1 = np.asarray(inp["L0_ffn_b1"], np.float32)
    w2 = np.asarray(inp["L0_ffn_w2"], np.float32)
    b2 = np.asarray(inp["L0_ffn_b2"], np.float32)
    fgm = np.asarray(inp["L0_ffn_gamma"], np.float32)
    fbt = np.asarray(inp["L0_ffn_beta"], np.float32)

    WqT = np.ascontiguousarray(Wq.T)
    WkT = np.ascontiguousarray(Wk.T)
    WvT = np.ascontiguousarray(Wv.T)
    WgT = np.ascontiguousarray(Wg.T)
    w1T = np.ascontiguousarray(w1.T)
    w2T = np.ascontiguousarray(w2.T)
    brows = np.ascontiguousarray(np.stack([bq, bk, bv], axis=0))
    bgc = np.ascontiguousarray(bg.reshape(8, 128).T)
    b1c = np.ascontiguousarray(b1.reshape(32, 128).T)
    b2c = np.ascontiguousarray(b2.reshape(8, 128).T)
    gmc = np.ascontiguousarray(fgm.reshape(8, 128).T)
    btc = np.ascontiguousarray(fbt.reshape(8, 128).T)
    gam = gamma.reshape(1, H)

    inv = (1.0 / 10000.0 ** (np.arange(0, 64, 2, dtype=np.float64) / 64.0))
    in_maps = []
    for c in range(8):
        b, p = divmod(c, 4)
        sl = slice(2048 * p, 2048 * (p + 1))
        xTc = np.ascontiguousarray(x[b, sl, :].T)
        pos = np.arange(2048 * p, 2048 * (p + 1), dtype=np.float64)
        fr = pos[:, None] * inv[None, :]
        emb = np.concatenate([fr, fr], axis=1)
        cosn = np.cos(emb).astype(np.float32)
        sinn = np.sin(emb).astype(np.float32)
        kcos = np.ascontiguousarray(np.tile(cosn, (1, H))).astype(BF)
        ksin = np.ascontiguousarray(np.tile(sinn, (1, H))).astype(BF)
        qcos = np.ascontiguousarray(np.tile(cosn * 0.125, (1, H))).astype(BF)
        qsin = np.ascontiguousarray(np.tile(sinn * 0.125, (1, H))).astype(BF)
        nd4 = np.array([[-2048.0 * (p - j - 1) if j < p else -1e9
                         for j in range(4)]], np.float32)
        zm4 = np.array([[1.0 if j > p else 0.0 for j in range(4)]],
                       np.float32)
        in_maps.append({
            "xT": xTc, "WqT": WqT, "WkT": WkT, "WvT": WvT, "WgT": WgT,
            "w1T": w1T, "w2T": w2T, "brows": brows, "bgc": bgc,
            "b1c": b1c, "b2c": b2c, "gmc": gmc, "btc": btc, "gam": gam,
            "nd4": nd4, "zm4": zm4, "kcos": kcos, "ksin": ksin,
            "qcos": qcos, "qsin": qsin})
    nc = _get_nc()
    res = run_bass_kernel_spmd(nc, in_maps, core_ids=list(range(8)))
    out = np.empty((2, 8192, 1024), np.float32)
    for c in range(8):
        b, p = divmod(c, 4)
        out[b, 2048 * p:2048 * (p + 1), :] = res.results[c]["y"].T
    if _DBG:
        DBG_RES["res"] = res.results
    return out
